# version 17
# baseline (speedup 1.0000x reference)
import math
import numpy as np
import ml_dtypes
import concourse.bass as bass
import concourse.mybir as mybir
from concourse.bass_utils import run_bass_kernel_spmd

F32 = mybir.dt.float32
BF16 = mybir.dt.bfloat16
AF = mybir.ActivationFunctionType
ALU = mybir.AluOpType
AX = mybir.AxisListType
bfdt = ml_dtypes.bfloat16

LAM_INIT = 0.8 - 0.6 * math.exp(-0.3)
SCALING = 1.0 / math.sqrt(128.0)
EPS = 1e-5


class Cfg:
    def __init__(self, tok=2048, heads=8, experts=16, dff=2048, debug=False, expose=False):
        self.expose = expose
        self.tok = tok; self.heads = heads; self.experts = experts
        self.dff = dff; self.d = 1024
        self.cap = tok * 2 // experts
        assert self.cap == 256 and experts == 16 and dff % 512 == 0
        self.nt = tok // 128
        self.nd = self.d // 128
        self.nq = tok // 512
        self.ndffc = dff // 512
        self.debug = debug


class DmaSem:
    def __init__(self, nc, name):
        self.sem = nc.alloc_semaphore(name)
        self.issued = 0


class Plan:
    ENGS = ("t", "v", "s", "g", "y")

    def __init__(self, nc):
        self.nc = nc
        self.ops = {k: [] for k in self.ENGS}
        self.sem = {k: nc.alloc_semaphore(f"prog_{k}") for k in self.ENGS}
        self.cnt = {k: 0 for k in self.ENGS}
        self.waited = {k: {j: 0 for j in self.ENGS} for k in self.ENGS}
        self.dma_waited = {k: {} for k in self.ENGS}
        self.dsems = []

    def dma_sem(self, name):
        ds = DmaSem(self.nc, name)
        self.dsems.append(ds)
        return ds

    def emit(self, e, fn):
        self.ops[e].append(fn)

    def prod(self, e, fn):
        self.cnt[e] += 1
        lvl = self.cnt[e]
        s = self.sem[e]
        self.ops[e].append(lambda eng, fn=fn, s=s: fn(eng).then_inc(s, 1))
        return (e, lvl)

    def need(self, e, *tokens):
        for tok in tokens:
            if tok is None:
                continue
            pe, lvl = tok
            if pe == e:
                if self.waited[e][e] < lvl:
                    self.ops[e].append(lambda eng: eng.drain())
                    self.waited[e][e] = self.cnt[e]
            else:
                if self.waited[e][pe] < lvl:
                    s = self.sem[pe]
                    self.ops[e].append(lambda eng, s=s, L=lvl: eng.wait_ge(s, L))
                    self.waited[e][pe] = lvl

    def dma(self, q, out_ap, in_ap, ds, **kw):
        ds.issued += 16
        self.ops[q].append(
            lambda eng, o=out_ap, i=in_ap, s=ds.sem, kw=kw: eng.dma_start(o, i, **kw).then_inc(s, 16))

    def need_dma(self, e, ds):
        lvl = ds.issued
        if lvl and self.dma_waited[e].get(id(ds), 0) < lvl:
            self.ops[e].append(lambda eng, s=ds.sem, L=lvl: eng.wait_ge(s, L))
            self.dma_waited[e][id(ds)] = lvl

    def token(self, e):
        return (e, self.cnt[e])

    def join_all(self):
        for e in self.ENGS:
            for pe in self.ENGS:
                if pe != e:
                    self.need(e, (pe, self.cnt[pe]))
            for ds in self.dsems:
                self.need_dma(e, ds)

    def realize(self, block):
        block.tensor(lambda eng: [f(eng) for f in self.ops["t"]])
        block.vector(lambda eng: [f(eng) for f in self.ops["v"]])
        block.scalar(lambda eng: [f(eng) for f in self.ops["s"]])
        block.gpsimd(lambda eng: [f(eng) for f in self.ops["g"]])
        block.sync(lambda eng: [f(eng) for f in self.ops["y"]])


def host_constants():
    return {
        "c_identf": np.eye(128, dtype=np.float32),
        "c_iota_bc": np.broadcast_to(np.arange(256, dtype=np.float32), (128, 256)).astype(bfdt).copy(),
        "c_iotacol0": np.arange(128, dtype=np.float32).reshape(128, 1),
        "c_iotacol1": (np.arange(128, dtype=np.float32) + 128.0).reshape(128, 1),
        "c_onesrow_b": np.ones((1, 128), dtype=bfdt),
        "c_onesrow_f": np.ones((1, 128), dtype=np.float32),
        "c_onescol_b": np.ones((128, 1), dtype=bfdt),
        "c_identb": np.eye(128, dtype=np.float32).astype(bfdt),
        "c_bar": np.zeros((1, 8), dtype=np.float32),
    }


def build_program(cfg):
    nc = bass.Bass("TRN2", target_bir_lowering=False)
    T, H, E, DFF, D = cfg.tok, cfg.heads, cfg.experts, cfg.dff, cfg.d
    NT, ND, NQ, NDC = cfg.nt, cfg.nd, cfg.nq, cfg.ndffc
    EH = 256

    din = {}
    def dram_in(name, shape, dt=F32):
        din[name] = nc.dram_tensor(name, list(shape), dt, kind="ExternalInput")
        return din[name]

    x_in = dram_in("x", (T, D))
    wq = dram_in("wq", (EH * H, D)); wk = dram_in("wk", (EH * H, D))
    wv = dram_in("wv", (EH * H, D)); wo = dram_in("wo", (D, EH * H))
    lq1 = dram_in("lq1", (H, 128)); lk1 = dram_in("lk1", (H, 128))
    lq2 = dram_in("lq2", (H, 128)); lk2 = dram_in("lk2", (H, 128))
    rmss = dram_in("rmss", (1, EH))
    rw1 = dram_in("rw1", (D, 128)); rw2 = dram_in("rw2", (128, E))
    w1 = dram_in("w1", (E * NDC, 128, ND * 512)); w2 = dram_in("w2", (E * NDC, 128, ND * 512))
    w3 = dram_in("w3", (E * NDC, 128, 4 * 1024))
    identf = dram_in("c_identf", (128, 128))
    iota_bc = dram_in("c_iota_bc", (128, 256), BF16)
    iotacol0 = dram_in("c_iotacol0", (128, 1))
    iotacol1 = dram_in("c_iotacol1", (128, 1))
    onesrow_b = dram_in("c_onesrow_b", (1, 128), BF16)
    onesrow_f = dram_in("c_onesrow_f", (1, 128))
    onescol_b = dram_in("c_onescol_b", (128, 1), BF16)
    identb = dram_in("c_identb", (128, 128), BF16)
    dram_in("c_bar", (1, 8), F32)

    out_t = nc.dram_tensor("out", [T, D], F32, kind="ExternalOutput")
    dbgkind = "ExternalOutput" if (cfg.debug or cfg.expose) else "Internal"
    sc_ot = nc.dram_tensor("sc_ot", [2 * H * 128, T], BF16, kind=dbgkind)
    sc_x2 = nc.dram_tensor("sc_x2", [T, D], F32, kind=dbgkind)
    sc_wot = nc.dram_tensor("sc_wot", [2 * H * 128, 1024], BF16, kind="Internal")
    dbg = {}
    if cfg.debug:
        for nm, shape, dt in [
            ("dbg_xnt", (128, ND * T), BF16),
            ("dbg_probs", (16, T), F32),
            ("dbg_sm", (16, T), F32),
            ("dbg_attnout", (128, NT * D), BF16),
            ("dbg_ye", (128, 2 * 1024), BF16),
            ("dbg_xgt", (128, 2 * 1024), BF16),
            ("dbg_wcol", (128, 2), F32),
        ]:
            dbg[nm] = nc.dram_tensor(nm, list(shape), dt, kind="ExternalOutput")

    sb = nc.alloc_sbuf_tensor
    s_identf = sb("s_identf", [128, 128], F32)
    s_iota = sb("s_iota", [128, 256], BF16)
    s_ic0 = sb("s_ic0", [128, 1], F32)
    s_ic1 = sb("s_ic1", [128, 1], F32)
    s_or_b = sb("s_or_b", [1, 128], BF16)
    s_or_f = sb("s_or_f", [1, 128], F32)
    s_oc_b = sb("s_oc_b", [128, 1], BF16)
    s_identb = sb("s_identb", [128, 128], BF16)
    s_rsbc = sb("s_rsbc", [128, 256], F32)
    s_lam = sb("s_lam", [H, 1], F32)
    s_lamt = sb("s_lamt", [1, H], F32)
    s_lambc = sb("s_lambc", [128, H], F32)
    s_rw1b = sb("s_rw1b", [128, ND * 128], BF16)
    s_rw2b = sb("s_rw2b", [128, E], BF16)
    s_scal = sb("s_scal", [1, 16], F32)
    s_col = [sb(f"s_col{i}", [128, 1], F32) for i in range(8)]
    s_bn = sb("s_bn", [128, 12], F32)
    s_mv = [sb(f"s_mv{i}", [128, 2], F32) for i in range(2)]
    s_xstage = [sb(f"s_xstage{i}", [128, D], F32) for i in range(2)]
    s_xn = [sb(f"s_xn{i}", [128, D], BF16) for i in range(2)]
    s_f32t = sb("s_f32t", [128, D], F32)
    s_x2tile = sb("s_x2tile", [128, D], F32)
    s_lq = sb("s_lq", [H, 128], F32); s_lk = sb("s_lk", [H, 128], F32)
    s_lq2 = sb("s_lq2", [H, 128], F32); s_lk2 = sb("s_lk2", [H, 128], F32)
    s_rms = sb("s_rms", [1, 256], F32)
    s_x2n = sb("s_x2n", [128, NT * D], BF16)
    s_smT = sb("s_smT", [128, NT * 16], BF16)
    s_smTf = sb("s_smTf", [128, NT * 16], F32)
    s_probsT = sb("s_probsT", [128, NT * 16], BF16)

    p = Plan(nc)
    ds_c = p.dma_sem("ds_c")
    ds_rw = p.dma_sem("ds_rw")
    ds_x = [p.dma_sem(f"ds_x{i}") for i in range(2)]
    ds_xt = p.dma_sem("ds_xt")
    ds_w = [p.dma_sem(f"ds_w{i}") for i in range(2)]
    ds_wt = [p.dma_sem(f"ds_wt{i}") for i in range(2)]
    ds_ott = p.dma_sem("ds_ott")
    ds_ots = p.dma_sem("ds_ots")
    ds_wos = p.dma_sem("ds_wos")
    ds_wor = p.dma_sem("ds_wor")
    ds_otr = [p.dma_sem(f"ds_otr{i}") for i in range(2)]
    ds_x2 = p.dma_sem("ds_x2")
    ds_x2t = [p.dma_sem(f"ds_x2t{i}") for i in range(2)]
    ds_xr = [p.dma_sem(f"ds_xr{i}") for i in range(2)]
    ds_smt = p.dma_sem("ds_smt")
    ds_wr = p.dma_sem("ds_wr")
    ds_oht = [p.dma_sem(f"ds_oht{i}") for i in range(2)]
    ds_x2r = [p.dma_sem(f"ds_x2r{i}") for i in range(2)]
    ds_out = p.dma_sem("ds_out")

    nc.psum_base = 0
    pQ = [nc.alloc_psum_tensor(f"pQ{i}", [128, 512], F32) for i in range(2)]
    pA = [nc.alloc_psum_tensor(f"pA{i}", [128, 512], F32) for i in range(2)]
    pO = [nc.alloc_psum_tensor(f"pO{i}", [128, 512], F32) for i in range(4)]

    for dst, src in [(s_identf, identf), (s_iota, iota_bc), (s_ic0, iotacol0),
                     (s_ic1, iotacol1), (s_or_b, onesrow_b), (s_or_f, onesrow_f),
                     (s_oc_b, onescol_b), (s_identb, identb), (s_lq, lq1), (s_lk, lk1), (s_lq2, lq2),
                     (s_lk2, lk2), (s_rms, rmss)]:
        p.dma("y", dst[:], src[:], ds_c)
    p.dma("g", s_rw1b.ap().rearrange("p (c f) -> p c f", c=ND),
          rw1.ap().rearrange("(c p) f -> p c f", p=128), ds_rw)
    p.dma("g", s_rw2b[:], rw2[:], ds_rw)

    p.need_dma("v", ds_c)
    s_ltmp = sb("s_ltmp", [H, 128], F32)
    s_l1 = sb("s_l1", [H, 1], F32); s_l2 = sb("s_l2", [H, 1], F32)
    tk = p.prod("v", lambda v: v.tensor_mul(s_ltmp[:], s_lq[:], s_lk[:]))
    p.need("v", tk)
    t1 = p.prod("v", lambda v: v.reduce_sum(s_l1[:], s_ltmp[:], AX.X))
    p.need("v", t1)
    tk = p.prod("v", lambda v: v.tensor_mul(s_ltmp[:], s_lq2[:], s_lk2[:]))
    p.need("v", tk)
    t2 = p.prod("v", lambda v: v.reduce_sum(s_l2[:], s_ltmp[:], AX.X))
    p.need("s", t1, t2)
    te1 = p.prod("s", lambda s: s.activation(s_l1[:], s_l1[:], AF.Exp))
    p.need("s", te1)
    te2 = p.prod("s", lambda s: s.activation(s_l2[:], s_l2[:], AF.Exp))
    p.need("v", te1, te2)
    tk = p.prod("v", lambda v: v.tensor_sub(s_lam[:], s_l1[:], s_l2[:]))
    p.need("v", tk)
    tlam = p.prod("v", lambda v: v.tensor_scalar_add(s_lam[:], s_lam[:], LAM_INIT))
    p.need("t", tlam)
    p.need_dma("t", ds_c)
    tk = p.prod("t", lambda t: t.transpose(pQ[0][0:1, 0:H], s_lam[:], s_identf[0:H, 0:H]))
    p.need("v", tk)
    tk = p.prod("v", lambda v: v.tensor_copy(s_lamt[:], pQ[0][0:1, 0:H]))
    p.need("t", tk)
    tk = p.prod("t", lambda t: t.matmul(pQ[1][:, 0:H], s_or_f[:], s_lamt[:], start=True, stop=True))
    p.need("v", tk)
    tlambc = p.prod("v", lambda v: v.tensor_copy(s_lambc[:], pQ[1][:, 0:H]))
    tk = p.prod("t", lambda t: t.matmul(pQ[0][:, 0:256], s_or_f[:], s_rms[:], start=True, stop=True))
    p.need("v", tk)
    trsbc = p.prod("v", lambda v: v.tensor_scalar_mul(s_rsbc[:], pQ[0][:, 0:256], 1.0 - LAM_INIT))
    pQ_tok = {0: trsbc, 1: tlambc}

    def ln_stats(tile_ap, mv_ap):
        t1 = p.prod("v", lambda v: v.bn_stats(s_bn[:, 0:6], tile_ap[:, 0:512]))
        t2 = p.prod("v", lambda v: v.bn_stats(s_bn[:, 6:12], tile_ap[:, 512:1024]))
        p.need("v", t1, t2)
        return p.prod("v", lambda v: v.bn_aggr(mv_ap, s_bn[:]))

    def rstd_chain(mv_ap, out_col):
        tv = p.prod("v", lambda v: v.tensor_scalar_add(out_col, mv_ap[:, 1:2], EPS))
        p.need("s", tv)
        t1 = p.prod("s", lambda s: s.activation(out_col, out_col, AF.Ln))
        p.need("s", t1)
        return p.prod("s", lambda s: s.activation(out_col, out_col, AF.Exp, scale=-0.5))

    mark0 = nc.sbuf_base
    s_xnT = sb("s_xnT", [128, ND * T], BF16)
    s_wsf = [sb(f"s_wsf{i}", [128, 1024], F32) for i in range(2)]
    s_wsb = [sb(f"s_wsb{i}", [128, 1024], BF16) for i in range(2)]
    s_wqT = [sb(f"s_wqT{i}", [128, ND * 256], BF16) for i in range(2)]
    s_wkT = [sb(f"s_wkT{i}", [128, ND * 256], BF16) for i in range(2)]
    s_wvT = [sb(f"s_wvT{i}", [128, ND * 256], BF16) for i in range(2)]
    s_woTh = sb("s_woTh", [128, 2 * 1024], BF16)
    s_qt = sb("s_qt", [128, 2 * T], BF16)
    s_kt = sb("s_kt", [128, 2 * T], BF16)
    s_v = sb("s_v", [128, NT * 257], BF16)
    s_sq = sb("s_sq", [128, 2 * T], BF16)
    s_et = sb("s_et", [128, NT * 512], BF16)
    s_o1 = sb("s_o1", [128, 4 * 257], F32)
    s_o2 = sb("s_o2", [128, 4 * 257], F32)
    s_oc = sb("s_oc", [128, 256], F32)
    s_t2 = sb("s_t2", [128, 256], F32)
    s_otil = [sb(f"s_otil{i}", [128, 256], BF16) for i in range(2)]
    s_otT = sb("s_otT", [128, 2 * T], BF16)
    s_bias = sb("s_bias", [128, 1], F32)

    xre = x_in.ap().rearrange("(n p) f -> n p f", p=128)
    txn_toks = []
    for i in range(NT):
        par = i % 2
        if i >= 2:
            p.need("y", txn_toks[i - 2])
        p.dma("y", s_xstage[par][:], xre[i], ds_x[par])
        p.need_dma("v", ds_x[par])
        if i >= 2:
            p.need_dma("v", ds_xt)
        tst = ln_stats(s_xstage[par], s_mv[par][:])
        p.need("v", tst)
        trs = rstd_chain(s_mv[par], s_col[par][:])
        p.need("v", trs)
        txn = p.prod("v", lambda v, par=par: v.tensor_scalar(
            s_xn[par][:], s_xstage[par][:], s_mv[par][:, 0:1], s_col[par][:],
            ALU.subtract, ALU.mult))
        txn_toks.append(txn)
        p.need("y", txn)
        for j in range(ND):
            p.dma("y", s_xnT[:, j * T + i * 128: j * T + (i + 1) * 128],
                  s_xn[par][:, j * 128:(j + 1) * 128], ds_xt, transpose=True)
    if cfg.debug:
        p.need_dma("y", ds_xt)
        p.dma("y", dbg["dbg_xnt"][:], s_xnT[:], ds_out)

    wslice_done = {}

    wcv_tok = [None, None]
    wtb_done = [None, None]

    def head_weight_load(h, par, prev_qkv_t):
        p.need("y", prev_qkv_t)
        idx = 0
        for wi, (wsrc, dstbuf) in enumerate([(wq, s_wqT[par]), (wk, s_wkT[par]),
                                             (wv, s_wvT[par]), (wo, None)]):
            for c in range(2):
                sp = idx % 2
                idx += 1
                p.need("s", wcv_tok[sp])
                if wi < 3:
                    src = wsrc.ap()[h * EH + c * 128: h * EH + (c + 1) * 128, :]
                else:
                    src = wo.ap()[4 * c * 128:(4 * c + 4) * 128,
                                  h * EH:(h + 1) * EH].rearrange("(c p) f -> p c f", p=128)
                p.dma("s", s_wsf[sp][:] if wi < 3 else
                      s_wsf[sp].ap().rearrange("p (c f) -> p c f", c=4), src, ds_w[sp])
                p.need_dma("v", ds_w[sp])
                p.need_dma("v", ds_wt[0])
                p.need_dma("v", ds_wt[1])
                tcv = p.prod("v", lambda v, sp=sp: v.tensor_copy(s_wsb[sp][:], s_wsf[sp][:]))
                wcv_tok[sp] = tcv
                p.need("y", tcv)
                if wi < 3:
                    for j in range(ND):
                        p.dma("y", dstbuf[:, j * 256 + c * 128: j * 256 + (c + 1) * 128],
                              s_wsb[sp][:, j * 128:(j + 1) * 128], ds_wt[par], transpose=True)
                else:
                    if c == 0:
                        p.need_dma("y", ds_wos)
                    for cp in range(4):
                        jd = 4 * c + cp
                        for ch in range(2):
                            p.dma("y", s_woTh[:, ch * 1024 + jd * 128: ch * 1024 + (jd + 1) * 128],
                                  s_wsb[sp][:, cp * 256 + ch * 128: cp * 256 + (ch + 1) * 128],
                                  ds_wt[par], transpose=True)
        p.need_dma("y", ds_wt[par])
        for ch in range(2):
            ec = 2 * h + ch
            p.dma("y", sc_wot.ap()[ec * 128:(ec + 1) * 128, :],
                  s_woTh[:, ch * 1024:(ch + 1) * 1024], ds_wos)

    head_weight_load(0, 0, None)
    p.need_dma("t", ds_xt)

    qkv_bank_init = {0: pQ_tok.get(0), 1: pQ_tok.get(1), 2: None, 3: None}
    qkv_t_done = {}
    prev_head_consumers = None
    for h in range(H):
        par = h % 2
        if h + 1 < H:
            head_weight_load(h + 1, (h + 1) % 2, qkv_t_done.get(h - 1))
        p.need_dma("t", ds_wt[par])

        if prev_head_consumers is not None:
            p.need("v", *prev_head_consumers)
        qkvb = [pQ[0], pQ[1], pA[0], pA[1]]
        bank_tok = dict(qkv_bank_init)
        gi = 0
        for (wbuf, obuf) in [(s_wqT[par], s_qt), (s_wkT[par], s_kt)]:
            for et in range(2):
                for qc in range(NQ):
                    bi = gi % 4
                    gi += 1
                    p.need("t", bank_tok.get(bi))
                    def mmf(t, wbuf=wbuf, et=et, qc=qc, bank=qkvb[bi]):
                        for j in range(ND):
                            ins = t.matmul(bank[:],
                                           wbuf[:, j * 256 + et * 128: j * 256 + et * 128 + 128],
                                           s_xnT[:, j * T + qc * 512: j * T + (qc + 1) * 512],
                                           start=(j == 0), stop=(j == ND - 1))
                        return ins
                    tmm = p.prod("t", mmf)
                    p.need("v", tmm)
                    bank_tok[bi] = p.prod("v", lambda v, obuf=obuf, et=et, qc=qc, bi=bi:
                                          v.tensor_copy(obuf[:, et * T + qc * 512: et * T + (qc + 1) * 512],
                                                        qkvb[bi][:]))
        if h == 0:
            t_vones = p.prod("v", lambda v: v.memset(s_v[:], 1.0))
            p.need("t", t_vones)
        for tt in range(NT):
            bi = gi % 4
            gi += 1
            p.need("t", bank_tok.get(bi))
            def mmv(t, tt=tt, bank=qkvb[bi], par=par):
                for j in range(ND):
                    ins = t.matmul(bank[:, 0:256],
                                   s_xnT[:, j * T + tt * 128: j * T + tt * 128 + 128],
                                   s_wvT[par][:, j * 256:(j + 1) * 256],
                                   start=(j == 0), stop=(j == ND - 1))
                return ins
            tmm = p.prod("t", mmv)
            p.need("v", tmm)
            bank_tok[bi] = p.prod("v", lambda v, tt=tt, bi=bi:
                                  v.tensor_copy(s_v[:, tt * 257: tt * 257 + 256], qkvb[bi][:, 0:256]))
        qkv_done_v = p.token("v")

        p.need("s", qkv_done_v)
        sq_consumed = None
        pA0_tok = None
        last_red = None
        for bi_src, srcbuf in [(0, s_qt), (1, s_kt)]:
            if sq_consumed is not None:
                p.need("s", sq_consumed)
            tsq0 = p.prod("s", lambda s, srcbuf=srcbuf: s.activation(
                s_sq[:, 0:T], srcbuf[:, 0:T], AF.Square))
            tsq1 = p.prod("s", lambda s, srcbuf=srcbuf: s.activation(
                s_sq[:, T:2 * T], srcbuf[:, T:2 * T], AF.Square))
            p.need("t", tsq0, tsq1)
            red_toks = []
            last_mm = None
            for ch in range(NQ):
                if last_red is not None:
                    p.need("t", last_red)
                def mmn(t, ch=ch):
                    t.matmul(pO[0][0:1, 0:512], s_oc_b[:], s_sq[:, ch * 512:(ch + 1) * 512],
                             start=True, stop=False)
                    return t.matmul(pO[0][0:1, 0:512], s_oc_b[:],
                                    s_sq[:, T + ch * 512: T + (ch + 1) * 512],
                                    start=False, stop=True)
                last_mm = p.prod("t", mmn)
                p.need("v", last_mm)
                last_red = p.prod("v", lambda v, ch=ch: v.reduce_max(
                    s_scal[0:1, ch:ch + 1], pO[0][0:1, 0:512], AX.X))
                red_toks.append(last_red)
            p.need("v", *red_toks)
            tmx = p.prod("v", lambda v, bi_src=bi_src: v.reduce_max(
                s_scal[0:1, 8 + bi_src: 9 + bi_src], s_scal[0:1, 0:NQ], AX.X))
            p.need("v", tmx)
            sq_consumed = last_mm
            pA0_tok = red_toks[-1]
        tmul = p.prod("v", lambda v: v.tensor_mul(s_scal[0:1, 10:11], s_scal[0:1, 8:9],
                                                  s_scal[0:1, 9:10]))
        p.need("s", tmul)
        tln = p.prod("s", lambda s: s.activation(s_scal[0:1, 10:11], s_scal[0:1, 10:11], AF.Ln))
        p.need("s", tln)
        tex = p.prod("s", lambda s: s.activation(s_scal[0:1, 10:11], s_scal[0:1, 10:11],
                                                 AF.Exp, scale=0.5))
        p.need("v", tex)
        tbn = p.prod("v", lambda v: v.tensor_scalar_mul(s_scal[0:1, 10:11],
                                                        s_scal[0:1, 10:11], -SCALING))
        p.need("t", tbn, pA0_tok)
        tbb = p.prod("t", lambda t: t.matmul(pO[0][:, 0:1], s_or_f[:], s_scal[0:1, 10:11],
                                             start=True, stop=True))
        p.need("v", tbb)
        tbias = p.prod("v", lambda v: v.tensor_copy(s_bias[:], pO[0][:, 0:1]))
        pAb = [pQ[0], pQ[1], pA[0], pA[1]]
        pA_tok = dict(bank_tok)

        p.need("s", tbias)
        pO_free = {0: tbias, 1: None, 2: None, 3: None}
        et_free = None
        last_a_mm = None
        for qc in range(NQ):
            o1_evac = [None] * 4
            ev_last = [None] * 4
            for half in range(2):
                def mk_a(kt):
                    bi = kt % 4
                    p.need("t", pA_tok.get(bi))
                    def mma(t, half=half, kt=kt, qc=qc, bi=bi):
                        return t.matmul(pAb[bi][:],
                                        s_kt[:, half * T + kt * 128: half * T + (kt + 1) * 128],
                                        s_qt[:, half * T + qc * 512: half * T + (qc + 1) * 512],
                                        start=True, stop=True)
                    return p.prod("t", mma)
                tka = {}
                for kt in range(min(4, NT)):
                    tka[kt] = mk_a(kt)
                for kt in range(NT):
                    bi = kt % 4
                    last_a_mm = tka[kt]
                    p.need("s", tka[kt])
                    if kt == 0 and et_free is not None:
                        p.need("s", *et_free)
                    texp = p.prod("s", lambda s, kt=kt, bi=bi: s.activation(
                        s_et[:, kt * 512:(kt + 1) * 512], pAb[bi][:], AF.Exp,
                        bias=s_bias[:], scale=SCALING))
                    pA_tok[bi] = texp
                    p.need("t", texp)
                    for qt in range(4):
                        if kt == 0:
                            p.need("t", pO_free[qt])
                        def mme(t, kt=kt, qt=qt):
                            return t.matmul(pO[qt][:, 0:257],
                                            s_et[:, kt * 512 + qt * 128: kt * 512 + qt * 128 + 128],
                                            s_v[:, kt * 257:(kt + 1) * 257],
                                            start=(kt == 0), stop=(kt == NT - 1))
                        ev_last[qt] = p.prod("t", mme)
                    if kt + 4 < NT:
                        def mma2(t, half=half, kt2=kt + 4, qc=qc, bi=(kt + 4) % 4):
                            return t.matmul(pAb[bi][:],
                                            s_kt[:, half * T + kt2 * 128: half * T + (kt2 + 1) * 128],
                                            s_qt[:, half * T + qc * 512: half * T + (qc + 1) * 512],
                                            start=True, stop=True)
                        tka[kt + 4] = p.prod("t", mma2)
                et_free = list(ev_last)
                dstb = s_o1 if half == 0 else s_o2
                for qt in range(4):
                    p.need("v", ev_last[qt])
                    tkev = p.prod("v", lambda v, qt=qt, dstb=dstb: v.tensor_copy(
                        dstb[:, qt * 257:(qt + 1) * 257], pO[qt][:, 0:257]))
                    pO_free[qt] = tkev
                    if half == 0:
                        o1_evac[qt] = tkev
                    else:
                        ev_last[qt] = tkev
            for qt in range(4):
                p.need("v", ev_last[qt], o1_evac[qt], tlambc, trsbc)
                tqt = qc * 4 + qt
                c0, c1, c2, c5 = s_col[2][:], s_col[3][:], s_col[4][:], s_col[5][:]
                tk = p.prod("v", lambda v, qt=qt, c0=c0: v.reciprocal(
                    c0, s_o1[:, qt * 257 + 256: qt * 257 + 257]))
                tk2 = p.prod("v", lambda v, qt=qt, c1=c1: v.reciprocal(
                    c1, s_o2[:, qt * 257 + 256: qt * 257 + 257]))
                p.need("v", tk, tk2)
                tk = p.prod("v", lambda v, h=h, c1=c1, c2=c2: v.tensor_mul(
                    c2, c1, s_lambc[:, h:h + 1]))
                p.need("v", tk)
                tk = p.prod("v", lambda v, qt=qt, c2=c2: v.tensor_scalar(
                    s_t2[:], s_o2[:, qt * 257: qt * 257 + 256], c2, None, ALU.mult))
                p.need("v", tk)
                tk = p.prod("v", lambda v, qt=qt, c0=c0: v.scalar_tensor_tensor(
                    s_oc[:], s_o1[:, qt * 257: qt * 257 + 256], c0, s_t2[:],
                    ALU.mult, ALU.subtract))
                p.need("v", tk)
                tssq = p.prod("v", lambda v, c5=c5: v.scalar_tensor_tensor(
                    s_t2[:], s_oc[:], 1.0, s_oc[:], ALU.mult, ALU.mult, accum_out=c5))
                p.need("v", tssq)
                tk = p.prod("v", lambda v, c5=c5: v.tensor_scalar(
                    c5, c5, 1.0 / 256.0, EPS, ALU.mult, ALU.add))
                p.need("s", tk)
                tk = p.prod("s", lambda s, c5=c5: s.activation(c5, c5, AF.Ln))
                p.need("s", tk)
                trst = p.prod("s", lambda s, c5=c5: s.activation(c5, c5, AF.Exp, scale=-0.5))
                p.need("v", trst)
                p.need_dma("v", ds_ott)
                tk = p.prod("v", lambda v, c5=c5: v.tensor_scalar(
                    s_t2[:], s_oc[:], c5, None, ALU.mult))
                p.need("v", tk)
                totil = p.prod("v", lambda v, qt=qt: v.tensor_mul(
                    s_otil[qt % 2][:], s_t2[:], s_rsbc[:]))
                p.need("y", totil)
                for c in range(2):
                    p.dma("y", s_otT[:, c * T + tqt * 128: c * T + (tqt + 1) * 128],
                          s_otil[qt % 2][:, c * 128:(c + 1) * 128], ds_ott, transpose=True)
        qkv_t_done[h] = last_a_mm
        qkv_bank_init = dict(pA_tok)
        pQ_tok = {0: pA_tok.get(0), 1: pA_tok.get(1)}
        prev_head_consumers = [last_a_mm, p.token("s")]
        p.need_dma("y", ds_ott)
        for c in range(2):
            ec = 2 * h + c
            p.dma("y", sc_ot.ap()[ec * 128:(ec + 1) * 128, :], s_otT[:, c * T:(c + 1) * T], ds_ots)
        p.need_dma("y", ds_ots)

    p.join_all()
    nc.sbuf_base = mark0
    s_woT = sb("s_woT", [128, 2 * H * 1024], BF16)
    s_otq = [sb(f"s_otq{i}", [128, 2 * H * 512], BF16) for i in range(2)]
    s_ao4 = sb("s_ao4", [128, 4 * D], BF16)
    s_x4 = sb("s_x4", [128, 4 * D], F32)
    s_x24 = sb("s_x24", [128, 4 * D], F32)
    s_bn4 = sb("s_bn4", [128, 48], F32)
    s_mv4 = sb("s_mv4", [128, 8], F32)

    for ec in range(2 * H):
        p.dma("y", s_woT[:, ec * 1024:(ec + 1) * 1024], sc_wot.ap()[ec * 128:(ec + 1) * 128, :],
              ds_wor)
    p.need_dma("t", ds_wor)

    otq_used = [None, None]
    ao_used = None
    x4_used = None
    x2n_done_tok = None
    wo_bank_tok = {i: None for i in range(4)}
    wob = [pQ[0], pQ[1], pA[0], pA[1]]
    for qc in range(NQ):
        par = qc % 2
        p.need("y", otq_used[par])
        for ec in range(2 * H):
            p.dma("y", s_otq[par][:, ec * 512:(ec + 1) * 512],
                  sc_ot.ap()[ec * 128:(ec + 1) * 128, qc * 512:(qc + 1) * 512], ds_otr[par])
        p.need("y", x4_used)
        for i in range(4):
            p.dma("y", s_x4[:, i * D:(i + 1) * D], xre[qc * 4 + i], ds_xr[0])
        p.need_dma("t", ds_otr[par])
        evs = []
        for qt in range(4):
            for dc in range(2):
                bi = (qt * 2 + dc) % 4
                p.need("t", wo_bank_tok.get(bi))
                def mmw(t, par=par, qt=qt, dc=dc, bank=wob[bi]):
                    for ec in range(2 * H):
                        ins = t.matmul(bank[:],
                                       s_otq[par][:, ec * 512 + qt * 128: ec * 512 + qt * 128 + 128],
                                       s_woT[:, ec * 1024 + dc * 512: ec * 1024 + (dc + 1) * 512],
                                       start=(ec == 0), stop=(ec == 2 * H - 1))
                    return ins
                tmm = p.prod("t", mmw)
                p.need("v", tmm)
                p.need("v", ao_used)
                if cfg.debug:
                    p.need_dma("v", ds_out)
                wo_bank_tok[bi] = p.prod("v", lambda v, qt=qt, dc=dc, bi=bi: v.tensor_copy(
                    s_ao4[:, qt * D + dc * 512: qt * D + (dc + 1) * 512], wob[bi][:]))
                evs.append(wo_bank_tok[bi])
        otq_used[par] = p.token("t")
        p.need("v", *evs)
        st_toks = []
        for i in range(4):
            t1 = p.prod("v", lambda v, i=i: v.bn_stats(
                s_bn4[:, i * 12: i * 12 + 6], s_ao4[:, i * D: i * D + 512]))
            t2 = p.prod("v", lambda v, i=i: v.bn_stats(
                s_bn4[:, i * 12 + 6: i * 12 + 12], s_ao4[:, i * D + 512: i * D + 1024]))
            st_toks += [t1, t2]
        p.need("v", *st_toks)
        ag_toks = [p.prod("v", lambda v, i=i: v.bn_aggr(
            s_mv4[:, 2 * i: 2 * i + 2], s_bn4[:, i * 12: i * 12 + 12])) for i in range(4)]
        p.need("v", *ag_toks)
        ep_toks = [p.prod("v", lambda v, i=i: v.tensor_scalar_add(
            s_col[i][:], s_mv4[:, 2 * i + 1: 2 * i + 2], EPS)) for i in range(4)]
        p.need("s", *ep_toks)
        ln_toks = [p.prod("s", lambda sx, i=i: sx.activation(
            s_col[i][:], s_col[i][:], AF.Ln)) for i in range(4)]
        p.need("s", *ln_toks)
        rs_toks = [p.prod("s", lambda sx, i=i: sx.activation(
            s_col[i][:], s_col[i][:], AF.Exp, scale=-0.5)) for i in range(4)]
        p.need("v", *rs_toks)
        p.need_dma("v", ds_xr[0])
        p.need_dma("v", ds_x2)
        ap_toks = [p.prod("v", lambda v, i=i: v.tensor_scalar(
            s_x24[:, i * D:(i + 1) * D], s_ao4[:, i * D:(i + 1) * D],
            s_mv4[:, 2 * i: 2 * i + 1], s_col[i][:], ALU.subtract, ALU.mult))
            for i in range(4)]
        ao_used = ap_toks[-1]
        p.need("v", *ap_toks)
        ad_toks = [p.prod("v", lambda v, i=i: v.tensor_add(
            s_x24[:, i * D:(i + 1) * D], s_x24[:, i * D:(i + 1) * D],
            s_x4[:, i * D:(i + 1) * D])) for i in range(4)]
        x4_used = ad_toks[-1]
        p.need("y", *ad_toks)
        for i in range(4):
            p.dma("y", sc_x2.ap()[(qc * 4 + i) * 128:(qc * 4 + i + 1) * 128, :],
                  s_x24[:, i * D:(i + 1) * D], ds_x2)
        p.need("v", *ad_toks)
        st_toks = []
        for i in range(4):
            t1 = p.prod("v", lambda v, i=i: v.bn_stats(
                s_bn4[:, i * 12: i * 12 + 6], s_x24[:, i * D: i * D + 512]))
            t2 = p.prod("v", lambda v, i=i: v.bn_stats(
                s_bn4[:, i * 12 + 6: i * 12 + 12], s_x24[:, i * D + 512: i * D + 1024]))
            st_toks += [t1, t2]
        p.need("v", *st_toks)
        ag_toks = [p.prod("v", lambda v, i=i: v.bn_aggr(
            s_mv4[:, 2 * i: 2 * i + 2], s_bn4[:, i * 12: i * 12 + 12])) for i in range(4)]
        p.need("v", *ag_toks)
        ep_toks = [p.prod("v", lambda v, i=i: v.tensor_scalar_add(
            s_col[i][:], s_mv4[:, 2 * i + 1: 2 * i + 2], EPS)) for i in range(4)]
        p.need("s", *ep_toks)
        ln_toks = [p.prod("s", lambda sx, i=i: sx.activation(
            s_col[i][:], s_col[i][:], AF.Ln)) for i in range(4)]
        p.need("s", *ln_toks)
        rs_toks = [p.prod("s", lambda sx, i=i: sx.activation(
            s_col[i][:], s_col[i][:], AF.Exp, scale=-0.5)) for i in range(4)]
        p.need("v", *rs_toks)
        xn_toks = [p.prod("v", lambda v, i=i, qc=qc: v.tensor_scalar(
            s_x2n[:, (qc * 4 + i) * D:(qc * 4 + i + 1) * D], s_x24[:, i * D:(i + 1) * D],
            s_mv4[:, 2 * i: 2 * i + 1], s_col[i][:], ALU.subtract, ALU.mult))
            for i in range(4)]
        x2n_done_tok = xn_toks[-1]
        if cfg.debug:
            p.need("y", evs[-1])
            for i in range(4):
                p.dma("y", dbg["dbg_attnout"].ap()[:, (qc * 4 + i) * D:(qc * 4 + i + 1) * D],
                      s_ao4[:, i * D:(i + 1) * D], ds_out)

    p.join_all()
    nc.sbuf_base = mark0
    s_x2nT = [sb(f"s_x2nT{i}", [128, ND * 512], BF16) for i in range(2)]
    s_t1s = sb("s_t1s", [128, T], BF16)
    s_pu = sb("s_pu", [16, T], F32)
    s_probs = sb("s_probs", [16, T], F32)
    s_ew = sb("s_ew", [16, 16], F32)
    s_ones16 = sb("s_ones16", [16, T], F32)
    s_m16 = sb("s_m16", [16, T], F32)
    s_gt = sb("s_gt", [16, T], F32)
    s_eq = sb("s_eq", [16, T], F32)
    s_sel = sb("s_sel", [16, T], F32)
    s_sm = sb("s_sm", [16, T], F32)
    s_smb = sb("s_smb", [16, T], BF16)
    s_wpb = sb("s_wpb", [16, T], BF16)
    s_bis = sb("s_bis", [16, 12], F32)

    sil_toks = []
    x2nt_used = [None, None]
    for qc in range(NQ):
        par = qc % 2
        p.need("y", x2n_done_tok)
        p.need("y", x2nt_used[par])
        p.need_dma("y", ds_x2t[par])
        for tq in range(4):
            tt = qc * 4 + tq
            for j in range(ND):
                p.dma("y", s_x2nT[par][:, j * 512 + tq * 128: j * 512 + (tq + 1) * 128],
                      s_x2n[:, tt * D + j * 128: tt * D + (j + 1) * 128],
                      ds_x2t[par], transpose=True)
        p.need_dma("t", ds_x2t[par])
        bi = qc % 2
        p.need("t", pQ_tok.get(bi))
        def mmr(t, par=par, bank=pQ[bi]):
            for j in range(ND):
                ins = t.matmul(bank[:], s_rw1b[:, j * 128:(j + 1) * 128],
                               s_x2nT[par][:, j * 512:(j + 1) * 512],
                               start=(j == 0), stop=(j == ND - 1))
            return ins
        tmm = p.prod("t", mmr)
        x2nt_used[par] = tmm
        p.need("s", tmm)
        tsl = p.prod("s", lambda s, qc=qc, bi=bi: s.activation(
            s_t1s[:, qc * 512:(qc + 1) * 512], pQ[bi][:], AF.Silu))
        pQ_tok[bi] = tsl
        sil_toks.append(tsl)
    p.need_dma("t", ds_rw)
    lg_toks = []
    for qc in range(NQ):
        p.need("t", sil_toks[qc])
        def mml(t, qc=qc):
            return t.matmul(pO[qc % 4][0:16, 0:512], s_rw2b[:, 0:E],
                            s_t1s[:, qc * 512:(qc + 1) * 512], start=True, stop=True)
        lg_toks.append(p.prod("t", mml))
    p.need("v", *lg_toks)
    mx_toks = []
    for qc in range(NQ):
        mx_toks.append(p.prod("v", lambda v, qc=qc: v.reduce_max(
            s_ew[0:16, qc:qc + 1], pO[qc % 4][0:16, 0:512], AX.X)))
    p.need("v", *mx_toks)
    tk = p.prod("v", lambda v: v.reduce_max(s_ew[0:16, 4:5], s_ew[0:16, 0:NQ], AX.X))
    p.need("v", tk)
    tnm = p.prod("v", lambda v: v.tensor_scalar_mul(s_ew[0:16, 5:6], s_ew[0:16, 4:5], -1.0))
    p.need("s", tnm)
    ex_toks = []
    for qc in range(NQ):
        ex_toks.append(p.prod("s", lambda s, qc=qc: s.activation(
            s_pu[0:16, qc * 512:(qc + 1) * 512], pO[qc % 4][0:16, 0:512], AF.Exp,
            bias=s_ew[0:16, 5:6], accum_out=s_ew[0:16, 8 + qc: 9 + qc])))
    p.need("v", *ex_toks)
    tk = p.prod("v", lambda v: v.reduce_sum(s_ew[0:16, 6:7], s_ew[0:16, 8:8 + NQ], AX.X))
    p.need("v", tk)
    tk = p.prod("v", lambda v: v.reciprocal(s_ew[0:16, 7:8], s_ew[0:16, 6:7]))
    p.need("v", tk)
    tprobs = p.prod("v", lambda v: v.tensor_scalar(
        s_probs[0:16, :], s_pu[0:16, :], s_ew[0:16, 7:8], None, ALU.mult))
    if cfg.debug:
        p.need("y", tprobs)
        p.dma("y", dbg["dbg_probs"][:], s_probs[:], ds_out)

    t_ones16 = p.prod("v", lambda v: v.memset(s_ones16[:], 1.0))
    p.need("v", tprobs)
    tk = p.prod("v", lambda v: v.reduce_max(s_bis[0:16, 1:2], s_probs[0:16, :], AX.X))
    p.need("v", tk)
    tk = p.prod("v", lambda v: v.memset(s_bis[0:16, 0:1], 0.0))
    p.need("v", tk)
    tk = p.prod("v", lambda v: v.tensor_scalar_mul(s_bis[0:16, 2:3], s_bis[0:16, 1:2], 0.5))
    p.need("v", tk)
    for it in range(40):
        tk = p.prod("v", lambda v: v.tensor_scalar(
            s_m16[0:16, :], s_probs[0:16, :], s_bis[0:16, 2:3], None, ALU.is_gt, ALU.add,
            accum_out=s_bis[0:16, 3:4]))
        p.need("v", tk)
        tk = p.prod("v", lambda v: v.tensor_single_scalar(
            s_bis[0:16, 4:5], s_bis[0:16, 3:4], 256.0, ALU.is_ge))
        p.need("v", tk)
        tk = p.prod("v", lambda v: v.tensor_scalar(
            s_bis[0:16, 9:10], s_bis[0:16, 4:5], -1.0, 1.0, ALU.mult, ALU.add))
        p.need("v", tk)
        tk = p.prod("v", lambda v: v.tensor_scalar(
            s_bis[0:16, 8:9], s_bis[0:16, 2:3], s_bis[0:16, 4:5], None, ALU.mult))
        p.need("v", tk)
        tk = p.prod("v", lambda v: v.scalar_tensor_tensor(
            s_bis[0:16, 0:1], s_bis[0:16, 0:1], s_bis[0:16, 9:10], s_bis[0:16, 8:9],
            ALU.mult, ALU.add))
        p.need("v", tk)
        tk = p.prod("v", lambda v: v.tensor_scalar(
            s_bis[0:16, 8:9], s_bis[0:16, 2:3], s_bis[0:16, 9:10], None, ALU.mult))
        p.need("v", tk)
        tk = p.prod("v", lambda v: v.scalar_tensor_tensor(
            s_bis[0:16, 1:2], s_bis[0:16, 1:2], s_bis[0:16, 4:5], s_bis[0:16, 8:9],
            ALU.mult, ALU.add))
        p.need("v", tk)
        tk = p.prod("v", lambda v: v.tensor_add(s_bis[0:16, 2:3], s_bis[0:16, 0:1],
                                                s_bis[0:16, 1:2]))
        p.need("v", tk)
        tk = p.prod("v", lambda v: v.tensor_scalar_mul(s_bis[0:16, 2:3], s_bis[0:16, 2:3], 0.5))
        p.need("v", tk)
    tk = p.prod("v", lambda v: v.tensor_scalar(
        s_m16[0:16, :], s_probs[0:16, :], s_bis[0:16, 1:2], None, ALU.is_le))
    p.need("v", tk)
    tk = p.prod("v", lambda v: v.tensor_mul(s_m16[0:16, :], s_m16[0:16, :], s_probs[0:16, :]))
    p.need("v", tk)
    tk = p.prod("v", lambda v: v.reduce_max(s_bis[0:16, 5:6], s_m16[0:16, :], AX.X))
    p.need("v", tk)
    tk = p.prod("v", lambda v: v.tensor_scalar(
        s_gt[0:16, :], s_probs[0:16, :], s_bis[0:16, 5:6], None, ALU.is_gt, ALU.add,
        accum_out=s_bis[0:16, 3:4]))
    p.need("v", tk)
    tk = p.prod("v", lambda v: v.tensor_scalar(
        s_bis[0:16, 6:7], s_bis[0:16, 3:4], -1.0, 256.0, ALU.mult, ALU.add))
    p.need("v", tk)
    tk = p.prod("v", lambda v: v.tensor_scalar(
        s_eq[0:16, :], s_probs[0:16, :], s_bis[0:16, 5:6], None, ALU.is_equal))
    p.need("v", tk, t_ones16)
    tk = p.prod("v", lambda v: v.tensor_tensor_scan(
        s_m16[0:16, :], s_ones16[0:16, :], s_eq[0:16, :], 0.0, ALU.mult, ALU.add))
    p.need("v", tk)
    tk = p.prod("v", lambda v: v.tensor_sub(s_m16[0:16, :], s_m16[0:16, :], s_eq[0:16, :]))
    p.need("v", tk)
    tk = p.prod("v", lambda v: v.tensor_scalar(
        s_m16[0:16, :], s_m16[0:16, :], s_bis[0:16, 6:7], None, ALU.is_lt))
    p.need("v", tk)
    tk = p.prod("v", lambda v: v.tensor_mul(s_m16[0:16, :], s_m16[0:16, :], s_eq[0:16, :]))
    p.need("v", tk)
    tsel = p.prod("v", lambda v: v.tensor_add(s_sel[0:16, :], s_gt[0:16, :], s_m16[0:16, :]))
    p.need("v", tsel)
    tk = p.prod("v", lambda v: v.tensor_tensor_scan(
        s_sm[0:16, :], s_ones16[0:16, :], s_sel[0:16, :], 0.0, ALU.mult, ALU.add))
    p.need("v", tk)
    tk = p.prod("v", lambda v: v.tensor_sub(s_sm[0:16, :], s_sm[0:16, :], s_sel[0:16, :]))
    p.need("v", tk)
    tk = p.prod("v", lambda v: v.scalar_tensor_tensor(
        s_sm[0:16, :], s_sel[0:16, :], -384.0, s_sm[0:16, :], ALU.mult, ALU.add))
    p.need("v", tk)
    tk = p.prod("v", lambda v: v.tensor_scalar_add(s_sm[0:16, :], s_sm[0:16, :], 384.0))
    p.need("v", tk)
    tsmb = p.prod("v", lambda v: v.tensor_copy(s_smb[0:16, :], s_sm[0:16, :]))
    if cfg.debug:
        p.need("y", tsmb)
        p.dma("y", dbg["dbg_sm"][:], s_sm[:], ds_out)
    tk = p.prod("v", lambda v: v.scalar_tensor_tensor(
        s_m16[0:16, :], s_sel[0:16, :], 1.0, s_probs[0:16, :], ALU.mult, ALU.mult,
        accum_out=s_bis[0:16, 7:8]))
    p.need("v", tk)
    tk = p.prod("v", lambda v: v.tensor_scalar_add(s_bis[0:16, 7:8], s_bis[0:16, 7:8], 1e-9))
    p.need("v", tk)
    tSinv = p.prod("v", lambda v: v.reciprocal(s_bis[0:16, 7:8], s_bis[0:16, 7:8]))
    p.need("v", tSinv)
    twp = p.prod("v", lambda v: v.tensor_scalar(
        s_pu[0:16, :], s_probs[0:16, :], s_bis[0:16, 7:8], None, ALU.mult))
    p.need("v", twp)
    twpb = p.prod("v", lambda v: v.tensor_copy(s_wpb[0:16, :], s_pu[0:16, :]))
    p.need("y", tsmb, twpb)
    for tt in range(NT):
        p.dma("y", s_smT[:, tt * 16:(tt + 1) * 16], s_smb[0:16, tt * 128:(tt + 1) * 128],
              ds_smt, transpose=True)
        p.dma("y", s_probsT[:, tt * 16:(tt + 1) * 16], s_wpb[0:16, tt * 128:(tt + 1) * 128],
              ds_smt, transpose=True)
    p.need_dma("v", ds_smt)
    tsmtf = p.prod("v", lambda v: v.tensor_copy(s_smTf[:], s_smT[:]))

    p.join_all()
    nc.sbuf_base = mark0
    s_ye = sb("s_ye", [128, E * 2 * 1024], BF16)
    markF = nc.sbuf_base
    s_xgT = sb("s_xgT", [128, 2 * 1024], BF16)
    s_ohg = sb("s_ohg", [128, NT * 256], BF16)
    s_w1c = [sb(f"s_w1c{i}", [128, ND * 512], BF16) for i in range(2)]
    s_w2c = [sb(f"s_w2c{i}", [128, ND * 512], BF16) for i in range(2)]
    s_w3c = [sb(f"s_w3c{i}", [128, 4 * 1024], BF16) for i in range(2)]
    s_gs = sb("s_gs", [128, 1024], BF16)
    s_hb = sb("s_hb", [128, 1024], BF16)
    s_gp = sb("s_gp", [16, 256], F32)
    s_wrow = sb("s_wrow", [1, 256], BF16)
    s_wrf = sb("s_wrf", [1, 256], F32)
    s_wcol = sb("s_wcol", [128, 2], F32)

    gv_free = None
    ye_free = None
    gather_free = None
    gs_free = None
    ohg_consumed = None
    prev_tym = None

    s_wf1 = [sb(f"s_wf1{i}", [128, 1024], F32) for i in range(2)]
    s_wf2 = sb("s_wf2", [128, 1024], F32)
    ds_f1 = [p.dma_sem("ds_f1a"), p.dma_sem("ds_f1b")]
    ds_f2 = p.dma_sem("ds_f2b")
    ds_w3 = [p.dma_sem("ds_w3a"), p.dma_sem("ds_w3b")]
    stg1_free = [None, None]
    stg2_free = None
    wcast = [{1: None, 2: None} for _ in range(2)]
    w_cons = [{1: None, 2: None, 3: None} for _ in range(2)]

    def wsrc(k, e, dffc, qtr):
        w = w1 if k == 1 else w2
        return w.ap()[e * NDC + dffc][:, qtr * 1024:(qtr + 1) * 1024]

    def issue_w1(e, dffc, qtr):
        buf = qtr % 2
        p.need("y", stg1_free[buf])
        p.dma("y", s_wf1[buf][:], wsrc(1, e, dffc, qtr), ds_f1[buf])

    def cast_w1(qtr, par):
        buf = qtr % 2
        p.need_dma("v", ds_f1[buf])
        p.need("v", w_cons[par][1])
        tok = p.prod("v", lambda v, qtr=qtr, par=par, buf=buf: v.tensor_copy(
            s_w1c[par][:, qtr * 1024:(qtr + 1) * 1024], s_wf1[buf][:]))
        stg1_free[buf] = tok
        wcast[par][1] = tok

    stg2_box = [None]

    def issue_w2(e, dffc, qtr):
        p.need("s", stg2_box[0])
        p.dma("s", s_wf2[:], wsrc(2, e, dffc, qtr), ds_f2)

    def cast_w2(qtr, par):
        p.need_dma("s", ds_f2)
        p.need("s", w_cons[par][2])
        tok = p.prod("s", lambda s, qtr=qtr, par=par: s.activation(
            s_w2c[par][:, qtr * 1024:(qtr + 1) * 1024], s_wf2[:], AF.Copy))
        stg2_box[0] = tok
        wcast[par][2] = tok

    def load_w3(e, dffc, par):
        p.need("g", w_cons[par][3])
        p.dma("g", s_w3c[par][:], w3.ap()[e * NDC + dffc], ds_w3[par])

    load_w3(0, 0, 0)
    for _q in range(4):
        issue_w1(0, 0, _q)
        cast_w1(_q, 0)
    for _q in range(4):
        issue_w2(0, 0, _q)
        cast_w2(_q, 0)

    def build_onehots(e):
        toks = []
        for tt in range(NT):
            toks.append(p.prod("v", lambda v, tt=tt, e=e: v.tensor_scalar(
                s_ohg[:, tt * 256:(tt + 1) * 256], s_iota[:],
                s_smTf[:, tt * 16 + e: tt * 16 + e + 1], None, ALU.is_equal)))
        return toks

    p.need("v", tsmtf)
    ohg_toks = build_onehots(0)
    for e in range(E):
        p.need("t", *ohg_toks)
        p.need("t", x2n_done_tok)
        p.need("t", gv_free)
        p.need("t", ye_free)
        if e == 0:
            p.need("t", *ex_toks)
        gbanks8 = [pQ[0], pQ[1], pA[0], pA[1], pO[0], pO[1], pO[2], pO[3]]
        def mmg(t, e=e):
            for tt in range(NT):
                for dt in range(ND):
                    ins = t.matmul(gbanks8[dt][:, 0:256],
                                   s_x2n[:, tt * D + dt * 128: tt * D + dt * 128 + 128],
                                   s_ohg[:, tt * 256:(tt + 1) * 256],
                                   start=(tt == 0), stop=(tt == NT - 1))
            return ins
        tgm = p.prod("t", mmg)
        p.need("v", tgm)
        if prev_tym is not None:
            p.need("v", prev_tym)
        if cfg.debug:
            p.need_dma("v", ds_out)
        tge = []
        for b in range(8):
            tge.append(p.prod("v", lambda v, b=b: v.tensor_copy(
                s_xgT[:, b * 256:(b + 1) * 256], gbanks8[b][:, 0:256])))
        gather_free = tge[-1]

        p.need("t", *tge)
        def mmp(t, e=e):
            for tt in range(NT):
                ins = t.matmul(pO[0][0:16, 0:256], s_probsT[:, tt * 16:(tt + 1) * 16],
                               s_ohg[:, tt * 256:(tt + 1) * 256],
                               start=(tt == 0), stop=(tt == NT - 1))
            return ins
        tpm = p.prod("t", mmp)
        ohg_consumed = tpm
        p.need("v", tpm)
        twall = p.prod("v", lambda v: v.tensor_copy(s_gp[0:16, :], pO[0][0:16, 0:256]))
        p.need("g", twall)
        p.dma("g", s_wrow[:], s_gp.ap()[e:e + 1, :], ds_wr)
        p.need_dma("v", ds_wr)
        twrf = p.prod("v", lambda v: v.tensor_copy(s_wrf[:], s_wrow[:]))
        p.need("t", twrf)
        ttr0 = p.prod("t", lambda t: t.transpose(pO[1][:, 0:1], s_wrf[0:1, 0:128],
                                                 s_identf[0:1, 0:1]))
        ttr1 = p.prod("t", lambda t: t.transpose(pO[1][:, 1:2], s_wrf[0:1, 128:256],
                                                 s_identf[0:1, 0:1]))
        p.need("v", ttr0, ttr1)
        twcol = p.prod("v", lambda v: v.tensor_copy(s_wcol[:], pO[1][:, 0:2]))
        if e + 1 < E:
            p.need("v", tpm)
            ohg_toks = build_onehots(e + 1)

        hb_used = None
        for dffc in range(NDC):
            par = (e * NDC + dffc) % 2
            npar = (e * NDC + dffc + 1) % 2
            nxt = ((e, dffc + 1) if dffc + 1 < NDC
                   else ((e + 1, 0) if e + 1 < E else None))
            if nxt is not None:
                ne, nd = nxt
                load_w3(ne, nd, npar)
                issue_w1(ne, nd, 0)
                issue_w1(ne, nd, 1)
                cast_w1(0, npar)
                issue_w1(ne, nd, 2)
                cast_w1(1, npar)
                issue_w1(ne, nd, 3)
                cast_w1(2, npar)
                cast_w1(3, npar)
                issue_w2(ne, nd, 0)
            p.need("t", wcast[par][1])
            p.need("t", gather_free)
            fb = [pQ[0], pQ[1], pA[0], pA[1]]
            def mk_mm1(f, wc, par=par):
                def mm(t, f=f, wc=wc, par=par):
                    for j in range(ND):
                        g_ins = t.matmul(fb[f][:, 0:256],
                                         wc[par][:, j * 512 + f * 128: j * 512 + f * 128 + 128],
                                         s_xgT[:, j * 256:(j + 1) * 256],
                                         start=(j == 0), stop=(j == ND - 1))
                    return g_ins
                return mm
            tgg_f = [p.prod("t", mk_mm1(f, s_w1c)) for f in range(4)]
            w_cons[par][1] = tgg_f[-1]
            if gs_free is not None:
                p.need("s", gs_free)
            sil = []
            for f in range(4):
                p.need("s", tgg_f[f])
                sil.append(p.prod("s", lambda s, f=f: s.activation(
                    s_gs[:, f * 256:(f + 1) * 256], fb[f][:, 0:256], AF.Silu)))
            if nxt is not None:
                for qq in range(4):
                    cast_w2(qq, npar)
                    if qq < 3:
                        issue_w2(nxt[0], nxt[1], qq + 1)
            p.need("t", wcast[par][2])
            tgv_f = []
            for f in range(4):
                p.need("t", sil[f])
                tgv_f.append(p.prod("t", mk_mm1(f, s_w2c)))
            w_cons[par][2] = tgv_f[-1]
            if hb_used is not None:
                p.need("v", hb_used)
            muls = []
            for f in range(4):
                p.need("v", tgv_f[f])
                p.need("v", sil[f])
                muls.append(p.prod("v", lambda v, f=f: v.tensor_mul(
                    s_hb[:, f * 256:(f + 1) * 256], fb[f][:, 0:256],
                    s_gs[:, f * 256:(f + 1) * 256])))
            thb = muls[-1]
            gs_free = thb
            gv_free = thb
            p.need_dma("t", ds_w3[par])
            if dffc == 0:
                p.need("t", tpm, ttr0, ttr1, twall, twcol)
            tym = None
            for f in range(4):
                p.need("t", muls[f])
                def mmyf(t, dffc=dffc, par=par, f=f):
                    for st in range(2):
                        for dc in range(2):
                            ins = t.matmul(
                                pO[2 * st + dc][:, 0:512],
                                s_hb[:, f * 256 + st * 128: f * 256 + st * 128 + 128],
                                s_w3c[par][:, f * 1024 + dc * 512: f * 1024 + (dc + 1) * 512],
                                start=(dffc == 0 and f == 0), stop=(dffc == NDC - 1 and f == 3))
                    return ins
                tym = p.prod("t", mmyf)
            w_cons[par][3] = tym
            hb_used = tym
        p.need("v", hb_used, twcol)
        yev = []
        for st in range(2):
            for dc in range(2):
                yev.append(p.prod("v", lambda v, st=st, dc=dc, e=e: v.tensor_scalar(
                    s_ye[:, e * 2048 + st * 1024 + dc * 512: e * 2048 + st * 1024 + (dc + 1) * 512],
                    pO[2 * st + dc][:, 0:512], s_wcol[:, st:st + 1], None, ALU.mult)))
        ye_free = yev[-1]
        p.need("t", ye_free)
        prev_tym = hb_used

        if cfg.debug and e == 0:
            p.need("y", *yev)
            p.dma("y", dbg["dbg_ye"][:], s_ye[:, 0:2048], ds_out)
            p.need("y", tge[-1])
            p.dma("y", dbg["dbg_xgt"][:], s_xgT[:], ds_out)
            p.need("y", twcol)
            p.dma("y", dbg["dbg_wcol"][:], s_wcol[:], ds_out)

    p.join_all()
    nc.sbuf_base = markF
    s_ohgs = [sb(f"s_ohgs{i}", [128, 2 * 256], BF16) for i in range(2)]
    s_ohs = [sb(f"s_ohs{i}", [128, 4 * 128], BF16) for i in range(2)]
    s_ffn = sb("s_ffn", [128, D], BF16)
    s_outt = sb("s_outt", [128, D], F32)

    NTC = NT // 2
    nc.psum_base = 0
    scat_banks = [nc.alloc_psum_tensor(f"pS{i}", [128, 512], F32) for i in range(4)]
    pT = nc.alloc_psum_tensor("pT", [128, 512], BF16)
    p.need_dma("y", ds_x2)
    ohs_used = [None, None]
    ohgs_guard = [None, None]
    pT_free = None
    bank_free = {i: None for i in range(4)}
    for tc in range(NTC):
        mm_last = {i: None for i in range(4)}
        for e in range(E):
            par = e % 2
            p.need("v", tsmtf)
            p.need("v", ohgs_guard[par])
            og = []
            for ts_ in range(2):
                tt = tc * 2 + ts_
                og.append(p.prod("v", lambda v, ts_=ts_, tt=tt, e=e, par=par: v.tensor_scalar(
                    s_ohgs[par][:, ts_ * 256:(ts_ + 1) * 256], s_iota[:],
                    s_smTf[:, tt * 16 + e: tt * 16 + e + 1], None, ALU.is_equal)))
            p.need("t", *og)
            p.need("t", pT_free)
            ttr = []
            for ts_ in range(2):
                for st in range(2):
                    ttr.append(p.prod("t", lambda t, ts_=ts_, st=st, par=par: t.transpose(
                        pT[:, (st * 2 + ts_) * 128:(st * 2 + ts_ + 1) * 128],
                        s_ohgs[par][:, ts_ * 256 + st * 128: ts_ * 256 + st * 128 + 128],
                        s_identb[:])))
            ohgs_guard[par] = ttr[-1]
            p.need("v", *ttr)
            p.need("v", ohs_used[par])
            tev = []
            for k4 in range(4):
                tev.append(p.prod("v", lambda v, k4=k4, par=par: v.tensor_copy(
                    s_ohs[par][:, k4 * 128:(k4 + 1) * 128], pT[:, k4 * 128:(k4 + 1) * 128])))
            pT_free = tev[-1]
            p.need("t", *tev)
            if e == 0:
                for bi in range(4):
                    p.need("t", bank_free[bi])
            for ts_ in range(2):
                for dc in range(2):
                    bi = ts_ * 2 + dc
                    def mms(t, e=e, ts_=ts_, dc=dc, par=par, bi=bi):
                        for st in range(2):
                            ins = t.matmul(
                                scat_banks[bi][:],
                                s_ohs[par][:, (st * 2 + ts_) * 128:(st * 2 + ts_ + 1) * 128],
                                s_ye[:, e * 2048 + st * 1024 + dc * 512:
                                     e * 2048 + st * 1024 + (dc + 1) * 512],
                                start=(e == 0 and st == 0), stop=(e == E - 1 and st == 1))
                        return ins
                    mm_last[bi] = p.prod("t", mms)
            ohs_used[par] = mm_last[3]
        for ts_ in range(2):
            tt = tc * 2 + ts_
            par = tt % 2
            p.dma("y", s_xstage[par][:], sc_x2.ap()[tt * 128:(tt + 1) * 128, :], ds_x2r[par])
            ev = []
            for dc in range(2):
                bi = ts_ * 2 + dc
                p.need("v", mm_last[bi])
                ev.append(p.prod("v", lambda v, dc=dc, bi=bi: v.tensor_copy(
                    s_ffn[:, dc * 512:(dc + 1) * 512], scat_banks[bi][:])))
                bank_free[bi] = ev[-1]
            p.need("v", *ev)
            tst = ln_stats(s_ffn, s_mv[par][:])
            p.need("v", tst)
            trs = rstd_chain(s_mv[par], s_col[par][:])
            p.need("v", trs)
            p.need_dma("v", ds_x2r[par])
            tk = p.prod("v", lambda v, par=par: v.tensor_scalar(
                s_f32t[:], s_ffn[:], s_mv[par][:, 0:1], s_col[par][:],
                ALU.subtract, ALU.mult))
            p.need("v", tk)
            p.need_dma("v", ds_out)
            tout = p.prod("v", lambda v, par=par: v.tensor_add(s_outt[:], s_f32t[:],
                                                               s_xstage[par][:]))
            p.need("y", tout)
            p.dma("y", out_t.ap()[tt * 128:(tt + 1) * 128, :], s_outt[:], ds_out)

    p.join_all()
    with nc.Block() as block:
        p.realize(block)
    nc.finalize()
    return nc, din, dbg


_CACHE = {}


def _reorder_w(w1, w2, w3):
    E, D, DFF = w1.shape
    NDC = DFF // 512
    ND = D // 128
    w1r = np.ascontiguousarray(
        w1.reshape(E, ND, 128, NDC, 512).transpose(0, 3, 2, 1, 4)
    ).reshape(E * NDC, 128, ND * 512)
    w2r = np.ascontiguousarray(
        w2.reshape(E, ND, 128, NDC, 512).transpose(0, 3, 2, 1, 4)
    ).reshape(E * NDC, 128, ND * 512)
    w3r = np.ascontiguousarray(
        w3.reshape(E, NDC, 4, 128, D).transpose(0, 1, 3, 2, 4)
    ).reshape(E * NDC, 128, 4 * D)
    return w1r, w2r, w3r


def kernel(L0_attn_X, L0_attn_Wq, L0_attn_Wk, L0_attn_Wv, L0_attn_Wo,
           L0_attn_lambda_q1, L0_attn_lambda_k1, L0_attn_lambda_q2, L0_attn_lambda_k2,
           L0_attn_rms_scale, L0_ffn_router_w1, L0_ffn_router_w2,
           L0_ffn_w1, L0_ffn_w2, L0_ffn_w3):
    cfg = Cfg()
    if "nc" not in _CACHE:
        _CACHE["nc"] = build_program(cfg)[0]
    nc = _CACHE["nc"]
    consts = host_constants()
    X = np.ascontiguousarray(np.asarray(L0_attn_X, dtype=np.float32))
    base = {
        "wq": np.ascontiguousarray(np.asarray(L0_attn_Wq, np.float32)),
        "wk": np.ascontiguousarray(np.asarray(L0_attn_Wk, np.float32)),
        "wv": np.ascontiguousarray(np.asarray(L0_attn_Wv, np.float32)),
        "wo": np.ascontiguousarray(np.asarray(L0_attn_Wo, np.float32)),
        "lq1": np.ascontiguousarray(np.asarray(L0_attn_lambda_q1, np.float32)),
        "lk1": np.ascontiguousarray(np.asarray(L0_attn_lambda_k1, np.float32)),
        "lq2": np.ascontiguousarray(np.asarray(L0_attn_lambda_q2, np.float32)),
        "lk2": np.ascontiguousarray(np.asarray(L0_attn_lambda_k2, np.float32)),
        "rmss": np.ascontiguousarray(np.asarray(L0_attn_rms_scale, np.float32).reshape(1, 256)),
        "rw1": np.ascontiguousarray(np.asarray(L0_ffn_router_w1, np.float32)),
        "rw2": np.ascontiguousarray(np.asarray(L0_ffn_router_w2, np.float32)),
    }
    base["w1"], base["w2"], base["w3"] = _reorder_w(
        np.asarray(L0_ffn_w1, np.float32), np.asarray(L0_ffn_w2, np.float32),
        np.asarray(L0_ffn_w3, np.float32))
    base.update(consts)
    in_maps = []
    for c in range(8):
        m = dict(base)
        m["x"] = np.ascontiguousarray(X[c])
        in_maps.append(m)
    res = run_bass_kernel_spmd(nc, in_maps, core_ids=list(range(8)))
    out = np.stack([np.asarray(r["out"], np.float32) for r in res.results], axis=0)
    return out


# version 18
# speedup vs baseline: 1.0131x; 1.0131x over previous
import math
import numpy as np
import ml_dtypes
import concourse.bass as bass
import concourse.mybir as mybir
from concourse.bass_utils import run_bass_kernel_spmd

F32 = mybir.dt.float32
BF16 = mybir.dt.bfloat16
AF = mybir.ActivationFunctionType
ALU = mybir.AluOpType
AX = mybir.AxisListType
bfdt = ml_dtypes.bfloat16

LAM_INIT = 0.8 - 0.6 * math.exp(-0.3)
SCALING = 1.0 / math.sqrt(128.0)
EPS = 1e-5


class Cfg:
    def __init__(self, tok=2048, heads=8, experts=16, dff=2048, debug=False, expose=False):
        self.expose = expose
        self.tok = tok; self.heads = heads; self.experts = experts
        self.dff = dff; self.d = 1024
        self.cap = tok * 2 // experts
        assert self.cap == 256 and experts == 16 and dff % 512 == 0
        self.nt = tok // 128
        self.nd = self.d // 128
        self.nq = tok // 512
        self.ndffc = dff // 512
        self.debug = debug


class DmaSem:
    def __init__(self, nc, name):
        self.sem = nc.alloc_semaphore(name)
        self.issued = 0


class Plan:
    ENGS = ("t", "v", "s", "g", "y")

    def __init__(self, nc):
        self.nc = nc
        self.ops = {k: [] for k in self.ENGS}
        self.sem = {k: nc.alloc_semaphore(f"prog_{k}") for k in self.ENGS}
        self.cnt = {k: 0 for k in self.ENGS}
        self.waited = {k: {j: 0 for j in self.ENGS} for k in self.ENGS}
        self.dma_waited = {k: {} for k in self.ENGS}
        self.dsems = []

    def dma_sem(self, name):
        ds = DmaSem(self.nc, name)
        self.dsems.append(ds)
        return ds

    def emit(self, e, fn):
        self.ops[e].append(fn)

    def prod(self, e, fn):
        self.cnt[e] += 1
        lvl = self.cnt[e]
        s = self.sem[e]
        self.ops[e].append(lambda eng, fn=fn, s=s: fn(eng).then_inc(s, 1))
        return (e, lvl)

    def need(self, e, *tokens):
        for tok in tokens:
            if tok is None:
                continue
            pe, lvl = tok
            if pe == e:
                if self.waited[e][e] < lvl:
                    self.ops[e].append(lambda eng: eng.drain())
                    self.waited[e][e] = self.cnt[e]
            else:
                if self.waited[e][pe] < lvl:
                    s = self.sem[pe]
                    self.ops[e].append(lambda eng, s=s, L=lvl: eng.wait_ge(s, L))
                    self.waited[e][pe] = lvl

    def dma(self, q, out_ap, in_ap, ds, **kw):
        ds.issued += 16
        self.ops[q].append(
            lambda eng, o=out_ap, i=in_ap, s=ds.sem, kw=kw: eng.dma_start(o, i, **kw).then_inc(s, 16))

    def need_dma(self, e, ds):
        lvl = ds.issued
        if lvl and self.dma_waited[e].get(id(ds), 0) < lvl:
            self.ops[e].append(lambda eng, s=ds.sem, L=lvl: eng.wait_ge(s, L))
            self.dma_waited[e][id(ds)] = lvl

    def token(self, e):
        return (e, self.cnt[e])

    def join_all(self):
        for e in self.ENGS:
            for pe in self.ENGS:
                if pe != e:
                    self.need(e, (pe, self.cnt[pe]))
            for ds in self.dsems:
                self.need_dma(e, ds)

    def realize(self, block):
        block.tensor(lambda eng: [f(eng) for f in self.ops["t"]])
        block.vector(lambda eng: [f(eng) for f in self.ops["v"]])
        block.scalar(lambda eng: [f(eng) for f in self.ops["s"]])
        block.gpsimd(lambda eng: [f(eng) for f in self.ops["g"]])
        block.sync(lambda eng: [f(eng) for f in self.ops["y"]])


def host_constants():
    return {
        "c_identf": np.eye(128, dtype=np.float32),
        "c_iota_bc": np.broadcast_to(np.arange(256, dtype=np.float32), (128, 256)).astype(bfdt).copy(),
        "c_iotacol0": np.arange(128, dtype=np.float32).reshape(128, 1),
        "c_iotacol1": (np.arange(128, dtype=np.float32) + 128.0).reshape(128, 1),
        "c_onesrow_b": np.ones((1, 128), dtype=bfdt),
        "c_onesrow_f": np.ones((1, 128), dtype=np.float32),
        "c_onescol_b": np.ones((128, 1), dtype=bfdt),
        "c_identb": np.eye(128, dtype=np.float32).astype(bfdt),
        "c_bar": np.zeros((1, 8), dtype=np.float32),
    }


def build_program(cfg):
    nc = bass.Bass("TRN2", target_bir_lowering=False)
    T, H, E, DFF, D = cfg.tok, cfg.heads, cfg.experts, cfg.dff, cfg.d
    NT, ND, NQ, NDC = cfg.nt, cfg.nd, cfg.nq, cfg.ndffc
    EH = 256

    din = {}
    def dram_in(name, shape, dt=F32):
        din[name] = nc.dram_tensor(name, list(shape), dt, kind="ExternalInput")
        return din[name]

    x_in = dram_in("x", (T, D))
    wq = dram_in("wq", (EH * H, D)); wk = dram_in("wk", (EH * H, D))
    wv = dram_in("wv", (EH * H, D)); wo = dram_in("wo", (D, EH * H))
    lq1 = dram_in("lq1", (H, 128)); lk1 = dram_in("lk1", (H, 128))
    lq2 = dram_in("lq2", (H, 128)); lk2 = dram_in("lk2", (H, 128))
    rmss = dram_in("rmss", (1, EH))
    rw1 = dram_in("rw1", (D, 128)); rw2 = dram_in("rw2", (128, E))
    w1 = dram_in("w1", (E * NDC, 128, ND * 512)); w2 = dram_in("w2", (E * NDC, 128, ND * 512))
    w3 = dram_in("w3", (E * NDC, 128, 4 * 1024))
    identf = dram_in("c_identf", (128, 128))
    iota_bc = dram_in("c_iota_bc", (128, 256), BF16)
    iotacol0 = dram_in("c_iotacol0", (128, 1))
    iotacol1 = dram_in("c_iotacol1", (128, 1))
    onesrow_b = dram_in("c_onesrow_b", (1, 128), BF16)
    onesrow_f = dram_in("c_onesrow_f", (1, 128))
    onescol_b = dram_in("c_onescol_b", (128, 1), BF16)
    identb = dram_in("c_identb", (128, 128), BF16)
    dram_in("c_bar", (1, 8), F32)

    out_t = nc.dram_tensor("out", [T, D], F32, kind="ExternalOutput")
    dbgkind = "ExternalOutput" if (cfg.debug or cfg.expose) else "Internal"
    sc_ot = nc.dram_tensor("sc_ot", [2 * H * 128, T], BF16, kind=dbgkind)
    sc_x2 = nc.dram_tensor("sc_x2", [T, D], F32, kind=dbgkind)
    sc_wot = nc.dram_tensor("sc_wot", [2 * H * 128, 1024], BF16, kind="Internal")
    dbg = {}
    if cfg.debug:
        for nm, shape, dt in [
            ("dbg_xnt", (128, ND * T), BF16),
            ("dbg_probs", (16, T), F32),
            ("dbg_sm", (16, T), F32),
            ("dbg_attnout", (128, NT * D), BF16),
            ("dbg_ye", (128, 2 * 1024), BF16),
            ("dbg_xgt", (128, 2 * 1024), BF16),
            ("dbg_wcol", (128, 2), F32),
        ]:
            dbg[nm] = nc.dram_tensor(nm, list(shape), dt, kind="ExternalOutput")

    sb = nc.alloc_sbuf_tensor
    s_identf = sb("s_identf", [128, 128], F32)
    s_iota = sb("s_iota", [128, 256], BF16)
    s_ic0 = sb("s_ic0", [128, 1], F32)
    s_ic1 = sb("s_ic1", [128, 1], F32)
    s_or_b = sb("s_or_b", [1, 128], BF16)
    s_or_f = sb("s_or_f", [1, 128], F32)
    s_oc_b = sb("s_oc_b", [128, 1], BF16)
    s_identb = sb("s_identb", [128, 128], BF16)
    s_rsbc = sb("s_rsbc", [128, 256], F32)
    s_lam = sb("s_lam", [H, 1], F32)
    s_lamt = sb("s_lamt", [1, H], F32)
    s_lambc = sb("s_lambc", [128, H], F32)
    s_rw1b = sb("s_rw1b", [128, ND * 128], BF16)
    s_rw2b = sb("s_rw2b", [128, E], BF16)
    s_scal = sb("s_scal", [1, 16], F32)
    s_col = [sb(f"s_col{i}", [128, 1], F32) for i in range(8)]
    s_bn = sb("s_bn", [128, 12], F32)
    s_mv = [sb(f"s_mv{i}", [128, 2], F32) for i in range(2)]
    s_xstage = [sb(f"s_xstage{i}", [128, D], F32) for i in range(2)]
    s_xn = [sb(f"s_xn{i}", [128, D], BF16) for i in range(2)]
    s_f32t = sb("s_f32t", [128, D], F32)
    s_x2tile = sb("s_x2tile", [128, D], F32)
    s_lq = sb("s_lq", [H, 128], F32); s_lk = sb("s_lk", [H, 128], F32)
    s_lq2 = sb("s_lq2", [H, 128], F32); s_lk2 = sb("s_lk2", [H, 128], F32)
    s_rms = sb("s_rms", [1, 256], F32)
    s_x2n = sb("s_x2n", [128, NT * D], BF16)
    s_smT = sb("s_smT", [128, NT * 16], BF16)
    s_smTf = sb("s_smTf", [128, NT * 16], F32)
    s_probsT = sb("s_probsT", [128, NT * 16], BF16)

    p = Plan(nc)
    ds_c = p.dma_sem("ds_c")
    ds_rw = p.dma_sem("ds_rw")
    ds_x = [p.dma_sem(f"ds_x{i}") for i in range(2)]
    ds_xt = p.dma_sem("ds_xt")
    ds_w = [p.dma_sem(f"ds_w{i}") for i in range(2)]
    ds_wt = [p.dma_sem(f"ds_wt{i}") for i in range(2)]
    ds_ott = p.dma_sem("ds_ott")
    ds_ots = p.dma_sem("ds_ots")
    ds_wos = p.dma_sem("ds_wos")
    ds_wor = p.dma_sem("ds_wor")
    ds_otr = [p.dma_sem(f"ds_otr{i}") for i in range(2)]
    ds_x2 = p.dma_sem("ds_x2")
    ds_x2t = [p.dma_sem(f"ds_x2t{i}") for i in range(2)]
    ds_xr = [p.dma_sem(f"ds_xr{i}") for i in range(2)]
    ds_smt = p.dma_sem("ds_smt")
    ds_wr = p.dma_sem("ds_wr")
    ds_oht = [p.dma_sem(f"ds_oht{i}") for i in range(2)]
    ds_x2r = [p.dma_sem(f"ds_x2r{i}") for i in range(2)]
    ds_out = p.dma_sem("ds_out")

    nc.psum_base = 0
    pQ = [nc.alloc_psum_tensor(f"pQ{i}", [128, 512], F32) for i in range(2)]
    pA = [nc.alloc_psum_tensor(f"pA{i}", [128, 512], F32) for i in range(2)]
    pO = [nc.alloc_psum_tensor(f"pO{i}", [128, 512], F32) for i in range(4)]

    for dst, src in [(s_identf, identf), (s_iota, iota_bc), (s_ic0, iotacol0),
                     (s_ic1, iotacol1), (s_or_b, onesrow_b), (s_or_f, onesrow_f),
                     (s_oc_b, onescol_b), (s_identb, identb), (s_lq, lq1), (s_lk, lk1), (s_lq2, lq2),
                     (s_lk2, lk2), (s_rms, rmss)]:
        p.dma("y", dst[:], src[:], ds_c)
    p.dma("g", s_rw1b.ap().rearrange("p (c f) -> p c f", c=ND),
          rw1.ap().rearrange("(c p) f -> p c f", p=128), ds_rw)
    p.dma("g", s_rw2b[:], rw2[:], ds_rw)

    p.need_dma("v", ds_c)
    s_ltmp = sb("s_ltmp", [H, 128], F32)
    s_l1 = sb("s_l1", [H, 1], F32); s_l2 = sb("s_l2", [H, 1], F32)
    tk = p.prod("v", lambda v: v.tensor_mul(s_ltmp[:], s_lq[:], s_lk[:]))
    p.need("v", tk)
    t1 = p.prod("v", lambda v: v.reduce_sum(s_l1[:], s_ltmp[:], AX.X))
    p.need("v", t1)
    tk = p.prod("v", lambda v: v.tensor_mul(s_ltmp[:], s_lq2[:], s_lk2[:]))
    p.need("v", tk)
    t2 = p.prod("v", lambda v: v.reduce_sum(s_l2[:], s_ltmp[:], AX.X))
    p.need("s", t1, t2)
    te1 = p.prod("s", lambda s: s.activation(s_l1[:], s_l1[:], AF.Exp))
    p.need("s", te1)
    te2 = p.prod("s", lambda s: s.activation(s_l2[:], s_l2[:], AF.Exp))
    p.need("v", te1, te2)
    tk = p.prod("v", lambda v: v.tensor_sub(s_lam[:], s_l1[:], s_l2[:]))
    p.need("v", tk)
    tlam = p.prod("v", lambda v: v.tensor_scalar_add(s_lam[:], s_lam[:], LAM_INIT))
    p.need("t", tlam)
    p.need_dma("t", ds_c)
    tk = p.prod("t", lambda t: t.transpose(pQ[0][0:1, 0:H], s_lam[:], s_identf[0:H, 0:H]))
    p.need("v", tk)
    tk = p.prod("v", lambda v: v.tensor_copy(s_lamt[:], pQ[0][0:1, 0:H]))
    p.need("t", tk)
    tk = p.prod("t", lambda t: t.matmul(pQ[1][:, 0:H], s_or_f[:], s_lamt[:], start=True, stop=True))
    p.need("v", tk)
    tlambc = p.prod("v", lambda v: v.tensor_copy(s_lambc[:], pQ[1][:, 0:H]))
    tk = p.prod("t", lambda t: t.matmul(pQ[0][:, 0:256], s_or_f[:], s_rms[:], start=True, stop=True))
    p.need("v", tk)
    trsbc = p.prod("v", lambda v: v.tensor_scalar_mul(s_rsbc[:], pQ[0][:, 0:256], 1.0 - LAM_INIT))
    pQ_tok = {0: trsbc, 1: tlambc}

    def ln_stats(tile_ap, mv_ap):
        t1 = p.prod("v", lambda v: v.bn_stats(s_bn[:, 0:6], tile_ap[:, 0:512]))
        t2 = p.prod("v", lambda v: v.bn_stats(s_bn[:, 6:12], tile_ap[:, 512:1024]))
        p.need("v", t1, t2)
        return p.prod("v", lambda v: v.bn_aggr(mv_ap, s_bn[:]))

    def rstd_chain(mv_ap, out_col):
        tv = p.prod("v", lambda v: v.tensor_scalar_add(out_col, mv_ap[:, 1:2], EPS))
        p.need("s", tv)
        t1 = p.prod("s", lambda s: s.activation(out_col, out_col, AF.Ln))
        p.need("s", t1)
        return p.prod("s", lambda s: s.activation(out_col, out_col, AF.Exp, scale=-0.5))

    mark0 = nc.sbuf_base
    s_xnT = sb("s_xnT", [128, ND * T], BF16)
    s_wsf = [sb(f"s_wsf{i}", [128, 1024], F32) for i in range(2)]
    s_wsb = [sb(f"s_wsb{i}", [128, 1024], BF16) for i in range(2)]
    s_wqT = [sb(f"s_wqT{i}", [128, ND * 256], BF16) for i in range(2)]
    s_wkT = [sb(f"s_wkT{i}", [128, ND * 256], BF16) for i in range(2)]
    s_wvT = [sb(f"s_wvT{i}", [128, ND * 256], BF16) for i in range(2)]
    s_woTh = sb("s_woTh", [128, 2 * 1024], BF16)
    s_qt = sb("s_qt", [128, 2 * T], BF16)
    s_kt = sb("s_kt", [128, 2 * T], BF16)
    s_v = sb("s_v", [128, NT * 257], BF16)
    s_sq = sb("s_sq", [128, 2 * T], BF16)
    s_et = sb("s_et", [128, NT * 512], BF16)
    s_o1 = sb("s_o1", [128, 4 * 257], F32)
    s_o2 = sb("s_o2", [128, 4 * 257], F32)
    s_oc = sb("s_oc", [128, 256], F32)
    s_t2 = sb("s_t2", [128, 256], F32)
    s_otil = [sb(f"s_otil{i}", [128, 256], BF16) for i in range(2)]
    s_otT = sb("s_otT", [128, 2 * T], BF16)
    s_bias = sb("s_bias", [128, 1], F32)

    xre = x_in.ap().rearrange("(n p) f -> n p f", p=128)
    txn_toks = []
    for i in range(NT):
        par = i % 2
        if i >= 2:
            p.need("y", txn_toks[i - 2])
        p.dma("y", s_xstage[par][:], xre[i], ds_x[par])
        p.need_dma("v", ds_x[par])
        if i >= 2:
            p.need_dma("v", ds_xt)
        tst = ln_stats(s_xstage[par], s_mv[par][:])
        p.need("v", tst)
        trs = rstd_chain(s_mv[par], s_col[par][:])
        p.need("v", trs)
        txn = p.prod("v", lambda v, par=par: v.tensor_scalar(
            s_xn[par][:], s_xstage[par][:], s_mv[par][:, 0:1], s_col[par][:],
            ALU.subtract, ALU.mult))
        txn_toks.append(txn)
        p.need("y", txn)
        for j in range(ND):
            p.dma("y", s_xnT[:, j * T + i * 128: j * T + (i + 1) * 128],
                  s_xn[par][:, j * 128:(j + 1) * 128], ds_xt, transpose=True)
    if cfg.debug:
        p.need_dma("y", ds_xt)
        p.dma("y", dbg["dbg_xnt"][:], s_xnT[:], ds_out)

    wslice_done = {}

    wcv_tok = [None, None]
    wtb_done = [None, None]

    def head_weight_load(h, par, prev_qkv_t):
        p.need("y", prev_qkv_t)
        idx = 0
        for wi, (wsrc, dstbuf) in enumerate([(wq, s_wqT[par]), (wk, s_wkT[par]),
                                             (wv, s_wvT[par]), (wo, None)]):
            for c in range(2):
                sp = idx % 2
                idx += 1
                p.need("s", wcv_tok[sp])
                if wi < 3:
                    src = wsrc.ap()[h * EH + c * 128: h * EH + (c + 1) * 128, :]
                else:
                    src = wo.ap()[4 * c * 128:(4 * c + 4) * 128,
                                  h * EH:(h + 1) * EH].rearrange("(c p) f -> p c f", p=128)
                p.dma("s", s_wsf[sp][:] if wi < 3 else
                      s_wsf[sp].ap().rearrange("p (c f) -> p c f", c=4), src, ds_w[sp])
                p.need_dma("v", ds_w[sp])
                p.need_dma("v", ds_wt[0])
                p.need_dma("v", ds_wt[1])
                tcv = p.prod("v", lambda v, sp=sp: v.tensor_copy(s_wsb[sp][:], s_wsf[sp][:]))
                wcv_tok[sp] = tcv
                p.need("y", tcv)
                if wi < 3:
                    for j in range(ND):
                        p.dma("y", dstbuf[:, j * 256 + c * 128: j * 256 + (c + 1) * 128],
                              s_wsb[sp][:, j * 128:(j + 1) * 128], ds_wt[par], transpose=True)
                else:
                    if c == 0:
                        p.need_dma("y", ds_wos)
                    for cp in range(4):
                        jd = 4 * c + cp
                        for ch in range(2):
                            p.dma("y", s_woTh[:, ch * 1024 + jd * 128: ch * 1024 + (jd + 1) * 128],
                                  s_wsb[sp][:, cp * 256 + ch * 128: cp * 256 + (ch + 1) * 128],
                                  ds_wt[par], transpose=True)
        p.need_dma("y", ds_wt[par])
        for ch in range(2):
            ec = 2 * h + ch
            p.dma("y", sc_wot.ap()[ec * 128:(ec + 1) * 128, :],
                  s_woTh[:, ch * 1024:(ch + 1) * 1024], ds_wos)

    head_weight_load(0, 0, None)
    p.need_dma("t", ds_xt)

    qkv_bank_init = {0: pQ_tok.get(0), 1: pQ_tok.get(1), 2: None, 3: None}
    qkv_t_done = {}
    prev_head_consumers = None
    for h in range(H):
        par = h % 2
        if h + 1 < H:
            head_weight_load(h + 1, (h + 1) % 2, qkv_t_done.get(h - 1))
        p.need_dma("t", ds_wt[par])

        if prev_head_consumers is not None:
            p.need("v", *prev_head_consumers)
        qkvb = [pQ[0], pQ[1], pA[0], pA[1]]
        bank_tok = dict(qkv_bank_init)
        gi = 0
        for (wbuf, obuf) in [(s_wqT[par], s_qt), (s_wkT[par], s_kt)]:
            for et in range(2):
                for qc in range(NQ):
                    bi = gi % 4
                    gi += 1
                    p.need("t", bank_tok.get(bi))
                    def mmf(t, wbuf=wbuf, et=et, qc=qc, bank=qkvb[bi]):
                        for j in range(ND):
                            ins = t.matmul(bank[:],
                                           wbuf[:, j * 256 + et * 128: j * 256 + et * 128 + 128],
                                           s_xnT[:, j * T + qc * 512: j * T + (qc + 1) * 512],
                                           start=(j == 0), stop=(j == ND - 1))
                        return ins
                    tmm = p.prod("t", mmf)
                    p.need("v", tmm)
                    bank_tok[bi] = p.prod("v", lambda v, obuf=obuf, et=et, qc=qc, bi=bi:
                                          v.tensor_copy(obuf[:, et * T + qc * 512: et * T + (qc + 1) * 512],
                                                        qkvb[bi][:]))
        if h == 0:
            t_vones = p.prod("v", lambda v: v.memset(s_v[:], 1.0))
            p.need("t", t_vones)
        for tt in range(NT):
            bi = gi % 4
            gi += 1
            p.need("t", bank_tok.get(bi))
            def mmv(t, tt=tt, bank=qkvb[bi], par=par):
                for j in range(ND):
                    ins = t.matmul(bank[:, 0:256],
                                   s_xnT[:, j * T + tt * 128: j * T + tt * 128 + 128],
                                   s_wvT[par][:, j * 256:(j + 1) * 256],
                                   start=(j == 0), stop=(j == ND - 1))
                return ins
            tmm = p.prod("t", mmv)
            p.need("v", tmm)
            bank_tok[bi] = p.prod("v", lambda v, tt=tt, bi=bi:
                                  v.tensor_copy(s_v[:, tt * 257: tt * 257 + 256], qkvb[bi][:, 0:256]))
        qkv_done_v = p.token("v")

        p.need("s", qkv_done_v)
        sq_consumed = None
        pA0_tok = None
        last_red = None
        for bi_src, srcbuf in [(0, s_qt), (1, s_kt)]:
            if sq_consumed is not None:
                p.need("s", sq_consumed)
            tsq0 = p.prod("s", lambda s, srcbuf=srcbuf: s.activation(
                s_sq[:, 0:T], srcbuf[:, 0:T], AF.Square))
            tsq1 = p.prod("s", lambda s, srcbuf=srcbuf: s.activation(
                s_sq[:, T:2 * T], srcbuf[:, T:2 * T], AF.Square))
            p.need("t", tsq0, tsq1)
            red_toks = []
            last_mm = None
            for ch in range(NQ):
                if last_red is not None:
                    p.need("t", last_red)
                def mmn(t, ch=ch):
                    t.matmul(pO[0][0:1, 0:512], s_oc_b[:], s_sq[:, ch * 512:(ch + 1) * 512],
                             start=True, stop=False)
                    return t.matmul(pO[0][0:1, 0:512], s_oc_b[:],
                                    s_sq[:, T + ch * 512: T + (ch + 1) * 512],
                                    start=False, stop=True)
                last_mm = p.prod("t", mmn)
                p.need("v", last_mm)
                last_red = p.prod("v", lambda v, ch=ch: v.reduce_max(
                    s_scal[0:1, ch:ch + 1], pO[0][0:1, 0:512], AX.X))
                red_toks.append(last_red)
            p.need("v", *red_toks)
            tmx = p.prod("v", lambda v, bi_src=bi_src: v.reduce_max(
                s_scal[0:1, 8 + bi_src: 9 + bi_src], s_scal[0:1, 0:NQ], AX.X))
            p.need("v", tmx)
            sq_consumed = last_mm
            pA0_tok = red_toks[-1]
        tmul = p.prod("v", lambda v: v.tensor_mul(s_scal[0:1, 10:11], s_scal[0:1, 8:9],
                                                  s_scal[0:1, 9:10]))
        p.need("s", tmul)
        tln = p.prod("s", lambda s: s.activation(s_scal[0:1, 10:11], s_scal[0:1, 10:11], AF.Ln))
        p.need("s", tln)
        tex = p.prod("s", lambda s: s.activation(s_scal[0:1, 10:11], s_scal[0:1, 10:11],
                                                 AF.Exp, scale=0.5))
        p.need("v", tex)
        tbn = p.prod("v", lambda v: v.tensor_scalar_mul(s_scal[0:1, 10:11],
                                                        s_scal[0:1, 10:11], -SCALING))
        p.need("t", tbn, pA0_tok)
        tbb = p.prod("t", lambda t: t.matmul(pO[0][:, 0:1], s_or_f[:], s_scal[0:1, 10:11],
                                             start=True, stop=True))
        p.need("v", tbb)
        tbias = p.prod("v", lambda v: v.tensor_copy(s_bias[:], pO[0][:, 0:1]))
        pAb = [pQ[0], pQ[1], pA[0], pA[1]]
        pA_tok = dict(bank_tok)

        p.need("s", tbias)
        pO_free = {0: tbias, 1: None, 2: None, 3: None}
        et_free = None
        last_a_mm = None
        for qc in range(NQ):
            o1_evac = [None] * 4
            ev_last = [None] * 4
            for half in range(2):
                def mk_a(kt):
                    bi = kt % 4
                    p.need("t", pA_tok.get(bi))
                    def mma(t, half=half, kt=kt, qc=qc, bi=bi):
                        return t.matmul(pAb[bi][:],
                                        s_kt[:, half * T + kt * 128: half * T + (kt + 1) * 128],
                                        s_qt[:, half * T + qc * 512: half * T + (qc + 1) * 512],
                                        start=True, stop=True)
                    return p.prod("t", mma)
                tka = {}
                for kt in range(min(4, NT)):
                    tka[kt] = mk_a(kt)
                for kt in range(NT):
                    bi = kt % 4
                    last_a_mm = tka[kt]
                    p.need("s", tka[kt])
                    if kt == 0 and et_free is not None:
                        p.need("s", *et_free)
                    texp = p.prod("s", lambda s, kt=kt, bi=bi: s.activation(
                        s_et[:, kt * 512:(kt + 1) * 512], pAb[bi][:], AF.Exp,
                        bias=s_bias[:], scale=SCALING))
                    pA_tok[bi] = texp
                    p.need("t", texp)
                    for qt in range(4):
                        if kt == 0:
                            p.need("t", pO_free[qt])
                        def mme(t, kt=kt, qt=qt):
                            return t.matmul(pO[qt][:, 0:257],
                                            s_et[:, kt * 512 + qt * 128: kt * 512 + qt * 128 + 128],
                                            s_v[:, kt * 257:(kt + 1) * 257],
                                            start=(kt == 0), stop=(kt == NT - 1))
                        ev_last[qt] = p.prod("t", mme)
                    if kt + 4 < NT:
                        def mma2(t, half=half, kt2=kt + 4, qc=qc, bi=(kt + 4) % 4):
                            return t.matmul(pAb[bi][:],
                                            s_kt[:, half * T + kt2 * 128: half * T + (kt2 + 1) * 128],
                                            s_qt[:, half * T + qc * 512: half * T + (qc + 1) * 512],
                                            start=True, stop=True)
                        tka[kt + 4] = p.prod("t", mma2)
                et_free = list(ev_last)
                dstb = s_o1 if half == 0 else s_o2
                for qt in range(4):
                    p.need("v", ev_last[qt])
                    tkev = p.prod("v", lambda v, qt=qt, dstb=dstb: v.tensor_copy(
                        dstb[:, qt * 257:(qt + 1) * 257], pO[qt][:, 0:257]))
                    pO_free[qt] = tkev
                    if half == 0:
                        o1_evac[qt] = tkev
                    else:
                        ev_last[qt] = tkev
            for qt in range(4):
                p.need("v", ev_last[qt], o1_evac[qt], tlambc, trsbc)
                tqt = qc * 4 + qt
                c0, c1, c2, c5 = s_col[2][:], s_col[3][:], s_col[4][:], s_col[5][:]
                tk = p.prod("v", lambda v, qt=qt, c0=c0: v.reciprocal(
                    c0, s_o1[:, qt * 257 + 256: qt * 257 + 257]))
                tk2 = p.prod("v", lambda v, qt=qt, c1=c1: v.reciprocal(
                    c1, s_o2[:, qt * 257 + 256: qt * 257 + 257]))
                p.need("v", tk, tk2)
                tk = p.prod("v", lambda v, h=h, c1=c1, c2=c2: v.tensor_mul(
                    c2, c1, s_lambc[:, h:h + 1]))
                p.need("v", tk)
                tk = p.prod("v", lambda v, qt=qt, c2=c2: v.tensor_scalar(
                    s_t2[:], s_o2[:, qt * 257: qt * 257 + 256], c2, None, ALU.mult))
                p.need("v", tk)
                tk = p.prod("v", lambda v, qt=qt, c0=c0: v.scalar_tensor_tensor(
                    s_oc[:], s_o1[:, qt * 257: qt * 257 + 256], c0, s_t2[:],
                    ALU.mult, ALU.subtract))
                p.need("v", tk)
                tssq = p.prod("v", lambda v, c5=c5: v.scalar_tensor_tensor(
                    s_t2[:], s_oc[:], 1.0, s_oc[:], ALU.mult, ALU.mult, accum_out=c5))
                p.need("v", tssq)
                tk = p.prod("v", lambda v, c5=c5: v.tensor_scalar(
                    c5, c5, 1.0 / 256.0, EPS, ALU.mult, ALU.add))
                p.need("s", tk)
                tk = p.prod("s", lambda s, c5=c5: s.activation(c5, c5, AF.Ln))
                p.need("s", tk)
                trst = p.prod("s", lambda s, c5=c5: s.activation(c5, c5, AF.Exp, scale=-0.5))
                p.need("v", trst)
                p.need_dma("v", ds_ott)
                tk = p.prod("v", lambda v, c5=c5: v.tensor_scalar(
                    s_t2[:], s_oc[:], c5, None, ALU.mult))
                p.need("v", tk)
                totil = p.prod("v", lambda v, qt=qt: v.tensor_mul(
                    s_otil[qt % 2][:], s_t2[:], s_rsbc[:]))
                p.need("y", totil)
                for c in range(2):
                    p.dma("y", s_otT[:, c * T + tqt * 128: c * T + (tqt + 1) * 128],
                          s_otil[qt % 2][:, c * 128:(c + 1) * 128], ds_ott, transpose=True)
        qkv_t_done[h] = last_a_mm
        qkv_bank_init = dict(pA_tok)
        pQ_tok = {0: pA_tok.get(0), 1: pA_tok.get(1)}
        prev_head_consumers = [last_a_mm, p.token("s")]
        p.need_dma("y", ds_ott)
        for c in range(2):
            ec = 2 * h + c
            p.dma("y", sc_ot.ap()[ec * 128:(ec + 1) * 128, :], s_otT[:, c * T:(c + 1) * T], ds_ots)
        p.need_dma("y", ds_ots)

    p.join_all()
    nc.sbuf_base = mark0
    s_woT = sb("s_woT", [128, 2 * H * 1024], BF16)
    s_otq = [sb(f"s_otq{i}", [128, 2 * H * 512], BF16) for i in range(2)]
    s_ao4 = sb("s_ao4", [128, 4 * D], BF16)
    s_x4 = sb("s_x4", [128, 4 * D], F32)
    s_x24 = sb("s_x24", [128, 4 * D], F32)
    s_bn4 = sb("s_bn4", [128, 48], F32)
    s_mv4 = sb("s_mv4", [128, 8], F32)

    for ec in range(2 * H):
        p.dma("y", s_woT[:, ec * 1024:(ec + 1) * 1024], sc_wot.ap()[ec * 128:(ec + 1) * 128, :],
              ds_wor)
    p.need_dma("t", ds_wor)

    otq_used = [None, None]
    ao_used = None
    x4_used = None
    x2n_done_tok = None
    wo_bank_tok = {i: None for i in range(4)}
    wob = [pQ[0], pQ[1], pA[0], pA[1]]
    for qc in range(NQ):
        par = qc % 2
        p.need("y", otq_used[par])
        for ec in range(2 * H):
            p.dma("y", s_otq[par][:, ec * 512:(ec + 1) * 512],
                  sc_ot.ap()[ec * 128:(ec + 1) * 128, qc * 512:(qc + 1) * 512], ds_otr[par])
        p.need("y", x4_used)
        for i in range(4):
            p.dma("y", s_x4[:, i * D:(i + 1) * D], xre[qc * 4 + i], ds_xr[0])
        p.need_dma("t", ds_otr[par])
        evs = []
        for qt in range(4):
            for dc in range(2):
                bi = (qt * 2 + dc) % 4
                p.need("t", wo_bank_tok.get(bi))
                def mmw(t, par=par, qt=qt, dc=dc, bank=wob[bi]):
                    for ec in range(2 * H):
                        ins = t.matmul(bank[:],
                                       s_otq[par][:, ec * 512 + qt * 128: ec * 512 + qt * 128 + 128],
                                       s_woT[:, ec * 1024 + dc * 512: ec * 1024 + (dc + 1) * 512],
                                       start=(ec == 0), stop=(ec == 2 * H - 1))
                    return ins
                tmm = p.prod("t", mmw)
                p.need("v", tmm)
                p.need("v", ao_used)
                if cfg.debug:
                    p.need_dma("v", ds_out)
                wo_bank_tok[bi] = p.prod("v", lambda v, qt=qt, dc=dc, bi=bi: v.tensor_copy(
                    s_ao4[:, qt * D + dc * 512: qt * D + (dc + 1) * 512], wob[bi][:]))
                evs.append(wo_bank_tok[bi])
        otq_used[par] = p.token("t")
        p.need("v", *evs)
        st_toks = []
        for i in range(4):
            t1 = p.prod("v", lambda v, i=i: v.bn_stats(
                s_bn4[:, i * 12: i * 12 + 6], s_ao4[:, i * D: i * D + 512]))
            t2 = p.prod("v", lambda v, i=i: v.bn_stats(
                s_bn4[:, i * 12 + 6: i * 12 + 12], s_ao4[:, i * D + 512: i * D + 1024]))
            st_toks += [t1, t2]
        p.need("v", *st_toks)
        ag_toks = [p.prod("v", lambda v, i=i: v.bn_aggr(
            s_mv4[:, 2 * i: 2 * i + 2], s_bn4[:, i * 12: i * 12 + 12])) for i in range(4)]
        p.need("v", *ag_toks)
        ep_toks = [p.prod("v", lambda v, i=i: v.tensor_scalar_add(
            s_col[i][:], s_mv4[:, 2 * i + 1: 2 * i + 2], EPS)) for i in range(4)]
        p.need("s", *ep_toks)
        ln_toks = [p.prod("s", lambda sx, i=i: sx.activation(
            s_col[i][:], s_col[i][:], AF.Ln)) for i in range(4)]
        p.need("s", *ln_toks)
        rs_toks = [p.prod("s", lambda sx, i=i: sx.activation(
            s_col[i][:], s_col[i][:], AF.Exp, scale=-0.5)) for i in range(4)]
        p.need("v", *rs_toks)
        p.need_dma("v", ds_xr[0])
        p.need_dma("v", ds_x2)
        ap_toks = [p.prod("v", lambda v, i=i: v.tensor_scalar(
            s_x24[:, i * D:(i + 1) * D], s_ao4[:, i * D:(i + 1) * D],
            s_mv4[:, 2 * i: 2 * i + 1], s_col[i][:], ALU.subtract, ALU.mult))
            for i in range(4)]
        ao_used = ap_toks[-1]
        p.need("v", *ap_toks)
        ad_toks = [p.prod("v", lambda v, i=i: v.tensor_add(
            s_x24[:, i * D:(i + 1) * D], s_x24[:, i * D:(i + 1) * D],
            s_x4[:, i * D:(i + 1) * D])) for i in range(4)]
        x4_used = ad_toks[-1]
        p.need("y", *ad_toks)
        for i in range(4):
            p.dma("y", sc_x2.ap()[(qc * 4 + i) * 128:(qc * 4 + i + 1) * 128, :],
                  s_x24[:, i * D:(i + 1) * D], ds_x2)
        p.need("v", *ad_toks)
        st_toks = []
        for i in range(4):
            t1 = p.prod("v", lambda v, i=i: v.bn_stats(
                s_bn4[:, i * 12: i * 12 + 6], s_x24[:, i * D: i * D + 512]))
            t2 = p.prod("v", lambda v, i=i: v.bn_stats(
                s_bn4[:, i * 12 + 6: i * 12 + 12], s_x24[:, i * D + 512: i * D + 1024]))
            st_toks += [t1, t2]
        p.need("v", *st_toks)
        ag_toks = [p.prod("v", lambda v, i=i: v.bn_aggr(
            s_mv4[:, 2 * i: 2 * i + 2], s_bn4[:, i * 12: i * 12 + 12])) for i in range(4)]
        p.need("v", *ag_toks)
        ep_toks = [p.prod("v", lambda v, i=i: v.tensor_scalar_add(
            s_col[i][:], s_mv4[:, 2 * i + 1: 2 * i + 2], EPS)) for i in range(4)]
        p.need("s", *ep_toks)
        ln_toks = [p.prod("s", lambda sx, i=i: sx.activation(
            s_col[i][:], s_col[i][:], AF.Ln)) for i in range(4)]
        p.need("s", *ln_toks)
        rs_toks = [p.prod("s", lambda sx, i=i: sx.activation(
            s_col[i][:], s_col[i][:], AF.Exp, scale=-0.5)) for i in range(4)]
        p.need("v", *rs_toks)
        xn_toks = [p.prod("v", lambda v, i=i, qc=qc: v.tensor_scalar(
            s_x2n[:, (qc * 4 + i) * D:(qc * 4 + i + 1) * D], s_x24[:, i * D:(i + 1) * D],
            s_mv4[:, 2 * i: 2 * i + 1], s_col[i][:], ALU.subtract, ALU.mult))
            for i in range(4)]
        x2n_done_tok = xn_toks[-1]
        if cfg.debug:
            p.need("y", evs[-1])
            for i in range(4):
                p.dma("y", dbg["dbg_attnout"].ap()[:, (qc * 4 + i) * D:(qc * 4 + i + 1) * D],
                      s_ao4[:, i * D:(i + 1) * D], ds_out)

    p.join_all()
    nc.sbuf_base = mark0
    s_x2nT = [sb(f"s_x2nT{i}", [128, ND * 512], BF16) for i in range(2)]
    s_t1s = sb("s_t1s", [128, T], BF16)
    s_pu = sb("s_pu", [16, T], F32)
    s_probs = sb("s_probs", [16, T], F32)
    s_ew = sb("s_ew", [16, 16], F32)
    s_ones16 = sb("s_ones16", [16, T], F32)
    s_m16 = sb("s_m16", [16, T], F32)
    s_gt = sb("s_gt", [16, T], F32)
    s_eq = sb("s_eq", [16, T], F32)
    s_sel = sb("s_sel", [16, T], F32)
    s_sm = sb("s_sm", [16, T], F32)
    s_smb = sb("s_smb", [16, T], BF16)
    s_wpb = sb("s_wpb", [16, T], BF16)
    s_bis = sb("s_bis", [16, 12], F32)

    sil_toks = []
    x2nt_used = [None, None]
    for qc in range(NQ):
        par = qc % 2
        p.need("y", x2n_done_tok)
        p.need("y", x2nt_used[par])
        p.need_dma("y", ds_x2t[par])
        for tq in range(4):
            tt = qc * 4 + tq
            for j in range(ND):
                p.dma("y", s_x2nT[par][:, j * 512 + tq * 128: j * 512 + (tq + 1) * 128],
                      s_x2n[:, tt * D + j * 128: tt * D + (j + 1) * 128],
                      ds_x2t[par], transpose=True)
        p.need_dma("t", ds_x2t[par])
        bi = qc % 2
        p.need("t", pQ_tok.get(bi))
        def mmr(t, par=par, bank=pQ[bi]):
            for j in range(ND):
                ins = t.matmul(bank[:], s_rw1b[:, j * 128:(j + 1) * 128],
                               s_x2nT[par][:, j * 512:(j + 1) * 512],
                               start=(j == 0), stop=(j == ND - 1))
            return ins
        tmm = p.prod("t", mmr)
        x2nt_used[par] = tmm
        p.need("s", tmm)
        tsl = p.prod("s", lambda s, qc=qc, bi=bi: s.activation(
            s_t1s[:, qc * 512:(qc + 1) * 512], pQ[bi][:], AF.Silu))
        pQ_tok[bi] = tsl
        sil_toks.append(tsl)
    p.need_dma("t", ds_rw)
    lg_toks = []
    for qc in range(NQ):
        p.need("t", sil_toks[qc])
        def mml(t, qc=qc):
            return t.matmul(pO[qc % 4][0:16, 0:512], s_rw2b[:, 0:E],
                            s_t1s[:, qc * 512:(qc + 1) * 512], start=True, stop=True)
        lg_toks.append(p.prod("t", mml))
    p.need("v", *lg_toks)
    mx_toks = []
    for qc in range(NQ):
        mx_toks.append(p.prod("v", lambda v, qc=qc: v.reduce_max(
            s_ew[0:16, qc:qc + 1], pO[qc % 4][0:16, 0:512], AX.X)))
    p.need("v", *mx_toks)
    tk = p.prod("v", lambda v: v.reduce_max(s_ew[0:16, 4:5], s_ew[0:16, 0:NQ], AX.X))
    p.need("v", tk)
    tnm = p.prod("v", lambda v: v.tensor_scalar_mul(s_ew[0:16, 5:6], s_ew[0:16, 4:5], -1.0))
    p.need("s", tnm)
    ex_toks = []
    for qc in range(NQ):
        ex_toks.append(p.prod("s", lambda s, qc=qc: s.activation(
            s_pu[0:16, qc * 512:(qc + 1) * 512], pO[qc % 4][0:16, 0:512], AF.Exp,
            bias=s_ew[0:16, 5:6], accum_out=s_ew[0:16, 8 + qc: 9 + qc])))
    p.need("v", *ex_toks)
    tk = p.prod("v", lambda v: v.reduce_sum(s_ew[0:16, 6:7], s_ew[0:16, 8:8 + NQ], AX.X))
    p.need("v", tk)
    tk = p.prod("v", lambda v: v.reciprocal(s_ew[0:16, 7:8], s_ew[0:16, 6:7]))
    p.need("v", tk)
    tprobs = p.prod("v", lambda v: v.tensor_scalar(
        s_probs[0:16, :], s_pu[0:16, :], s_ew[0:16, 7:8], None, ALU.mult))
    if cfg.debug:
        p.need("y", tprobs)
        p.dma("y", dbg["dbg_probs"][:], s_probs[:], ds_out)

    t_ones16 = p.prod("v", lambda v: v.memset(s_ones16[:], 1.0))
    p.need("v", tprobs)
    tk = p.prod("v", lambda v: v.reduce_max(s_bis[0:16, 1:2], s_probs[0:16, :], AX.X))
    p.need("v", tk)
    tk = p.prod("v", lambda v: v.memset(s_bis[0:16, 0:1], 0.0))
    p.need("v", tk)
    tk = p.prod("v", lambda v: v.tensor_scalar_mul(s_bis[0:16, 2:3], s_bis[0:16, 1:2], 0.5))
    p.need("v", tk)
    for it in range(40):
        tk = p.prod("v", lambda v: v.tensor_scalar(
            s_m16[0:16, :], s_probs[0:16, :], s_bis[0:16, 2:3], None, ALU.is_gt, ALU.add,
            accum_out=s_bis[0:16, 3:4]))
        p.need("v", tk)
        tk = p.prod("v", lambda v: v.tensor_single_scalar(
            s_bis[0:16, 4:5], s_bis[0:16, 3:4], 256.0, ALU.is_ge))
        p.need("v", tk)
        tk = p.prod("v", lambda v: v.tensor_scalar(
            s_bis[0:16, 9:10], s_bis[0:16, 4:5], -1.0, 1.0, ALU.mult, ALU.add))
        p.need("v", tk)
        tk = p.prod("v", lambda v: v.tensor_scalar(
            s_bis[0:16, 8:9], s_bis[0:16, 2:3], s_bis[0:16, 4:5], None, ALU.mult))
        p.need("v", tk)
        tk = p.prod("v", lambda v: v.scalar_tensor_tensor(
            s_bis[0:16, 0:1], s_bis[0:16, 0:1], s_bis[0:16, 9:10], s_bis[0:16, 8:9],
            ALU.mult, ALU.add))
        p.need("v", tk)
        tk = p.prod("v", lambda v: v.tensor_scalar(
            s_bis[0:16, 8:9], s_bis[0:16, 2:3], s_bis[0:16, 9:10], None, ALU.mult))
        p.need("v", tk)
        tk = p.prod("v", lambda v: v.scalar_tensor_tensor(
            s_bis[0:16, 1:2], s_bis[0:16, 1:2], s_bis[0:16, 4:5], s_bis[0:16, 8:9],
            ALU.mult, ALU.add))
        p.need("v", tk)
        tk = p.prod("v", lambda v: v.tensor_add(s_bis[0:16, 2:3], s_bis[0:16, 0:1],
                                                s_bis[0:16, 1:2]))
        p.need("v", tk)
        tk = p.prod("v", lambda v: v.tensor_scalar_mul(s_bis[0:16, 2:3], s_bis[0:16, 2:3], 0.5))
        p.need("v", tk)
    tk = p.prod("v", lambda v: v.tensor_scalar(
        s_m16[0:16, :], s_probs[0:16, :], s_bis[0:16, 1:2], None, ALU.is_le))
    p.need("v", tk)
    tk = p.prod("v", lambda v: v.tensor_mul(s_m16[0:16, :], s_m16[0:16, :], s_probs[0:16, :]))
    p.need("v", tk)
    tk = p.prod("v", lambda v: v.reduce_max(s_bis[0:16, 5:6], s_m16[0:16, :], AX.X))
    p.need("v", tk)
    tk = p.prod("v", lambda v: v.tensor_scalar(
        s_gt[0:16, :], s_probs[0:16, :], s_bis[0:16, 5:6], None, ALU.is_gt, ALU.add,
        accum_out=s_bis[0:16, 3:4]))
    p.need("v", tk)
    tk = p.prod("v", lambda v: v.tensor_scalar(
        s_bis[0:16, 6:7], s_bis[0:16, 3:4], -1.0, 256.0, ALU.mult, ALU.add))
    p.need("v", tk)
    tk = p.prod("v", lambda v: v.tensor_scalar(
        s_eq[0:16, :], s_probs[0:16, :], s_bis[0:16, 5:6], None, ALU.is_equal))
    p.need("v", tk, t_ones16)
    tk = p.prod("v", lambda v: v.tensor_tensor_scan(
        s_m16[0:16, :], s_ones16[0:16, :], s_eq[0:16, :], 0.0, ALU.mult, ALU.add))
    p.need("v", tk)
    tk = p.prod("v", lambda v: v.tensor_sub(s_m16[0:16, :], s_m16[0:16, :], s_eq[0:16, :]))
    p.need("v", tk)
    tk = p.prod("v", lambda v: v.tensor_scalar(
        s_m16[0:16, :], s_m16[0:16, :], s_bis[0:16, 6:7], None, ALU.is_lt))
    p.need("v", tk)
    tk = p.prod("v", lambda v: v.tensor_mul(s_m16[0:16, :], s_m16[0:16, :], s_eq[0:16, :]))
    p.need("v", tk)
    tsel = p.prod("v", lambda v: v.tensor_add(s_sel[0:16, :], s_gt[0:16, :], s_m16[0:16, :]))
    p.need("v", tsel)
    tk = p.prod("v", lambda v: v.tensor_tensor_scan(
        s_sm[0:16, :], s_ones16[0:16, :], s_sel[0:16, :], 0.0, ALU.mult, ALU.add))
    p.need("v", tk)
    tk = p.prod("v", lambda v: v.tensor_sub(s_sm[0:16, :], s_sm[0:16, :], s_sel[0:16, :]))
    p.need("v", tk)
    tk = p.prod("v", lambda v: v.scalar_tensor_tensor(
        s_sm[0:16, :], s_sel[0:16, :], -384.0, s_sm[0:16, :], ALU.mult, ALU.add))
    p.need("v", tk)
    tk = p.prod("v", lambda v: v.tensor_scalar_add(s_sm[0:16, :], s_sm[0:16, :], 384.0))
    p.need("v", tk)
    tsmb = p.prod("v", lambda v: v.tensor_copy(s_smb[0:16, :], s_sm[0:16, :]))
    if cfg.debug:
        p.need("y", tsmb)
        p.dma("y", dbg["dbg_sm"][:], s_sm[:], ds_out)
    tk = p.prod("v", lambda v: v.scalar_tensor_tensor(
        s_m16[0:16, :], s_sel[0:16, :], 1.0, s_probs[0:16, :], ALU.mult, ALU.mult,
        accum_out=s_bis[0:16, 7:8]))
    p.need("v", tk)
    tk = p.prod("v", lambda v: v.tensor_scalar_add(s_bis[0:16, 7:8], s_bis[0:16, 7:8], 1e-9))
    p.need("v", tk)
    tSinv = p.prod("v", lambda v: v.reciprocal(s_bis[0:16, 7:8], s_bis[0:16, 7:8]))
    p.need("v", tSinv)
    twp = p.prod("v", lambda v: v.tensor_scalar(
        s_pu[0:16, :], s_probs[0:16, :], s_bis[0:16, 7:8], None, ALU.mult))
    p.need("v", twp)
    twpb = p.prod("v", lambda v: v.tensor_copy(s_wpb[0:16, :], s_pu[0:16, :]))
    p.need("y", tsmb, twpb)
    for tt in range(NT):
        p.dma("y", s_smT[:, tt * 16:(tt + 1) * 16], s_smb[0:16, tt * 128:(tt + 1) * 128],
              ds_smt, transpose=True)
        p.dma("y", s_probsT[:, tt * 16:(tt + 1) * 16], s_wpb[0:16, tt * 128:(tt + 1) * 128],
              ds_smt, transpose=True)
    p.need_dma("v", ds_smt)
    tsmtf = p.prod("v", lambda v: v.tensor_copy(s_smTf[:], s_smT[:]))

    p.join_all()
    nc.sbuf_base = mark0
    s_ye = sb("s_ye", [128, E * 2 * 1024], BF16)
    markF = nc.sbuf_base
    s_xgT = sb("s_xgT", [128, 2 * 1024], BF16)
    s_ohg = sb("s_ohg", [128, NT * 256], BF16)
    s_w1c = [sb(f"s_w1c{i}", [128, ND * 512], BF16) for i in range(2)]
    s_w2c = [sb(f"s_w2c{i}", [128, ND * 512], BF16) for i in range(2)]
    s_w3c = [sb(f"s_w3c{i}", [128, 4 * 1024], BF16) for i in range(2)]
    s_gs = sb("s_gs", [128, 1024], BF16)
    s_hb = sb("s_hb", [128, 1024], BF16)
    s_gp = sb("s_gp", [16, 256], F32)
    s_wrow = sb("s_wrow", [1, 256], BF16)
    s_wrf = sb("s_wrf", [1, 256], F32)
    s_wcol = sb("s_wcol", [128, 2], F32)

    gv_free = None
    ye_free = None
    gather_free = None
    gs_free = None
    ohg_consumed = None
    prev_tym = None

    s_wf1 = [sb(f"s_wf1{i}", [128, 1024], F32) for i in range(2)]
    s_wf2 = sb("s_wf2", [128, 1024], F32)
    ds_f1 = [p.dma_sem("ds_f1a"), p.dma_sem("ds_f1b")]
    ds_f2 = p.dma_sem("ds_f2b")
    ds_w3 = [p.dma_sem("ds_w3a"), p.dma_sem("ds_w3b")]
    stg1_free = [None, None]
    wcast1 = [[None] * 4 for _ in range(2)]
    wcast2 = [[None] * 4 for _ in range(2)]
    w_cons1 = [[None] * 4 for _ in range(2)]
    w_cons2 = [[None] * 4 for _ in range(2)]
    w_cons3 = [None, None]

    def wsrc(k, e, dffc, qtr):
        w = w1 if k == 1 else w2
        return w.ap()[e * NDC + dffc][:, qtr * 1024:(qtr + 1) * 1024]

    def issue_w1(e, dffc, qtr):
        buf = qtr % 2
        p.need("y", stg1_free[buf])
        p.dma("y", s_wf1[buf][:], wsrc(1, e, dffc, qtr), ds_f1[buf])

    def cast_w1(qtr, par):
        buf = qtr % 2
        p.need_dma("v", ds_f1[buf])
        p.need("v", w_cons1[par][qtr])
        tok = p.prod("v", lambda v, qtr=qtr, par=par, buf=buf: v.tensor_copy(
            s_w1c[par][:, qtr * 1024:(qtr + 1) * 1024], s_wf1[buf][:]))
        stg1_free[buf] = tok
        wcast1[par][qtr] = tok

    stg2_box = [None]

    def issue_w2(e, dffc, qtr):
        p.need("s", stg2_box[0])
        p.dma("s", s_wf2[:], wsrc(2, e, dffc, qtr), ds_f2)

    def cast_w2(qtr, par):
        p.need_dma("s", ds_f2)
        p.need("s", w_cons2[par][qtr])
        tok = p.prod("s", lambda s, qtr=qtr, par=par: s.activation(
            s_w2c[par][:, qtr * 1024:(qtr + 1) * 1024], s_wf2[:], AF.Copy))
        stg2_box[0] = tok
        wcast2[par][qtr] = tok

    def load_w3(e, dffc, par):
        p.need("g", w_cons3[par])
        p.dma("g", s_w3c[par][:], w3.ap()[e * NDC + dffc], ds_w3[par])

    load_w3(0, 0, 0)
    for _q in range(4):
        issue_w1(0, 0, _q)
        cast_w1(_q, 0)
    for _q in range(4):
        issue_w2(0, 0, _q)
        cast_w2(_q, 0)

    def build_onehots(e):
        toks = []
        for tt in range(NT):
            toks.append(p.prod("v", lambda v, tt=tt, e=e: v.tensor_scalar(
                s_ohg[:, tt * 256:(tt + 1) * 256], s_iota[:],
                s_smTf[:, tt * 16 + e: tt * 16 + e + 1], None, ALU.is_equal)))
        return toks

    p.need("v", tsmtf)
    ohg_toks = build_onehots(0)
    for e in range(E):
        p.need("t", *ohg_toks)
        p.need("t", x2n_done_tok)
        p.need("t", gv_free)
        p.need("t", ye_free)
        if e == 0:
            p.need("t", *ex_toks)
        gbanks8 = [pQ[0], pQ[1], pA[0], pA[1], pO[0], pO[1], pO[2], pO[3]]
        def mmg(t, e=e):
            for tt in range(NT):
                for dt in range(ND):
                    ins = t.matmul(gbanks8[dt][:, 0:256],
                                   s_x2n[:, tt * D + dt * 128: tt * D + dt * 128 + 128],
                                   s_ohg[:, tt * 256:(tt + 1) * 256],
                                   start=(tt == 0), stop=(tt == NT - 1))
            return ins
        tgm = p.prod("t", mmg)
        p.need("v", tgm)
        if prev_tym is not None:
            p.need("v", prev_tym)
        if cfg.debug:
            p.need_dma("v", ds_out)
        tge = []
        for b in range(8):
            tge.append(p.prod("v", lambda v, b=b: v.tensor_copy(
                s_xgT[:, b * 256:(b + 1) * 256], gbanks8[b][:, 0:256])))
        gather_free = tge[-1]

        p.need("t", *tge)
        def mmp(t, e=e):
            for tt in range(NT):
                ins = t.matmul(pO[0][0:16, 0:256], s_probsT[:, tt * 16:(tt + 1) * 16],
                               s_ohg[:, tt * 256:(tt + 1) * 256],
                               start=(tt == 0), stop=(tt == NT - 1))
            return ins
        tpm = p.prod("t", mmp)
        ohg_consumed = tpm
        p.need("v", tpm)
        twall = p.prod("v", lambda v: v.tensor_copy(s_gp[0:16, :], pO[0][0:16, 0:256]))
        p.need("g", twall)
        p.dma("g", s_wrow[:], s_gp.ap()[e:e + 1, :], ds_wr)
        p.need_dma("v", ds_wr)
        twrf = p.prod("v", lambda v: v.tensor_copy(s_wrf[:], s_wrow[:]))
        p.need("t", twrf)
        ttr0 = p.prod("t", lambda t: t.transpose(pO[1][:, 0:1], s_wrf[0:1, 0:128],
                                                 s_identf[0:1, 0:1]))
        ttr1 = p.prod("t", lambda t: t.transpose(pO[1][:, 1:2], s_wrf[0:1, 128:256],
                                                 s_identf[0:1, 0:1]))
        p.need("v", ttr0, ttr1)
        twcol = p.prod("v", lambda v: v.tensor_copy(s_wcol[:], pO[1][:, 0:2]))
        if e + 1 < E:
            p.need("v", tpm)
            ohg_toks = build_onehots(e + 1)

        hb_used = None
        for dffc in range(NDC):
            par = (e * NDC + dffc) % 2
            npar = (e * NDC + dffc + 1) % 2
            nxt = ((e, dffc + 1) if dffc + 1 < NDC
                   else ((e + 1, 0) if e + 1 < E else None))
            if nxt is not None:
                ne, nd = nxt
                load_w3(ne, nd, npar)
                issue_w1(ne, nd, 0)
                issue_w1(ne, nd, 1)
                cast_w1(0, npar)
                issue_w1(ne, nd, 2)
                cast_w1(1, npar)
                issue_w1(ne, nd, 3)
                cast_w1(2, npar)
                cast_w1(3, npar)
                issue_w2(ne, nd, 0)
            p.need("t", gather_free)
            fb = [pQ[0], pQ[1], pA[0], pA[1]]
            def mk_mm1(f, wc, par=par):
                def mm(t, f=f, wc=wc, par=par):
                    for j in range(ND):
                        g_ins = t.matmul(fb[f][:, 0:256],
                                         wc[par][:, f * 1024 + j * 128: f * 1024 + j * 128 + 128],
                                         s_xgT[:, j * 256:(j + 1) * 256],
                                         start=(j == 0), stop=(j == ND - 1))
                    return g_ins
                return mm
            tgg_f = []
            for f in range(4):
                p.need("t", wcast1[par][f])
                tgg_f.append(p.prod("t", mk_mm1(f, s_w1c)))
                w_cons1[par][f] = tgg_f[-1]
            if gs_free is not None:
                p.need("s", gs_free)
            sil = []
            for f in range(4):
                p.need("s", tgg_f[f])
                sil.append(p.prod("s", lambda s, f=f: s.activation(
                    s_gs[:, f * 256:(f + 1) * 256], fb[f][:, 0:256], AF.Silu)))
            if nxt is not None:
                for qq in range(4):
                    cast_w2(qq, npar)
                    if qq < 3:
                        issue_w2(nxt[0], nxt[1], qq + 1)
            tgv_f = []
            for f in range(4):
                p.need("t", sil[f])
                p.need("t", wcast2[par][f])
                tgv_f.append(p.prod("t", mk_mm1(f, s_w2c)))
                w_cons2[par][f] = tgv_f[-1]
            if hb_used is not None:
                p.need("v", hb_used)
            muls = []
            for f in range(4):
                p.need("v", tgv_f[f])
                p.need("v", sil[f])
                muls.append(p.prod("v", lambda v, f=f: v.tensor_mul(
                    s_hb[:, f * 256:(f + 1) * 256], fb[f][:, 0:256],
                    s_gs[:, f * 256:(f + 1) * 256])))
            thb = muls[-1]
            gs_free = thb
            gv_free = thb
            p.need_dma("t", ds_w3[par])
            if dffc == 0:
                p.need("t", tpm, ttr0, ttr1, twall, twcol)
            tym = None
            for f in range(4):
                p.need("t", muls[f])
                def mmyf(t, dffc=dffc, par=par, f=f):
                    for st in range(2):
                        for dc in range(2):
                            ins = t.matmul(
                                pO[2 * st + dc][:, 0:512],
                                s_hb[:, f * 256 + st * 128: f * 256 + st * 128 + 128],
                                s_w3c[par][:, f * 1024 + dc * 512: f * 1024 + (dc + 1) * 512],
                                start=(dffc == 0 and f == 0), stop=(dffc == NDC - 1 and f == 3))
                    return ins
                tym = p.prod("t", mmyf)
            w_cons3[par] = tym
            hb_used = tym
        p.need("v", hb_used, twcol)
        yev = []
        for st in range(2):
            for dc in range(2):
                yev.append(p.prod("v", lambda v, st=st, dc=dc, e=e: v.tensor_scalar(
                    s_ye[:, e * 2048 + st * 1024 + dc * 512: e * 2048 + st * 1024 + (dc + 1) * 512],
                    pO[2 * st + dc][:, 0:512], s_wcol[:, st:st + 1], None, ALU.mult)))
        ye_free = yev[-1]
        p.need("t", ye_free)
        prev_tym = hb_used

        if cfg.debug and e == 0:
            p.need("y", *yev)
            p.dma("y", dbg["dbg_ye"][:], s_ye[:, 0:2048], ds_out)
            p.need("y", tge[-1])
            p.dma("y", dbg["dbg_xgt"][:], s_xgT[:], ds_out)
            p.need("y", twcol)
            p.dma("y", dbg["dbg_wcol"][:], s_wcol[:], ds_out)

    p.join_all()
    nc.sbuf_base = markF
    s_ohgs = [sb(f"s_ohgs{i}", [128, 2 * 256], BF16) for i in range(2)]
    s_ohs = [sb(f"s_ohs{i}", [128, 4 * 128], BF16) for i in range(2)]
    s_ffn = sb("s_ffn", [128, D], BF16)
    s_outt = sb("s_outt", [128, D], F32)

    NTC = NT // 2
    nc.psum_base = 0
    scat_banks = [nc.alloc_psum_tensor(f"pS{i}", [128, 512], F32) for i in range(4)]
    pT = nc.alloc_psum_tensor("pT", [128, 512], BF16)
    p.need_dma("y", ds_x2)
    ohs_used = [None, None]
    ohgs_guard = [None, None]
    pT_free = None
    bank_free = {i: None for i in range(4)}
    for tc in range(NTC):
        mm_last = {i: None for i in range(4)}
        for e in range(E):
            par = e % 2
            p.need("v", tsmtf)
            p.need("v", ohgs_guard[par])
            og = []
            for ts_ in range(2):
                tt = tc * 2 + ts_
                og.append(p.prod("v", lambda v, ts_=ts_, tt=tt, e=e, par=par: v.tensor_scalar(
                    s_ohgs[par][:, ts_ * 256:(ts_ + 1) * 256], s_iota[:],
                    s_smTf[:, tt * 16 + e: tt * 16 + e + 1], None, ALU.is_equal)))
            p.need("t", *og)
            p.need("t", pT_free)
            ttr = []
            for ts_ in range(2):
                for st in range(2):
                    ttr.append(p.prod("t", lambda t, ts_=ts_, st=st, par=par: t.transpose(
                        pT[:, (st * 2 + ts_) * 128:(st * 2 + ts_ + 1) * 128],
                        s_ohgs[par][:, ts_ * 256 + st * 128: ts_ * 256 + st * 128 + 128],
                        s_identb[:])))
            ohgs_guard[par] = ttr[-1]
            p.need("v", *ttr)
            p.need("v", ohs_used[par])
            tev = []
            for k4 in range(4):
                tev.append(p.prod("v", lambda v, k4=k4, par=par: v.tensor_copy(
                    s_ohs[par][:, k4 * 128:(k4 + 1) * 128], pT[:, k4 * 128:(k4 + 1) * 128])))
            pT_free = tev[-1]
            p.need("t", *tev)
            if e == 0:
                for bi in range(4):
                    p.need("t", bank_free[bi])
            for ts_ in range(2):
                for dc in range(2):
                    bi = ts_ * 2 + dc
                    def mms(t, e=e, ts_=ts_, dc=dc, par=par, bi=bi):
                        for st in range(2):
                            ins = t.matmul(
                                scat_banks[bi][:],
                                s_ohs[par][:, (st * 2 + ts_) * 128:(st * 2 + ts_ + 1) * 128],
                                s_ye[:, e * 2048 + st * 1024 + dc * 512:
                                     e * 2048 + st * 1024 + (dc + 1) * 512],
                                start=(e == 0 and st == 0), stop=(e == E - 1 and st == 1))
                        return ins
                    mm_last[bi] = p.prod("t", mms)
            ohs_used[par] = mm_last[3]
        for ts_ in range(2):
            tt = tc * 2 + ts_
            par = tt % 2
            p.dma("y", s_xstage[par][:], sc_x2.ap()[tt * 128:(tt + 1) * 128, :], ds_x2r[par])
            ev = []
            for dc in range(2):
                bi = ts_ * 2 + dc
                p.need("v", mm_last[bi])
                ev.append(p.prod("v", lambda v, dc=dc, bi=bi: v.tensor_copy(
                    s_ffn[:, dc * 512:(dc + 1) * 512], scat_banks[bi][:])))
                bank_free[bi] = ev[-1]
            p.need("v", *ev)
            tst = ln_stats(s_ffn, s_mv[par][:])
            p.need("v", tst)
            trs = rstd_chain(s_mv[par], s_col[par][:])
            p.need("v", trs)
            p.need_dma("v", ds_x2r[par])
            tk = p.prod("v", lambda v, par=par: v.tensor_scalar(
                s_f32t[:], s_ffn[:], s_mv[par][:, 0:1], s_col[par][:],
                ALU.subtract, ALU.mult))
            p.need("v", tk)
            p.need_dma("v", ds_out)
            tout = p.prod("v", lambda v, par=par: v.tensor_add(s_outt[:], s_f32t[:],
                                                               s_xstage[par][:]))
            p.need("y", tout)
            p.dma("y", out_t.ap()[tt * 128:(tt + 1) * 128, :], s_outt[:], ds_out)

    p.join_all()
    with nc.Block() as block:
        p.realize(block)
    nc.finalize()
    return nc, din, dbg


_CACHE = {}


def _reorder_w(w1, w2, w3):
    E, D, DFF = w1.shape
    NDC = DFF // 512
    ND = D // 128
    w1r = np.ascontiguousarray(
        w1.reshape(E, ND, 128, NDC, 4, 128).transpose(0, 3, 2, 4, 1, 5)
    ).reshape(E * NDC, 128, ND * 512)
    w2r = np.ascontiguousarray(
        w2.reshape(E, ND, 128, NDC, 4, 128).transpose(0, 3, 2, 4, 1, 5)
    ).reshape(E * NDC, 128, ND * 512)
    w3r = np.ascontiguousarray(
        w3.reshape(E, NDC, 4, 128, D).transpose(0, 1, 3, 2, 4)
    ).reshape(E * NDC, 128, 4 * D)
    return w1r, w2r, w3r


def kernel(L0_attn_X, L0_attn_Wq, L0_attn_Wk, L0_attn_Wv, L0_attn_Wo,
           L0_attn_lambda_q1, L0_attn_lambda_k1, L0_attn_lambda_q2, L0_attn_lambda_k2,
           L0_attn_rms_scale, L0_ffn_router_w1, L0_ffn_router_w2,
           L0_ffn_w1, L0_ffn_w2, L0_ffn_w3):
    cfg = Cfg()
    if "nc" not in _CACHE:
        _CACHE["nc"] = build_program(cfg)[0]
    nc = _CACHE["nc"]
    consts = host_constants()
    X = np.ascontiguousarray(np.asarray(L0_attn_X, dtype=np.float32))
    base = {
        "wq": np.ascontiguousarray(np.asarray(L0_attn_Wq, np.float32)),
        "wk": np.ascontiguousarray(np.asarray(L0_attn_Wk, np.float32)),
        "wv": np.ascontiguousarray(np.asarray(L0_attn_Wv, np.float32)),
        "wo": np.ascontiguousarray(np.asarray(L0_attn_Wo, np.float32)),
        "lq1": np.ascontiguousarray(np.asarray(L0_attn_lambda_q1, np.float32)),
        "lk1": np.ascontiguousarray(np.asarray(L0_attn_lambda_k1, np.float32)),
        "lq2": np.ascontiguousarray(np.asarray(L0_attn_lambda_q2, np.float32)),
        "lk2": np.ascontiguousarray(np.asarray(L0_attn_lambda_k2, np.float32)),
        "rmss": np.ascontiguousarray(np.asarray(L0_attn_rms_scale, np.float32).reshape(1, 256)),
        "rw1": np.ascontiguousarray(np.asarray(L0_ffn_router_w1, np.float32)),
        "rw2": np.ascontiguousarray(np.asarray(L0_ffn_router_w2, np.float32)),
    }
    base["w1"], base["w2"], base["w3"] = _reorder_w(
        np.asarray(L0_ffn_w1, np.float32), np.asarray(L0_ffn_w2, np.float32),
        np.asarray(L0_ffn_w3, np.float32))
    base.update(consts)
    in_maps = []
    for c in range(8):
        m = dict(base)
        m["x"] = np.ascontiguousarray(X[c])
        in_maps.append(m)
    res = run_bass_kernel_spmd(nc, in_maps, core_ids=list(range(8)))
    out = np.stack([np.asarray(r["out"], np.float32) for r in res.results], axis=0)
    return out


# version 19
# speedup vs baseline: 1.1082x; 1.0938x over previous
import math
import numpy as np
import ml_dtypes
import concourse.bass as bass
import concourse.mybir as mybir
from concourse.bass_utils import run_bass_kernel_spmd

F32 = mybir.dt.float32
BF16 = mybir.dt.bfloat16
AF = mybir.ActivationFunctionType
ALU = mybir.AluOpType
AX = mybir.AxisListType
bfdt = ml_dtypes.bfloat16

LAM_INIT = 0.8 - 0.6 * math.exp(-0.3)
SCALING = 1.0 / math.sqrt(128.0)
EPS = 1e-5


class Cfg:
    def __init__(self, tok=2048, heads=8, experts=16, dff=2048, debug=False, expose=False):
        self.expose = expose
        self.tok = tok; self.heads = heads; self.experts = experts
        self.dff = dff; self.d = 1024
        self.cap = tok * 2 // experts
        assert self.cap == 256 and experts == 16 and dff % 512 == 0
        self.nt = tok // 128
        self.nd = self.d // 128
        self.nq = tok // 512
        self.ndffc = dff // 512
        self.debug = debug


class DmaSem:
    def __init__(self, nc, name):
        self.sem = nc.alloc_semaphore(name)
        self.issued = 0


class Plan:
    ENGS = ("t", "v", "s", "g", "y")

    def __init__(self, nc):
        self.nc = nc
        self.ops = {k: [] for k in self.ENGS}
        self.sem = {k: nc.alloc_semaphore(f"prog_{k}") for k in self.ENGS}
        self.cnt = {k: 0 for k in self.ENGS}
        self.waited = {k: {j: 0 for j in self.ENGS} for k in self.ENGS}
        self.dma_waited = {k: {} for k in self.ENGS}
        self.dsems = []

    def dma_sem(self, name):
        ds = DmaSem(self.nc, name)
        self.dsems.append(ds)
        return ds

    def emit(self, e, fn):
        self.ops[e].append(fn)

    def prod(self, e, fn):
        self.cnt[e] += 1
        lvl = self.cnt[e]
        s = self.sem[e]
        self.ops[e].append(lambda eng, fn=fn, s=s: fn(eng).then_inc(s, 1))
        return (e, lvl)

    def need(self, e, *tokens):
        for tok in tokens:
            if tok is None:
                continue
            pe, lvl = tok
            if pe == e:
                if self.waited[e][e] < lvl:
                    self.ops[e].append(lambda eng: eng.drain())
                    self.waited[e][e] = self.cnt[e]
            else:
                if self.waited[e][pe] < lvl:
                    s = self.sem[pe]
                    self.ops[e].append(lambda eng, s=s, L=lvl: eng.wait_ge(s, L))
                    self.waited[e][pe] = lvl

    def dma(self, q, out_ap, in_ap, ds, **kw):
        ds.issued += 16
        self.ops[q].append(
            lambda eng, o=out_ap, i=in_ap, s=ds.sem, kw=kw: eng.dma_start(o, i, **kw).then_inc(s, 16))

    def need_dma(self, e, ds):
        lvl = ds.issued
        if lvl and self.dma_waited[e].get(id(ds), 0) < lvl:
            self.ops[e].append(lambda eng, s=ds.sem, L=lvl: eng.wait_ge(s, L))
            self.dma_waited[e][id(ds)] = lvl

    def token(self, e):
        return (e, self.cnt[e])

    def join_all(self):
        for e in self.ENGS:
            for pe in self.ENGS:
                if pe != e:
                    self.need(e, (pe, self.cnt[pe]))
            for ds in self.dsems:
                self.need_dma(e, ds)

    def realize(self, block):
        block.tensor(lambda eng: [f(eng) for f in self.ops["t"]])
        block.vector(lambda eng: [f(eng) for f in self.ops["v"]])
        block.scalar(lambda eng: [f(eng) for f in self.ops["s"]])
        block.gpsimd(lambda eng: [f(eng) for f in self.ops["g"]])
        block.sync(lambda eng: [f(eng) for f in self.ops["y"]])


def host_constants():
    return {
        "c_identf": np.eye(128, dtype=np.float32),
        "c_iota_bc": np.broadcast_to(np.arange(256, dtype=np.float32), (128, 256)).astype(bfdt).copy(),
        "c_iotacol0": np.arange(128, dtype=np.float32).reshape(128, 1),
        "c_iotacol1": (np.arange(128, dtype=np.float32) + 128.0).reshape(128, 1),
        "c_onesrow_b": np.ones((1, 128), dtype=bfdt),
        "c_onesrow_f": np.ones((1, 128), dtype=np.float32),
        "c_onescol_b": np.ones((128, 1), dtype=bfdt),
        "c_identb": np.eye(128, dtype=np.float32).astype(bfdt),
        "c_bar": np.zeros((1, 8), dtype=np.float32),
    }


def build_program(cfg):
    nc = bass.Bass("TRN2", target_bir_lowering=False)
    T, H, E, DFF, D = cfg.tok, cfg.heads, cfg.experts, cfg.dff, cfg.d
    NT, ND, NQ, NDC = cfg.nt, cfg.nd, cfg.nq, cfg.ndffc
    EH = 256

    din = {}
    def dram_in(name, shape, dt=F32):
        din[name] = nc.dram_tensor(name, list(shape), dt, kind="ExternalInput")
        return din[name]

    x_in = dram_in("x", (T, D))
    wq = dram_in("wq", (H, 128, 2 * 1024)); wk = dram_in("wk", (H, 128, 2 * 1024))
    wv = dram_in("wv", (H, 128, 2 * 1024)); wo = dram_in("wo", (H, 128, 2 * 1024))
    lq1 = dram_in("lq1", (H, 128)); lk1 = dram_in("lk1", (H, 128))
    lq2 = dram_in("lq2", (H, 128)); lk2 = dram_in("lk2", (H, 128))
    rmss = dram_in("rmss", (1, EH))
    rw1 = dram_in("rw1", (D, 128)); rw2 = dram_in("rw2", (128, E))
    w1 = dram_in("w1", (E * NDC, 128, ND * 512)); w2 = dram_in("w2", (E * NDC, 128, ND * 512))
    w3 = dram_in("w3", (E * NDC, 128, 4 * 1024))
    identf = dram_in("c_identf", (128, 128))
    iota_bc = dram_in("c_iota_bc", (128, 256), BF16)
    iotacol0 = dram_in("c_iotacol0", (128, 1))
    iotacol1 = dram_in("c_iotacol1", (128, 1))
    onesrow_b = dram_in("c_onesrow_b", (1, 128), BF16)
    onesrow_f = dram_in("c_onesrow_f", (1, 128))
    onescol_b = dram_in("c_onescol_b", (128, 1), BF16)
    identb = dram_in("c_identb", (128, 128), BF16)
    dram_in("c_bar", (1, 8), F32)

    out_t = nc.dram_tensor("out", [T, D], F32, kind="ExternalOutput")
    dbgkind = "ExternalOutput" if (cfg.debug or cfg.expose) else "Internal"
    sc_ot = nc.dram_tensor("sc_ot", [2 * H * 128, T], BF16, kind=dbgkind)
    sc_x2 = nc.dram_tensor("sc_x2", [T, D], F32, kind=dbgkind)
    sc_wot = nc.dram_tensor("sc_wot", [2 * H * 128, 1024], BF16, kind="Internal")
    dbg = {}
    if cfg.debug:
        for nm, shape, dt in [
            ("dbg_xnt", (128, ND * T), BF16),
            ("dbg_probs", (16, T), F32),
            ("dbg_sm", (16, T), F32),
            ("dbg_attnout", (128, NT * D), BF16),
            ("dbg_ye", (128, 2 * 1024), BF16),
            ("dbg_xgt", (128, 2 * 1024), BF16),
            ("dbg_wcol", (128, 2), F32),
        ]:
            dbg[nm] = nc.dram_tensor(nm, list(shape), dt, kind="ExternalOutput")

    sb = nc.alloc_sbuf_tensor
    s_identf = sb("s_identf", [128, 128], F32)
    s_iota = sb("s_iota", [128, 256], BF16)
    s_ic0 = sb("s_ic0", [128, 1], F32)
    s_ic1 = sb("s_ic1", [128, 1], F32)
    s_or_b = sb("s_or_b", [1, 128], BF16)
    s_or_f = sb("s_or_f", [1, 128], F32)
    s_oc_b = sb("s_oc_b", [128, 1], BF16)
    s_identb = sb("s_identb", [128, 128], BF16)
    s_rsbc = sb("s_rsbc", [128, 256], F32)
    s_lam = sb("s_lam", [H, 1], F32)
    s_lamt = sb("s_lamt", [1, H], F32)
    s_lambc = sb("s_lambc", [128, H], F32)
    s_rw1b = sb("s_rw1b", [128, ND * 128], BF16)
    s_rw2b = sb("s_rw2b", [128, E], BF16)
    s_scal = sb("s_scal", [1, 16], F32)
    s_col = [sb(f"s_col{i}", [128, 1], F32) for i in range(8)]
    s_bn = sb("s_bn", [128, 12], F32)
    s_mv = [sb(f"s_mv{i}", [128, 2], F32) for i in range(2)]
    s_xstage = [sb(f"s_xstage{i}", [128, D], F32) for i in range(2)]
    s_xn = [sb(f"s_xn{i}", [128, D], BF16) for i in range(2)]
    s_f32t = sb("s_f32t", [128, D], F32)
    s_x2tile = sb("s_x2tile", [128, D], F32)
    s_lq = sb("s_lq", [H, 128], F32); s_lk = sb("s_lk", [H, 128], F32)
    s_lq2 = sb("s_lq2", [H, 128], F32); s_lk2 = sb("s_lk2", [H, 128], F32)
    s_rms = sb("s_rms", [1, 256], F32)
    s_x2n = sb("s_x2n", [128, NT * D], BF16)
    s_smT = sb("s_smT", [128, NT * 16], BF16)
    s_smTf = sb("s_smTf", [128, NT * 16], F32)
    s_probsT = sb("s_probsT", [128, NT * 16], BF16)

    p = Plan(nc)
    ds_c = p.dma_sem("ds_c")
    ds_rw = p.dma_sem("ds_rw")
    ds_x = [p.dma_sem(f"ds_x{i}") for i in range(2)]
    ds_xt = p.dma_sem("ds_xt")
    ds_w = [p.dma_sem(f"ds_w{i}") for i in range(2)]
    ds_ott = p.dma_sem("ds_ott")
    ds_ots = p.dma_sem("ds_ots")
    ds_wos = p.dma_sem("ds_wos")
    ds_wor = p.dma_sem("ds_wor")
    ds_otr = [p.dma_sem(f"ds_otr{i}") for i in range(2)]
    ds_x2 = p.dma_sem("ds_x2")
    ds_x2t = [p.dma_sem(f"ds_x2t{i}") for i in range(2)]
    ds_xr = [p.dma_sem(f"ds_xr{i}") for i in range(2)]
    ds_smt = p.dma_sem("ds_smt")
    ds_wr = p.dma_sem("ds_wr")
    ds_oht = [p.dma_sem(f"ds_oht{i}") for i in range(2)]
    ds_x2r = [p.dma_sem(f"ds_x2r{i}") for i in range(2)]
    ds_out = p.dma_sem("ds_out")

    nc.psum_base = 0
    pQ = [nc.alloc_psum_tensor(f"pQ{i}", [128, 512], F32) for i in range(2)]
    pA = [nc.alloc_psum_tensor(f"pA{i}", [128, 512], F32) for i in range(2)]
    pO = [nc.alloc_psum_tensor(f"pO{i}", [128, 512], F32) for i in range(4)]

    for dst, src in [(s_identf, identf), (s_iota, iota_bc), (s_ic0, iotacol0),
                     (s_ic1, iotacol1), (s_or_b, onesrow_b), (s_or_f, onesrow_f),
                     (s_oc_b, onescol_b), (s_identb, identb), (s_lq, lq1), (s_lk, lk1), (s_lq2, lq2),
                     (s_lk2, lk2), (s_rms, rmss)]:
        p.dma("y", dst[:], src[:], ds_c)
    p.dma("g", s_rw1b.ap().rearrange("p (c f) -> p c f", c=ND),
          rw1.ap().rearrange("(c p) f -> p c f", p=128), ds_rw)
    p.dma("g", s_rw2b[:], rw2[:], ds_rw)

    p.need_dma("v", ds_c)
    s_ltmp = sb("s_ltmp", [H, 128], F32)
    s_l1 = sb("s_l1", [H, 1], F32); s_l2 = sb("s_l2", [H, 1], F32)
    tk = p.prod("v", lambda v: v.tensor_mul(s_ltmp[:], s_lq[:], s_lk[:]))
    p.need("v", tk)
    t1 = p.prod("v", lambda v: v.reduce_sum(s_l1[:], s_ltmp[:], AX.X))
    p.need("v", t1)
    tk = p.prod("v", lambda v: v.tensor_mul(s_ltmp[:], s_lq2[:], s_lk2[:]))
    p.need("v", tk)
    t2 = p.prod("v", lambda v: v.reduce_sum(s_l2[:], s_ltmp[:], AX.X))
    p.need("s", t1, t2)
    te1 = p.prod("s", lambda s: s.activation(s_l1[:], s_l1[:], AF.Exp))
    p.need("s", te1)
    te2 = p.prod("s", lambda s: s.activation(s_l2[:], s_l2[:], AF.Exp))
    p.need("v", te1, te2)
    tk = p.prod("v", lambda v: v.tensor_sub(s_lam[:], s_l1[:], s_l2[:]))
    p.need("v", tk)
    tlam = p.prod("v", lambda v: v.tensor_scalar_add(s_lam[:], s_lam[:], LAM_INIT))
    p.need("t", tlam)
    p.need_dma("t", ds_c)
    tk = p.prod("t", lambda t: t.transpose(pQ[0][0:1, 0:H], s_lam[:], s_identf[0:H, 0:H]))
    p.need("v", tk)
    tk = p.prod("v", lambda v: v.tensor_copy(s_lamt[:], pQ[0][0:1, 0:H]))
    p.need("t", tk)
    tk = p.prod("t", lambda t: t.matmul(pQ[1][:, 0:H], s_or_f[:], s_lamt[:], start=True, stop=True))
    p.need("v", tk)
    tlambc = p.prod("v", lambda v: v.tensor_copy(s_lambc[:], pQ[1][:, 0:H]))
    tk = p.prod("t", lambda t: t.matmul(pQ[0][:, 0:256], s_or_f[:], s_rms[:], start=True, stop=True))
    p.need("v", tk)
    trsbc = p.prod("v", lambda v: v.tensor_scalar_mul(s_rsbc[:], pQ[0][:, 0:256], 1.0 - LAM_INIT))
    pQ_tok = {0: trsbc, 1: tlambc}

    def ln_stats(tile_ap, mv_ap):
        t1 = p.prod("v", lambda v: v.bn_stats(s_bn[:, 0:6], tile_ap[:, 0:512]))
        t2 = p.prod("v", lambda v: v.bn_stats(s_bn[:, 6:12], tile_ap[:, 512:1024]))
        p.need("v", t1, t2)
        return p.prod("v", lambda v: v.bn_aggr(mv_ap, s_bn[:]))

    def rstd_chain(mv_ap, out_col):
        tv = p.prod("v", lambda v: v.tensor_scalar_add(out_col, mv_ap[:, 1:2], EPS))
        p.need("s", tv)
        t1 = p.prod("s", lambda s: s.activation(out_col, out_col, AF.Ln))
        p.need("s", t1)
        return p.prod("s", lambda s: s.activation(out_col, out_col, AF.Exp, scale=-0.5))

    mark0 = nc.sbuf_base
    s_xnT = sb("s_xnT", [128, ND * T], BF16)
    s_wsf = [sb(f"s_wsf{i}", [128, 1024], F32) for i in range(2)]
    s_wqT = [sb(f"s_wqT{i}", [128, ND * 256], BF16) for i in range(2)]
    s_wkT = [sb(f"s_wkT{i}", [128, ND * 256], BF16) for i in range(2)]
    s_wvT = [sb(f"s_wvT{i}", [128, ND * 256], BF16) for i in range(2)]
    s_woTh = sb("s_woTh", [128, 2 * 1024], BF16)
    s_qt = sb("s_qt", [128, 2 * T], BF16)
    s_kt = sb("s_kt", [128, 2 * T], BF16)
    s_v = sb("s_v", [128, NT * 257], BF16)
    s_sq = sb("s_sq", [128, 2 * T], BF16)
    s_et = sb("s_et", [128, NT * 512], BF16)
    s_o1 = sb("s_o1", [128, 4 * 257], F32)
    s_o2 = sb("s_o2", [128, 4 * 257], F32)
    s_oc = sb("s_oc", [128, 256], F32)
    s_t2 = sb("s_t2", [128, 256], F32)
    s_otil = [sb(f"s_otil{i}", [128, 256], BF16) for i in range(2)]
    s_otT = sb("s_otT", [128, 2 * T], BF16)
    s_bias = sb("s_bias", [128, 1], F32)

    xre = x_in.ap().rearrange("(n p) f -> n p f", p=128)
    txn_toks = []
    for i in range(NT):
        par = i % 2
        if i >= 2:
            p.need("y", txn_toks[i - 2])
        p.dma("y", s_xstage[par][:], xre[i], ds_x[par])
        p.need_dma("v", ds_x[par])
        if i >= 2:
            p.need_dma("v", ds_xt)
        tst = ln_stats(s_xstage[par], s_mv[par][:])
        p.need("v", tst)
        trs = rstd_chain(s_mv[par], s_col[par][:])
        p.need("v", trs)
        txn = p.prod("v", lambda v, par=par: v.tensor_scalar(
            s_xn[par][:], s_xstage[par][:], s_mv[par][:, 0:1], s_col[par][:],
            ALU.subtract, ALU.mult))
        txn_toks.append(txn)
        p.need("y", txn)
        for j in range(ND):
            p.dma("y", s_xnT[:, j * T + i * 128: j * T + (i + 1) * 128],
                  s_xn[par][:, j * 128:(j + 1) * 128], ds_xt, transpose=True)
    if cfg.debug:
        p.need_dma("y", ds_xt)
        p.dma("y", dbg["dbg_xnt"][:], s_xnT[:], ds_out)

    wslice_done = {}

    wcv_tok = [None, None]
    hw_idx = [0]
    conv_done = {0: [], 1: []}

    def head_weight_load(h, par, prev_qkv_t):
        toks = []
        p.need("v", prev_qkv_t)
        for wi, (wp, dstbuf) in enumerate([(wq, s_wqT[par]), (wk, s_wkT[par]),
                                           (wv, s_wvT[par]), (wo, s_woTh)]):
            for c in range(2):
                sp = hw_idx[0] % 2
                hw_idx[0] += 1
                p.need("s", wcv_tok[sp])
                p.dma("s", s_wsf[sp][:], wp.ap()[h][:, c * 1024:(c + 1) * 1024],
                      ds_w[sp])
                p.need_dma("v", ds_w[sp])
                if wi == 3 and c == 0:
                    p.need_dma("v", ds_wos)
                tcv = p.prod("v", lambda v, sp=sp, dstbuf=dstbuf, c=c: v.tensor_copy(
                    dstbuf[:, c * 1024:(c + 1) * 1024], s_wsf[sp][:]))
                wcv_tok[sp] = tcv
                toks.append(tcv)
        conv_done[par] = toks
        p.need("y", toks[-1])
        for ch in range(2):
            ec = 2 * h + ch
            p.dma("y", sc_wot.ap()[ec * 128:(ec + 1) * 128, :],
                  s_woTh[:, ch * 1024:(ch + 1) * 1024], ds_wos)

    head_weight_load(0, 0, None)
    p.need_dma("t", ds_xt)

    qkv_bank_init = {0: pQ_tok.get(0), 1: pQ_tok.get(1), 2: None, 3: None}
    qkv_t_done = {}
    prev_head_consumers = None
    for h in range(H):
        par = h % 2
        if h + 1 < H:
            head_weight_load(h + 1, (h + 1) % 2, qkv_t_done.get(h - 1))
        p.need("t", *conv_done[par])

        if prev_head_consumers is not None:
            p.need("v", *prev_head_consumers)
        qkvb = [pQ[0], pQ[1], pA[0], pA[1]]
        bank_tok = dict(qkv_bank_init)
        gi = 0
        for (wbuf, obuf) in [(s_wqT[par], s_qt), (s_wkT[par], s_kt)]:
            for et in range(2):
                for qc in range(NQ):
                    bi = gi % 4
                    gi += 1
                    p.need("t", bank_tok.get(bi))
                    def mmf(t, wbuf=wbuf, et=et, qc=qc, bank=qkvb[bi]):
                        for j in range(ND):
                            ins = t.matmul(bank[:],
                                           wbuf[:, j * 256 + et * 128: j * 256 + et * 128 + 128],
                                           s_xnT[:, j * T + qc * 512: j * T + (qc + 1) * 512],
                                           start=(j == 0), stop=(j == ND - 1))
                        return ins
                    tmm = p.prod("t", mmf)
                    p.need("v", tmm)
                    bank_tok[bi] = p.prod("v", lambda v, obuf=obuf, et=et, qc=qc, bi=bi:
                                          v.tensor_copy(obuf[:, et * T + qc * 512: et * T + (qc + 1) * 512],
                                                        qkvb[bi][:]))
        if h == 0:
            t_vones = p.prod("v", lambda v: v.memset(s_v[:], 1.0))
            p.need("t", t_vones)
        for tt in range(NT):
            bi = gi % 4
            gi += 1
            p.need("t", bank_tok.get(bi))
            def mmv(t, tt=tt, bank=qkvb[bi], par=par):
                for j in range(ND):
                    ins = t.matmul(bank[:, 0:256],
                                   s_xnT[:, j * T + tt * 128: j * T + tt * 128 + 128],
                                   s_wvT[par][:, j * 256:(j + 1) * 256],
                                   start=(j == 0), stop=(j == ND - 1))
                return ins
            tmm = p.prod("t", mmv)
            p.need("v", tmm)
            bank_tok[bi] = p.prod("v", lambda v, tt=tt, bi=bi:
                                  v.tensor_copy(s_v[:, tt * 257: tt * 257 + 256], qkvb[bi][:, 0:256]))
        qkv_done_v = p.token("v")

        p.need("s", qkv_done_v)
        sq_consumed = None
        pA0_tok = None
        last_red = None
        for bi_src, srcbuf in [(0, s_qt), (1, s_kt)]:
            if sq_consumed is not None:
                p.need("s", sq_consumed)
            tsq0 = p.prod("s", lambda s, srcbuf=srcbuf: s.activation(
                s_sq[:, 0:T], srcbuf[:, 0:T], AF.Square))
            tsq1 = p.prod("s", lambda s, srcbuf=srcbuf: s.activation(
                s_sq[:, T:2 * T], srcbuf[:, T:2 * T], AF.Square))
            p.need("t", tsq0, tsq1)
            red_toks = []
            last_mm = None
            for ch in range(NQ):
                if last_red is not None:
                    p.need("t", last_red)
                def mmn(t, ch=ch):
                    t.matmul(pO[0][0:1, 0:512], s_oc_b[:], s_sq[:, ch * 512:(ch + 1) * 512],
                             start=True, stop=False)
                    return t.matmul(pO[0][0:1, 0:512], s_oc_b[:],
                                    s_sq[:, T + ch * 512: T + (ch + 1) * 512],
                                    start=False, stop=True)
                last_mm = p.prod("t", mmn)
                p.need("v", last_mm)
                last_red = p.prod("v", lambda v, ch=ch: v.reduce_max(
                    s_scal[0:1, ch:ch + 1], pO[0][0:1, 0:512], AX.X))
                red_toks.append(last_red)
            p.need("v", *red_toks)
            tmx = p.prod("v", lambda v, bi_src=bi_src: v.reduce_max(
                s_scal[0:1, 8 + bi_src: 9 + bi_src], s_scal[0:1, 0:NQ], AX.X))
            p.need("v", tmx)
            sq_consumed = last_mm
            pA0_tok = red_toks[-1]
        tmul = p.prod("v", lambda v: v.tensor_mul(s_scal[0:1, 10:11], s_scal[0:1, 8:9],
                                                  s_scal[0:1, 9:10]))
        p.need("s", tmul)
        tln = p.prod("s", lambda s: s.activation(s_scal[0:1, 10:11], s_scal[0:1, 10:11], AF.Ln))
        p.need("s", tln)
        tex = p.prod("s", lambda s: s.activation(s_scal[0:1, 10:11], s_scal[0:1, 10:11],
                                                 AF.Exp, scale=0.5))
        p.need("v", tex)
        tbn = p.prod("v", lambda v: v.tensor_scalar_mul(s_scal[0:1, 10:11],
                                                        s_scal[0:1, 10:11], -SCALING))
        p.need("t", tbn, pA0_tok)
        tbb = p.prod("t", lambda t: t.matmul(pO[0][:, 0:1], s_or_f[:], s_scal[0:1, 10:11],
                                             start=True, stop=True))
        p.need("v", tbb)
        tbias = p.prod("v", lambda v: v.tensor_copy(s_bias[:], pO[0][:, 0:1]))
        pAb = [pQ[0], pQ[1], pA[0], pA[1]]
        pA_tok = dict(bank_tok)

        p.need("s", tbias)
        pO_free = {0: tbias, 1: None, 2: None, 3: None}
        et_free = None
        last_a_mm = None
        for qc in range(NQ):
            o1_evac = [None] * 4
            ev_last = [None] * 4
            for half in range(2):
                def mk_a(kt):
                    bi = kt % 4
                    p.need("t", pA_tok.get(bi))
                    def mma(t, half=half, kt=kt, qc=qc, bi=bi):
                        return t.matmul(pAb[bi][:],
                                        s_kt[:, half * T + kt * 128: half * T + (kt + 1) * 128],
                                        s_qt[:, half * T + qc * 512: half * T + (qc + 1) * 512],
                                        start=True, stop=True)
                    return p.prod("t", mma)
                tka = {}
                for kt in range(min(4, NT)):
                    tka[kt] = mk_a(kt)
                for kt in range(NT):
                    bi = kt % 4
                    last_a_mm = tka[kt]
                    p.need("s", tka[kt])
                    if kt == 0 and et_free is not None:
                        p.need("s", *et_free)
                    texp = p.prod("s", lambda s, kt=kt, bi=bi: s.activation(
                        s_et[:, kt * 512:(kt + 1) * 512], pAb[bi][:], AF.Exp,
                        bias=s_bias[:], scale=SCALING))
                    pA_tok[bi] = texp
                    p.need("t", texp)
                    for qt in range(4):
                        if kt == 0:
                            p.need("t", pO_free[qt])
                        def mme(t, kt=kt, qt=qt):
                            return t.matmul(pO[qt][:, 0:257],
                                            s_et[:, kt * 512 + qt * 128: kt * 512 + qt * 128 + 128],
                                            s_v[:, kt * 257:(kt + 1) * 257],
                                            start=(kt == 0), stop=(kt == NT - 1))
                        ev_last[qt] = p.prod("t", mme)
                    if kt + 4 < NT:
                        def mma2(t, half=half, kt2=kt + 4, qc=qc, bi=(kt + 4) % 4):
                            return t.matmul(pAb[bi][:],
                                            s_kt[:, half * T + kt2 * 128: half * T + (kt2 + 1) * 128],
                                            s_qt[:, half * T + qc * 512: half * T + (qc + 1) * 512],
                                            start=True, stop=True)
                        tka[kt + 4] = p.prod("t", mma2)
                et_free = list(ev_last)
                dstb = s_o1 if half == 0 else s_o2
                for qt in range(4):
                    p.need("v", ev_last[qt])
                    tkev = p.prod("v", lambda v, qt=qt, dstb=dstb: v.tensor_copy(
                        dstb[:, qt * 257:(qt + 1) * 257], pO[qt][:, 0:257]))
                    pO_free[qt] = tkev
                    if half == 0:
                        o1_evac[qt] = tkev
                    else:
                        ev_last[qt] = tkev
            for qt in range(4):
                p.need("v", ev_last[qt], o1_evac[qt], tlambc, trsbc)
                tqt = qc * 4 + qt
                c0, c1, c2, c5 = s_col[2][:], s_col[3][:], s_col[4][:], s_col[5][:]
                tk = p.prod("v", lambda v, qt=qt, c0=c0: v.reciprocal(
                    c0, s_o1[:, qt * 257 + 256: qt * 257 + 257]))
                tk2 = p.prod("v", lambda v, qt=qt, c1=c1: v.reciprocal(
                    c1, s_o2[:, qt * 257 + 256: qt * 257 + 257]))
                p.need("v", tk, tk2)
                tk = p.prod("v", lambda v, h=h, c1=c1, c2=c2: v.tensor_mul(
                    c2, c1, s_lambc[:, h:h + 1]))
                p.need("v", tk)
                tk = p.prod("v", lambda v, qt=qt, c2=c2: v.tensor_scalar(
                    s_t2[:], s_o2[:, qt * 257: qt * 257 + 256], c2, None, ALU.mult))
                p.need("v", tk)
                tk = p.prod("v", lambda v, qt=qt, c0=c0: v.scalar_tensor_tensor(
                    s_oc[:], s_o1[:, qt * 257: qt * 257 + 256], c0, s_t2[:],
                    ALU.mult, ALU.subtract))
                p.need("v", tk)
                tssq = p.prod("v", lambda v, c5=c5: v.scalar_tensor_tensor(
                    s_t2[:], s_oc[:], 1.0, s_oc[:], ALU.mult, ALU.mult, accum_out=c5))
                p.need("v", tssq)
                tk = p.prod("v", lambda v, c5=c5: v.tensor_scalar(
                    c5, c5, 1.0 / 256.0, EPS, ALU.mult, ALU.add))
                p.need("s", tk)
                tk = p.prod("s", lambda s, c5=c5: s.activation(c5, c5, AF.Ln))
                p.need("s", tk)
                trst = p.prod("s", lambda s, c5=c5: s.activation(c5, c5, AF.Exp, scale=-0.5))
                p.need("v", trst)
                p.need_dma("v", ds_ott)
                tk = p.prod("v", lambda v, c5=c5: v.tensor_scalar(
                    s_t2[:], s_oc[:], c5, None, ALU.mult))
                p.need("v", tk)
                totil = p.prod("v", lambda v, qt=qt: v.tensor_mul(
                    s_otil[qt % 2][:], s_t2[:], s_rsbc[:]))
                p.need("y", totil)
                for c in range(2):
                    p.dma("y", s_otT[:, c * T + tqt * 128: c * T + (tqt + 1) * 128],
                          s_otil[qt % 2][:, c * 128:(c + 1) * 128], ds_ott, transpose=True)
        qkv_t_done[h] = last_a_mm
        qkv_bank_init = dict(pA_tok)
        pQ_tok = {0: pA_tok.get(0), 1: pA_tok.get(1)}
        prev_head_consumers = [last_a_mm, p.token("s")]
        p.need_dma("y", ds_ott)
        for c in range(2):
            ec = 2 * h + c
            p.dma("y", sc_ot.ap()[ec * 128:(ec + 1) * 128, :], s_otT[:, c * T:(c + 1) * T], ds_ots)
        p.need_dma("y", ds_ots)

    p.join_all()
    nc.sbuf_base = mark0
    s_woT = sb("s_woT", [128, 2 * H * 1024], BF16)
    s_otq = [sb(f"s_otq{i}", [128, 2 * H * 512], BF16) for i in range(2)]
    s_ao4 = sb("s_ao4", [128, 4 * D], BF16)
    s_x4 = sb("s_x4", [128, 4 * D], F32)
    s_x24 = sb("s_x24", [128, 4 * D], F32)
    s_bn4 = sb("s_bn4", [128, 48], F32)
    s_mv4 = sb("s_mv4", [128, 8], F32)

    for ec in range(2 * H):
        p.dma("y", s_woT[:, ec * 1024:(ec + 1) * 1024], sc_wot.ap()[ec * 128:(ec + 1) * 128, :],
              ds_wor)
    p.need_dma("t", ds_wor)

    otq_used = [None, None]
    ao_used = None
    x4_used = None
    x2n_done_tok = None
    wo_bank_tok = {i: None for i in range(4)}
    wob = [pQ[0], pQ[1], pA[0], pA[1]]
    for qc in range(NQ):
        par = qc % 2
        p.need("y", otq_used[par])
        for ec in range(2 * H):
            p.dma("y", s_otq[par][:, ec * 512:(ec + 1) * 512],
                  sc_ot.ap()[ec * 128:(ec + 1) * 128, qc * 512:(qc + 1) * 512], ds_otr[par])
        p.need("y", x4_used)
        for i in range(4):
            p.dma("y", s_x4[:, i * D:(i + 1) * D], xre[qc * 4 + i], ds_xr[0])
        p.need_dma("t", ds_otr[par])
        evs = []
        for qt in range(4):
            for dc in range(2):
                bi = (qt * 2 + dc) % 4
                p.need("t", wo_bank_tok.get(bi))
                def mmw(t, par=par, qt=qt, dc=dc, bank=wob[bi]):
                    for ec in range(2 * H):
                        ins = t.matmul(bank[:],
                                       s_otq[par][:, ec * 512 + qt * 128: ec * 512 + qt * 128 + 128],
                                       s_woT[:, ec * 1024 + dc * 512: ec * 1024 + (dc + 1) * 512],
                                       start=(ec == 0), stop=(ec == 2 * H - 1))
                    return ins
                tmm = p.prod("t", mmw)
                p.need("v", tmm)
                p.need("v", ao_used)
                if cfg.debug:
                    p.need_dma("v", ds_out)
                wo_bank_tok[bi] = p.prod("v", lambda v, qt=qt, dc=dc, bi=bi: v.tensor_copy(
                    s_ao4[:, qt * D + dc * 512: qt * D + (dc + 1) * 512], wob[bi][:]))
                evs.append(wo_bank_tok[bi])
        otq_used[par] = p.token("t")
        p.need("v", *evs)
        st_toks = []
        for i in range(4):
            t1 = p.prod("v", lambda v, i=i: v.bn_stats(
                s_bn4[:, i * 12: i * 12 + 6], s_ao4[:, i * D: i * D + 512]))
            t2 = p.prod("v", lambda v, i=i: v.bn_stats(
                s_bn4[:, i * 12 + 6: i * 12 + 12], s_ao4[:, i * D + 512: i * D + 1024]))
            st_toks += [t1, t2]
        p.need("v", *st_toks)
        ag_toks = [p.prod("v", lambda v, i=i: v.bn_aggr(
            s_mv4[:, 2 * i: 2 * i + 2], s_bn4[:, i * 12: i * 12 + 12])) for i in range(4)]
        p.need("v", *ag_toks)
        ep_toks = [p.prod("v", lambda v, i=i: v.tensor_scalar_add(
            s_col[i][:], s_mv4[:, 2 * i + 1: 2 * i + 2], EPS)) for i in range(4)]
        p.need("s", *ep_toks)
        ln_toks = [p.prod("s", lambda sx, i=i: sx.activation(
            s_col[i][:], s_col[i][:], AF.Ln)) for i in range(4)]
        p.need("s", *ln_toks)
        rs_toks = [p.prod("s", lambda sx, i=i: sx.activation(
            s_col[i][:], s_col[i][:], AF.Exp, scale=-0.5)) for i in range(4)]
        p.need("v", *rs_toks)
        p.need_dma("v", ds_xr[0])
        p.need_dma("v", ds_x2)
        ap_toks = [p.prod("v", lambda v, i=i: v.tensor_scalar(
            s_x24[:, i * D:(i + 1) * D], s_ao4[:, i * D:(i + 1) * D],
            s_mv4[:, 2 * i: 2 * i + 1], s_col[i][:], ALU.subtract, ALU.mult))
            for i in range(4)]
        ao_used = ap_toks[-1]
        p.need("v", *ap_toks)
        ad_toks = [p.prod("v", lambda v, i=i: v.tensor_add(
            s_x24[:, i * D:(i + 1) * D], s_x24[:, i * D:(i + 1) * D],
            s_x4[:, i * D:(i + 1) * D])) for i in range(4)]
        x4_used = ad_toks[-1]
        p.need("y", *ad_toks)
        for i in range(4):
            p.dma("y", sc_x2.ap()[(qc * 4 + i) * 128:(qc * 4 + i + 1) * 128, :],
                  s_x24[:, i * D:(i + 1) * D], ds_x2)
        p.need("v", *ad_toks)
        st_toks = []
        for i in range(4):
            t1 = p.prod("v", lambda v, i=i: v.bn_stats(
                s_bn4[:, i * 12: i * 12 + 6], s_x24[:, i * D: i * D + 512]))
            t2 = p.prod("v", lambda v, i=i: v.bn_stats(
                s_bn4[:, i * 12 + 6: i * 12 + 12], s_x24[:, i * D + 512: i * D + 1024]))
            st_toks += [t1, t2]
        p.need("v", *st_toks)
        ag_toks = [p.prod("v", lambda v, i=i: v.bn_aggr(
            s_mv4[:, 2 * i: 2 * i + 2], s_bn4[:, i * 12: i * 12 + 12])) for i in range(4)]
        p.need("v", *ag_toks)
        ep_toks = [p.prod("v", lambda v, i=i: v.tensor_scalar_add(
            s_col[i][:], s_mv4[:, 2 * i + 1: 2 * i + 2], EPS)) for i in range(4)]
        p.need("s", *ep_toks)
        ln_toks = [p.prod("s", lambda sx, i=i: sx.activation(
            s_col[i][:], s_col[i][:], AF.Ln)) for i in range(4)]
        p.need("s", *ln_toks)
        rs_toks = [p.prod("s", lambda sx, i=i: sx.activation(
            s_col[i][:], s_col[i][:], AF.Exp, scale=-0.5)) for i in range(4)]
        p.need("v", *rs_toks)
        xn_toks = [p.prod("v", lambda v, i=i, qc=qc: v.tensor_scalar(
            s_x2n[:, (qc * 4 + i) * D:(qc * 4 + i + 1) * D], s_x24[:, i * D:(i + 1) * D],
            s_mv4[:, 2 * i: 2 * i + 1], s_col[i][:], ALU.subtract, ALU.mult))
            for i in range(4)]
        x2n_done_tok = xn_toks[-1]
        if cfg.debug:
            p.need("y", evs[-1])
            for i in range(4):
                p.dma("y", dbg["dbg_attnout"].ap()[:, (qc * 4 + i) * D:(qc * 4 + i + 1) * D],
                      s_ao4[:, i * D:(i + 1) * D], ds_out)

    p.join_all()
    nc.sbuf_base = mark0
    s_x2nT = [sb(f"s_x2nT{i}", [128, ND * 512], BF16) for i in range(2)]
    s_t1s = sb("s_t1s", [128, T], BF16)
    s_pu = sb("s_pu", [16, T], F32)
    s_probs = sb("s_probs", [16, T], F32)
    s_ew = sb("s_ew", [16, 16], F32)
    s_ones16 = sb("s_ones16", [16, T], F32)
    s_m16 = sb("s_m16", [16, T], F32)
    s_gt = sb("s_gt", [16, T], F32)
    s_eq = sb("s_eq", [16, T], F32)
    s_sel = sb("s_sel", [16, T], F32)
    s_sm = sb("s_sm", [16, T], F32)
    s_smb = sb("s_smb", [16, T], BF16)
    s_wpb = sb("s_wpb", [16, T], BF16)
    s_bis = sb("s_bis", [16, 12], F32)

    sil_toks = []
    x2nt_used = [None, None]
    for qc in range(NQ):
        par = qc % 2
        p.need("y", x2n_done_tok)
        p.need("y", x2nt_used[par])
        p.need_dma("y", ds_x2t[par])
        for tq in range(4):
            tt = qc * 4 + tq
            for j in range(ND):
                p.dma("y", s_x2nT[par][:, j * 512 + tq * 128: j * 512 + (tq + 1) * 128],
                      s_x2n[:, tt * D + j * 128: tt * D + (j + 1) * 128],
                      ds_x2t[par], transpose=True)
        p.need_dma("t", ds_x2t[par])
        bi = qc % 2
        p.need("t", pQ_tok.get(bi))
        def mmr(t, par=par, bank=pQ[bi]):
            for j in range(ND):
                ins = t.matmul(bank[:], s_rw1b[:, j * 128:(j + 1) * 128],
                               s_x2nT[par][:, j * 512:(j + 1) * 512],
                               start=(j == 0), stop=(j == ND - 1))
            return ins
        tmm = p.prod("t", mmr)
        x2nt_used[par] = tmm
        p.need("s", tmm)
        tsl = p.prod("s", lambda s, qc=qc, bi=bi: s.activation(
            s_t1s[:, qc * 512:(qc + 1) * 512], pQ[bi][:], AF.Silu))
        pQ_tok[bi] = tsl
        sil_toks.append(tsl)
    p.need_dma("t", ds_rw)
    lg_toks = []
    for qc in range(NQ):
        p.need("t", sil_toks[qc])
        def mml(t, qc=qc):
            return t.matmul(pO[qc % 4][0:16, 0:512], s_rw2b[:, 0:E],
                            s_t1s[:, qc * 512:(qc + 1) * 512], start=True, stop=True)
        lg_toks.append(p.prod("t", mml))
    p.need("v", *lg_toks)
    mx_toks = []
    for qc in range(NQ):
        mx_toks.append(p.prod("v", lambda v, qc=qc: v.reduce_max(
            s_ew[0:16, qc:qc + 1], pO[qc % 4][0:16, 0:512], AX.X)))
    p.need("v", *mx_toks)
    tk = p.prod("v", lambda v: v.reduce_max(s_ew[0:16, 4:5], s_ew[0:16, 0:NQ], AX.X))
    p.need("v", tk)
    tnm = p.prod("v", lambda v: v.tensor_scalar_mul(s_ew[0:16, 5:6], s_ew[0:16, 4:5], -1.0))
    p.need("s", tnm)
    ex_toks = []
    for qc in range(NQ):
        ex_toks.append(p.prod("s", lambda s, qc=qc: s.activation(
            s_pu[0:16, qc * 512:(qc + 1) * 512], pO[qc % 4][0:16, 0:512], AF.Exp,
            bias=s_ew[0:16, 5:6], accum_out=s_ew[0:16, 8 + qc: 9 + qc])))
    p.need("v", *ex_toks)
    tk = p.prod("v", lambda v: v.reduce_sum(s_ew[0:16, 6:7], s_ew[0:16, 8:8 + NQ], AX.X))
    p.need("v", tk)
    tk = p.prod("v", lambda v: v.reciprocal(s_ew[0:16, 7:8], s_ew[0:16, 6:7]))
    p.need("v", tk)
    tprobs = p.prod("v", lambda v: v.tensor_scalar(
        s_probs[0:16, :], s_pu[0:16, :], s_ew[0:16, 7:8], None, ALU.mult))
    if cfg.debug:
        p.need("y", tprobs)
        p.dma("y", dbg["dbg_probs"][:], s_probs[:], ds_out)

    t_ones16 = p.prod("v", lambda v: v.memset(s_ones16[:], 1.0))
    p.need("v", tprobs)
    tk = p.prod("v", lambda v: v.reduce_max(s_bis[0:16, 1:2], s_probs[0:16, :], AX.X))
    p.need("v", tk)
    tk = p.prod("v", lambda v: v.memset(s_bis[0:16, 0:1], 0.0))
    p.need("v", tk)
    tk = p.prod("v", lambda v: v.tensor_scalar_mul(s_bis[0:16, 2:3], s_bis[0:16, 1:2], 0.5))
    p.need("v", tk)
    for it in range(40):
        tk = p.prod("v", lambda v: v.tensor_scalar(
            s_m16[0:16, :], s_probs[0:16, :], s_bis[0:16, 2:3], None, ALU.is_gt, ALU.add,
            accum_out=s_bis[0:16, 3:4]))
        p.need("v", tk)
        tk = p.prod("v", lambda v: v.tensor_single_scalar(
            s_bis[0:16, 4:5], s_bis[0:16, 3:4], 256.0, ALU.is_ge))
        p.need("v", tk)
        tk = p.prod("v", lambda v: v.tensor_scalar(
            s_bis[0:16, 9:10], s_bis[0:16, 4:5], -1.0, 1.0, ALU.mult, ALU.add))
        p.need("v", tk)
        tk = p.prod("v", lambda v: v.tensor_scalar(
            s_bis[0:16, 8:9], s_bis[0:16, 2:3], s_bis[0:16, 4:5], None, ALU.mult))
        p.need("v", tk)
        tk = p.prod("v", lambda v: v.scalar_tensor_tensor(
            s_bis[0:16, 0:1], s_bis[0:16, 0:1], s_bis[0:16, 9:10], s_bis[0:16, 8:9],
            ALU.mult, ALU.add))
        p.need("v", tk)
        tk = p.prod("v", lambda v: v.tensor_scalar(
            s_bis[0:16, 8:9], s_bis[0:16, 2:3], s_bis[0:16, 9:10], None, ALU.mult))
        p.need("v", tk)
        tk = p.prod("v", lambda v: v.scalar_tensor_tensor(
            s_bis[0:16, 1:2], s_bis[0:16, 1:2], s_bis[0:16, 4:5], s_bis[0:16, 8:9],
            ALU.mult, ALU.add))
        p.need("v", tk)
        tk = p.prod("v", lambda v: v.tensor_add(s_bis[0:16, 2:3], s_bis[0:16, 0:1],
                                                s_bis[0:16, 1:2]))
        p.need("v", tk)
        tk = p.prod("v", lambda v: v.tensor_scalar_mul(s_bis[0:16, 2:3], s_bis[0:16, 2:3], 0.5))
        p.need("v", tk)
    tk = p.prod("v", lambda v: v.tensor_scalar(
        s_m16[0:16, :], s_probs[0:16, :], s_bis[0:16, 1:2], None, ALU.is_le))
    p.need("v", tk)
    tk = p.prod("v", lambda v: v.tensor_mul(s_m16[0:16, :], s_m16[0:16, :], s_probs[0:16, :]))
    p.need("v", tk)
    tk = p.prod("v", lambda v: v.reduce_max(s_bis[0:16, 5:6], s_m16[0:16, :], AX.X))
    p.need("v", tk)
    tk = p.prod("v", lambda v: v.tensor_scalar(
        s_gt[0:16, :], s_probs[0:16, :], s_bis[0:16, 5:6], None, ALU.is_gt, ALU.add,
        accum_out=s_bis[0:16, 3:4]))
    p.need("v", tk)
    tk = p.prod("v", lambda v: v.tensor_scalar(
        s_bis[0:16, 6:7], s_bis[0:16, 3:4], -1.0, 256.0, ALU.mult, ALU.add))
    p.need("v", tk)
    tk = p.prod("v", lambda v: v.tensor_scalar(
        s_eq[0:16, :], s_probs[0:16, :], s_bis[0:16, 5:6], None, ALU.is_equal))
    p.need("v", tk, t_ones16)
    tk = p.prod("v", lambda v: v.tensor_tensor_scan(
        s_m16[0:16, :], s_ones16[0:16, :], s_eq[0:16, :], 0.0, ALU.mult, ALU.add))
    p.need("v", tk)
    tk = p.prod("v", lambda v: v.tensor_sub(s_m16[0:16, :], s_m16[0:16, :], s_eq[0:16, :]))
    p.need("v", tk)
    tk = p.prod("v", lambda v: v.tensor_scalar(
        s_m16[0:16, :], s_m16[0:16, :], s_bis[0:16, 6:7], None, ALU.is_lt))
    p.need("v", tk)
    tk = p.prod("v", lambda v: v.tensor_mul(s_m16[0:16, :], s_m16[0:16, :], s_eq[0:16, :]))
    p.need("v", tk)
    tsel = p.prod("v", lambda v: v.tensor_add(s_sel[0:16, :], s_gt[0:16, :], s_m16[0:16, :]))
    p.need("v", tsel)
    tk = p.prod("v", lambda v: v.tensor_tensor_scan(
        s_sm[0:16, :], s_ones16[0:16, :], s_sel[0:16, :], 0.0, ALU.mult, ALU.add))
    p.need("v", tk)
    tk = p.prod("v", lambda v: v.tensor_sub(s_sm[0:16, :], s_sm[0:16, :], s_sel[0:16, :]))
    p.need("v", tk)
    tk = p.prod("v", lambda v: v.scalar_tensor_tensor(
        s_sm[0:16, :], s_sel[0:16, :], -384.0, s_sm[0:16, :], ALU.mult, ALU.add))
    p.need("v", tk)
    tk = p.prod("v", lambda v: v.tensor_scalar_add(s_sm[0:16, :], s_sm[0:16, :], 384.0))
    p.need("v", tk)
    tsmb = p.prod("v", lambda v: v.tensor_copy(s_smb[0:16, :], s_sm[0:16, :]))
    if cfg.debug:
        p.need("y", tsmb)
        p.dma("y", dbg["dbg_sm"][:], s_sm[:], ds_out)
    tk = p.prod("v", lambda v: v.scalar_tensor_tensor(
        s_m16[0:16, :], s_sel[0:16, :], 1.0, s_probs[0:16, :], ALU.mult, ALU.mult,
        accum_out=s_bis[0:16, 7:8]))
    p.need("v", tk)
    tk = p.prod("v", lambda v: v.tensor_scalar_add(s_bis[0:16, 7:8], s_bis[0:16, 7:8], 1e-9))
    p.need("v", tk)
    tSinv = p.prod("v", lambda v: v.reciprocal(s_bis[0:16, 7:8], s_bis[0:16, 7:8]))
    p.need("v", tSinv)
    twp = p.prod("v", lambda v: v.tensor_scalar(
        s_pu[0:16, :], s_probs[0:16, :], s_bis[0:16, 7:8], None, ALU.mult))
    p.need("v", twp)
    twpb = p.prod("v", lambda v: v.tensor_copy(s_wpb[0:16, :], s_pu[0:16, :]))
    p.need("y", tsmb, twpb)
    for tt in range(NT):
        p.dma("y", s_smT[:, tt * 16:(tt + 1) * 16], s_smb[0:16, tt * 128:(tt + 1) * 128],
              ds_smt, transpose=True)
        p.dma("y", s_probsT[:, tt * 16:(tt + 1) * 16], s_wpb[0:16, tt * 128:(tt + 1) * 128],
              ds_smt, transpose=True)
    p.need_dma("v", ds_smt)
    tsmtf = p.prod("v", lambda v: v.tensor_copy(s_smTf[:], s_smT[:]))

    p.join_all()
    nc.sbuf_base = mark0
    s_ye = sb("s_ye", [128, E * 2 * 1024], BF16)
    markF = nc.sbuf_base
    s_xgT = sb("s_xgT", [128, 2 * 1024], BF16)
    s_ohg = sb("s_ohg", [128, NT * 256], BF16)
    s_w1c = [sb(f"s_w1c{i}", [128, ND * 512], BF16) for i in range(2)]
    s_w2c = [sb(f"s_w2c{i}", [128, ND * 512], BF16) for i in range(2)]
    s_w3c = [sb(f"s_w3c{i}", [128, 4 * 1024], BF16) for i in range(2)]
    s_gs = sb("s_gs", [128, 1024], BF16)
    s_hb = sb("s_hb", [128, 1024], BF16)
    s_gp = sb("s_gp", [16, 256], F32)
    s_wrow = sb("s_wrow", [1, 256], BF16)
    s_wrf = sb("s_wrf", [1, 256], F32)
    s_wcol = sb("s_wcol", [128, 2], F32)

    gv_free = None
    ye_free = None
    gather_free = None
    gs_free = None
    ohg_consumed = None
    prev_tym = None

    s_wf1 = [sb(f"s_wf1{i}", [128, 1024], F32) for i in range(2)]
    s_wf2 = sb("s_wf2", [128, 1024], F32)
    ds_f1 = [p.dma_sem("ds_f1a"), p.dma_sem("ds_f1b")]
    ds_f2 = p.dma_sem("ds_f2b")
    ds_w3 = [p.dma_sem("ds_w3a"), p.dma_sem("ds_w3b")]
    stg1_free = [None, None]
    wcast1 = [[None] * 4 for _ in range(2)]
    wcast2 = [[None] * 4 for _ in range(2)]
    w_cons1 = [[None] * 4 for _ in range(2)]
    w_cons2 = [[None] * 4 for _ in range(2)]
    w_cons3 = [None, None]

    def wsrc(k, e, dffc, qtr):
        w = w1 if k == 1 else w2
        return w.ap()[e * NDC + dffc][:, qtr * 1024:(qtr + 1) * 1024]

    def issue_w1(e, dffc, qtr):
        buf = qtr % 2
        p.need("y", stg1_free[buf])
        p.dma("y", s_wf1[buf][:], wsrc(1, e, dffc, qtr), ds_f1[buf])

    def cast_w1(qtr, par):
        buf = qtr % 2
        p.need_dma("v", ds_f1[buf])
        p.need("v", w_cons1[par][qtr])
        tok = p.prod("v", lambda v, qtr=qtr, par=par, buf=buf: v.tensor_copy(
            s_w1c[par][:, qtr * 1024:(qtr + 1) * 1024], s_wf1[buf][:]))
        stg1_free[buf] = tok
        wcast1[par][qtr] = tok

    stg2_box = [None]

    def issue_w2(e, dffc, qtr):
        p.need("s", stg2_box[0])
        p.dma("s", s_wf2[:], wsrc(2, e, dffc, qtr), ds_f2)

    def cast_w2(qtr, par):
        p.need_dma("s", ds_f2)
        p.need("s", w_cons2[par][qtr])
        tok = p.prod("s", lambda s, qtr=qtr, par=par: s.activation(
            s_w2c[par][:, qtr * 1024:(qtr + 1) * 1024], s_wf2[:], AF.Copy))
        stg2_box[0] = tok
        wcast2[par][qtr] = tok

    def load_w3(e, dffc, par):
        p.need("g", w_cons3[par])
        p.dma("g", s_w3c[par][:], w3.ap()[e * NDC + dffc], ds_w3[par])

    load_w3(0, 0, 0)
    for _q in range(4):
        issue_w1(0, 0, _q)
        cast_w1(_q, 0)
    for _q in range(4):
        issue_w2(0, 0, _q)
        cast_w2(_q, 0)

    def build_onehots(e):
        toks = []
        for tt in range(NT):
            toks.append(p.prod("v", lambda v, tt=tt, e=e: v.tensor_scalar(
                s_ohg[:, tt * 256:(tt + 1) * 256], s_iota[:],
                s_smTf[:, tt * 16 + e: tt * 16 + e + 1], None, ALU.is_equal)))
        return toks

    p.need("v", tsmtf)
    ohg_toks = build_onehots(0)
    for e in range(E):
        p.need("t", *ohg_toks)
        p.need("t", x2n_done_tok)
        p.need("t", gv_free)
        p.need("t", ye_free)
        if e == 0:
            p.need("t", *ex_toks)
        gbanks8 = [pQ[0], pQ[1], pA[0], pA[1], pO[0], pO[1], pO[2], pO[3]]
        def mmg(t, e=e):
            for tt in range(NT):
                for dt in range(ND):
                    ins = t.matmul(gbanks8[dt][:, 0:256],
                                   s_x2n[:, tt * D + dt * 128: tt * D + dt * 128 + 128],
                                   s_ohg[:, tt * 256:(tt + 1) * 256],
                                   start=(tt == 0), stop=(tt == NT - 1))
            return ins
        tgm = p.prod("t", mmg)
        p.need("v", tgm)
        if prev_tym is not None:
            p.need("v", prev_tym)
        if cfg.debug:
            p.need_dma("v", ds_out)
        tge = []
        for b in range(8):
            tge.append(p.prod("v", lambda v, b=b: v.tensor_copy(
                s_xgT[:, b * 256:(b + 1) * 256], gbanks8[b][:, 0:256])))
        gather_free = tge[-1]

        p.need("t", *tge)
        def mmp(t, e=e):
            for tt in range(NT):
                ins = t.matmul(pO[0][0:16, 0:256], s_probsT[:, tt * 16:(tt + 1) * 16],
                               s_ohg[:, tt * 256:(tt + 1) * 256],
                               start=(tt == 0), stop=(tt == NT - 1))
            return ins
        tpm = p.prod("t", mmp)
        ohg_consumed = tpm
        p.need("v", tpm)
        twall = p.prod("v", lambda v: v.tensor_copy(s_gp[0:16, :], pO[0][0:16, 0:256]))
        p.need("g", twall)
        p.dma("g", s_wrow[:], s_gp.ap()[e:e + 1, :], ds_wr)
        p.need_dma("v", ds_wr)
        twrf = p.prod("v", lambda v: v.tensor_copy(s_wrf[:], s_wrow[:]))
        p.need("t", twrf)
        ttr0 = p.prod("t", lambda t: t.transpose(pO[1][:, 0:1], s_wrf[0:1, 0:128],
                                                 s_identf[0:1, 0:1]))
        ttr1 = p.prod("t", lambda t: t.transpose(pO[1][:, 1:2], s_wrf[0:1, 128:256],
                                                 s_identf[0:1, 0:1]))
        p.need("v", ttr0, ttr1)
        twcol = p.prod("v", lambda v: v.tensor_copy(s_wcol[:], pO[1][:, 0:2]))
        if e + 1 < E:
            p.need("v", tpm)
            ohg_toks = build_onehots(e + 1)

        hb_used = None
        for dffc in range(NDC):
            par = (e * NDC + dffc) % 2
            npar = (e * NDC + dffc + 1) % 2
            nxt = ((e, dffc + 1) if dffc + 1 < NDC
                   else ((e + 1, 0) if e + 1 < E else None))
            if nxt is not None:
                ne, nd = nxt
                load_w3(ne, nd, npar)
                issue_w1(ne, nd, 0)
                issue_w1(ne, nd, 1)
                cast_w1(0, npar)
                issue_w1(ne, nd, 2)
                cast_w1(1, npar)
                issue_w1(ne, nd, 3)
                cast_w1(2, npar)
                cast_w1(3, npar)
                issue_w2(ne, nd, 0)
            p.need("t", gather_free)
            fb = [pQ[0], pQ[1], pA[0], pA[1]]
            def mk_mm1(f, wc, par=par):
                def mm(t, f=f, wc=wc, par=par):
                    for j in range(ND):
                        g_ins = t.matmul(fb[f][:, 0:256],
                                         wc[par][:, f * 1024 + j * 128: f * 1024 + j * 128 + 128],
                                         s_xgT[:, j * 256:(j + 1) * 256],
                                         start=(j == 0), stop=(j == ND - 1))
                    return g_ins
                return mm
            tgg_f = []
            for f in range(4):
                p.need("t", wcast1[par][f])
                tgg_f.append(p.prod("t", mk_mm1(f, s_w1c)))
                w_cons1[par][f] = tgg_f[-1]
            if gs_free is not None:
                p.need("s", gs_free)
            sil = []
            for f in range(4):
                p.need("s", tgg_f[f])
                sil.append(p.prod("s", lambda s, f=f: s.activation(
                    s_gs[:, f * 256:(f + 1) * 256], fb[f][:, 0:256], AF.Silu)))
            if nxt is not None:
                for qq in range(4):
                    cast_w2(qq, npar)
                    if qq < 3:
                        issue_w2(nxt[0], nxt[1], qq + 1)
            tgv_f = []
            for f in range(4):
                p.need("t", sil[f])
                p.need("t", wcast2[par][f])
                tgv_f.append(p.prod("t", mk_mm1(f, s_w2c)))
                w_cons2[par][f] = tgv_f[-1]
            if hb_used is not None:
                p.need("v", hb_used)
            muls = []
            for f in range(4):
                p.need("v", tgv_f[f])
                p.need("v", sil[f])
                muls.append(p.prod("v", lambda v, f=f: v.tensor_mul(
                    s_hb[:, f * 256:(f + 1) * 256], fb[f][:, 0:256],
                    s_gs[:, f * 256:(f + 1) * 256])))
            thb = muls[-1]
            gs_free = thb
            gv_free = thb
            p.need_dma("t", ds_w3[par])
            if dffc == 0:
                p.need("t", tpm, ttr0, ttr1, twall, twcol)
            tym = None
            for f in range(4):
                p.need("t", muls[f])
                def mmyf(t, dffc=dffc, par=par, f=f):
                    for st in range(2):
                        for dc in range(2):
                            ins = t.matmul(
                                pO[2 * st + dc][:, 0:512],
                                s_hb[:, f * 256 + st * 128: f * 256 + st * 128 + 128],
                                s_w3c[par][:, f * 1024 + dc * 512: f * 1024 + (dc + 1) * 512],
                                start=(dffc == 0 and f == 0), stop=(dffc == NDC - 1 and f == 3))
                    return ins
                tym = p.prod("t", mmyf)
            w_cons3[par] = tym
            hb_used = tym
        p.need("v", hb_used, twcol)
        yev = []
        for st in range(2):
            for dc in range(2):
                yev.append(p.prod("v", lambda v, st=st, dc=dc, e=e: v.tensor_scalar(
                    s_ye[:, e * 2048 + st * 1024 + dc * 512: e * 2048 + st * 1024 + (dc + 1) * 512],
                    pO[2 * st + dc][:, 0:512], s_wcol[:, st:st + 1], None, ALU.mult)))
        ye_free = yev[-1]
        p.need("t", ye_free)
        prev_tym = hb_used

        if cfg.debug and e == 0:
            p.need("y", *yev)
            p.dma("y", dbg["dbg_ye"][:], s_ye[:, 0:2048], ds_out)
            p.need("y", tge[-1])
            p.dma("y", dbg["dbg_xgt"][:], s_xgT[:], ds_out)
            p.need("y", twcol)
            p.dma("y", dbg["dbg_wcol"][:], s_wcol[:], ds_out)

    p.join_all()
    nc.sbuf_base = markF
    s_ohgs = [sb(f"s_ohgs{i}", [128, 2 * 256], BF16) for i in range(2)]
    s_ohs = [sb(f"s_ohs{i}", [128, 4 * 128], BF16) for i in range(2)]
    s_ffn = sb("s_ffn", [128, D], BF16)
    s_outt = sb("s_outt", [128, D], F32)

    NTC = NT // 2
    nc.psum_base = 0
    scat_banks = [nc.alloc_psum_tensor(f"pS{i}", [128, 512], F32) for i in range(4)]
    pT = nc.alloc_psum_tensor("pT", [128, 512], BF16)
    p.need_dma("y", ds_x2)
    ohs_used = [None, None]
    ohgs_guard = [None, None]
    pT_free = None
    bank_free = {i: None for i in range(4)}
    for tc in range(NTC):
        mm_last = {i: None for i in range(4)}
        for e in range(E):
            par = e % 2
            p.need("v", tsmtf)
            p.need("v", ohgs_guard[par])
            og = []
            for ts_ in range(2):
                tt = tc * 2 + ts_
                og.append(p.prod("v", lambda v, ts_=ts_, tt=tt, e=e, par=par: v.tensor_scalar(
                    s_ohgs[par][:, ts_ * 256:(ts_ + 1) * 256], s_iota[:],
                    s_smTf[:, tt * 16 + e: tt * 16 + e + 1], None, ALU.is_equal)))
            p.need("t", *og)
            p.need("t", pT_free)
            ttr = []
            for ts_ in range(2):
                for st in range(2):
                    ttr.append(p.prod("t", lambda t, ts_=ts_, st=st, par=par: t.transpose(
                        pT[:, (st * 2 + ts_) * 128:(st * 2 + ts_ + 1) * 128],
                        s_ohgs[par][:, ts_ * 256 + st * 128: ts_ * 256 + st * 128 + 128],
                        s_identb[:])))
            ohgs_guard[par] = ttr[-1]
            p.need("v", *ttr)
            p.need("v", ohs_used[par])
            tev = []
            for k4 in range(4):
                tev.append(p.prod("v", lambda v, k4=k4, par=par: v.tensor_copy(
                    s_ohs[par][:, k4 * 128:(k4 + 1) * 128], pT[:, k4 * 128:(k4 + 1) * 128])))
            pT_free = tev[-1]
            p.need("t", *tev)
            if e == 0:
                for bi in range(4):
                    p.need("t", bank_free[bi])
            for ts_ in range(2):
                for dc in range(2):
                    bi = ts_ * 2 + dc
                    def mms(t, e=e, ts_=ts_, dc=dc, par=par, bi=bi):
                        for st in range(2):
                            ins = t.matmul(
                                scat_banks[bi][:],
                                s_ohs[par][:, (st * 2 + ts_) * 128:(st * 2 + ts_ + 1) * 128],
                                s_ye[:, e * 2048 + st * 1024 + dc * 512:
                                     e * 2048 + st * 1024 + (dc + 1) * 512],
                                start=(e == 0 and st == 0), stop=(e == E - 1 and st == 1))
                        return ins
                    mm_last[bi] = p.prod("t", mms)
            ohs_used[par] = mm_last[3]
        for ts_ in range(2):
            tt = tc * 2 + ts_
            par = tt % 2
            p.dma("y", s_xstage[par][:], sc_x2.ap()[tt * 128:(tt + 1) * 128, :], ds_x2r[par])
            ev = []
            for dc in range(2):
                bi = ts_ * 2 + dc
                p.need("v", mm_last[bi])
                ev.append(p.prod("v", lambda v, dc=dc, bi=bi: v.tensor_copy(
                    s_ffn[:, dc * 512:(dc + 1) * 512], scat_banks[bi][:])))
                bank_free[bi] = ev[-1]
            p.need("v", *ev)
            tst = ln_stats(s_ffn, s_mv[par][:])
            p.need("v", tst)
            trs = rstd_chain(s_mv[par], s_col[par][:])
            p.need("v", trs)
            p.need_dma("v", ds_x2r[par])
            tk = p.prod("v", lambda v, par=par: v.tensor_scalar(
                s_f32t[:], s_ffn[:], s_mv[par][:, 0:1], s_col[par][:],
                ALU.subtract, ALU.mult))
            p.need("v", tk)
            p.need_dma("v", ds_out)
            tout = p.prod("v", lambda v, par=par: v.tensor_add(s_outt[:], s_f32t[:],
                                                               s_xstage[par][:]))
            p.need("y", tout)
            p.dma("y", out_t.ap()[tt * 128:(tt + 1) * 128, :], s_outt[:], ds_out)

    p.join_all()
    with nc.Block() as block:
        p.realize(block)
    nc.finalize()
    return nc, din, dbg


_CACHE = {}


def _pack_attn_w(wq, wk, wv, wo):
    HH = wq.shape[0] // 256
    def pk(w):
        return np.ascontiguousarray(
            w.reshape(HH, 2, 128, 8, 128).transpose(0, 4, 3, 1, 2)).reshape(HH, 128, 2048)
    woP = np.ascontiguousarray(
        wo.reshape(8, 128, HH, 2, 128).transpose(2, 4, 3, 0, 1)).reshape(HH, 128, 2048)
    return pk(wq), pk(wk), pk(wv), woP


def _reorder_w(w1, w2, w3):
    E, D, DFF = w1.shape
    NDC = DFF // 512
    ND = D // 128
    w1r = np.ascontiguousarray(
        w1.reshape(E, ND, 128, NDC, 4, 128).transpose(0, 3, 2, 4, 1, 5)
    ).reshape(E * NDC, 128, ND * 512)
    w2r = np.ascontiguousarray(
        w2.reshape(E, ND, 128, NDC, 4, 128).transpose(0, 3, 2, 4, 1, 5)
    ).reshape(E * NDC, 128, ND * 512)
    w3r = np.ascontiguousarray(
        w3.reshape(E, NDC, 4, 128, D).transpose(0, 1, 3, 2, 4)
    ).reshape(E * NDC, 128, 4 * D)
    return w1r, w2r, w3r


def kernel(L0_attn_X, L0_attn_Wq, L0_attn_Wk, L0_attn_Wv, L0_attn_Wo,
           L0_attn_lambda_q1, L0_attn_lambda_k1, L0_attn_lambda_q2, L0_attn_lambda_k2,
           L0_attn_rms_scale, L0_ffn_router_w1, L0_ffn_router_w2,
           L0_ffn_w1, L0_ffn_w2, L0_ffn_w3):
    cfg = Cfg()
    if "nc" not in _CACHE:
        _CACHE["nc"] = build_program(cfg)[0]
    nc = _CACHE["nc"]
    consts = host_constants()
    X = np.ascontiguousarray(np.asarray(L0_attn_X, dtype=np.float32))
    wqP, wkP, wvP, woP = _pack_attn_w(
        np.asarray(L0_attn_Wq, np.float32), np.asarray(L0_attn_Wk, np.float32),
        np.asarray(L0_attn_Wv, np.float32), np.asarray(L0_attn_Wo, np.float32))
    base = {
        "wq": wqP, "wk": wkP, "wv": wvP, "wo": woP,
        "lq1": np.ascontiguousarray(np.asarray(L0_attn_lambda_q1, np.float32)),
        "lk1": np.ascontiguousarray(np.asarray(L0_attn_lambda_k1, np.float32)),
        "lq2": np.ascontiguousarray(np.asarray(L0_attn_lambda_q2, np.float32)),
        "lk2": np.ascontiguousarray(np.asarray(L0_attn_lambda_k2, np.float32)),
        "rmss": np.ascontiguousarray(np.asarray(L0_attn_rms_scale, np.float32).reshape(1, 256)),
        "rw1": np.ascontiguousarray(np.asarray(L0_ffn_router_w1, np.float32)),
        "rw2": np.ascontiguousarray(np.asarray(L0_ffn_router_w2, np.float32)),
    }
    base["w1"], base["w2"], base["w3"] = _reorder_w(
        np.asarray(L0_ffn_w1, np.float32), np.asarray(L0_ffn_w2, np.float32),
        np.asarray(L0_ffn_w3, np.float32))
    base.update(consts)
    in_maps = []
    for c in range(8):
        m = dict(base)
        m["x"] = np.ascontiguousarray(X[c])
        in_maps.append(m)
    res = run_bass_kernel_spmd(nc, in_maps, core_ids=list(range(8)))
    out = np.stack([np.asarray(r["out"], np.float32) for r in res.results], axis=0)
    return out


# version 20 (final)
# speedup vs baseline: 1.1128x; 1.0042x over previous
import math
import numpy as np
import ml_dtypes
import concourse.bass as bass
import concourse.mybir as mybir
from concourse.bass_utils import run_bass_kernel_spmd

F32 = mybir.dt.float32
BF16 = mybir.dt.bfloat16
AF = mybir.ActivationFunctionType
ALU = mybir.AluOpType
AX = mybir.AxisListType
bfdt = ml_dtypes.bfloat16

LAM_INIT = 0.8 - 0.6 * math.exp(-0.3)
SCALING = 1.0 / math.sqrt(128.0)
EPS = 1e-5


class Cfg:
    def __init__(self, tok=2048, heads=8, experts=16, dff=2048, debug=False, expose=False):
        self.expose = expose
        self.tok = tok; self.heads = heads; self.experts = experts
        self.dff = dff; self.d = 1024
        self.cap = tok * 2 // experts
        assert self.cap == 256 and experts == 16 and dff % 512 == 0
        self.nt = tok // 128
        self.nd = self.d // 128
        self.nq = tok // 512
        self.ndffc = dff // 512
        self.debug = debug


class DmaSem:
    def __init__(self, nc, name):
        self.sem = nc.alloc_semaphore(name)
        self.issued = 0


class Plan:
    ENGS = ("t", "v", "s", "g", "y")

    def __init__(self, nc):
        self.nc = nc
        self.ops = {k: [] for k in self.ENGS}
        self.sem = {k: nc.alloc_semaphore(f"prog_{k}") for k in self.ENGS}
        self.cnt = {k: 0 for k in self.ENGS}
        self.waited = {k: {j: 0 for j in self.ENGS} for k in self.ENGS}
        self.dma_waited = {k: {} for k in self.ENGS}
        self.dsems = []

    def dma_sem(self, name):
        ds = DmaSem(self.nc, name)
        self.dsems.append(ds)
        return ds

    def emit(self, e, fn):
        self.ops[e].append(fn)

    def prod(self, e, fn):
        self.cnt[e] += 1
        lvl = self.cnt[e]
        s = self.sem[e]
        self.ops[e].append(lambda eng, fn=fn, s=s: fn(eng).then_inc(s, 1))
        return (e, lvl)

    def need(self, e, *tokens):
        for tok in tokens:
            if tok is None:
                continue
            pe, lvl = tok
            if pe == e:
                if self.waited[e][e] < lvl:
                    self.ops[e].append(lambda eng: eng.drain())
                    self.waited[e][e] = self.cnt[e]
            else:
                if self.waited[e][pe] < lvl:
                    s = self.sem[pe]
                    self.ops[e].append(lambda eng, s=s, L=lvl: eng.wait_ge(s, L))
                    self.waited[e][pe] = lvl

    def dma(self, q, out_ap, in_ap, ds, **kw):
        ds.issued += 16
        self.ops[q].append(
            lambda eng, o=out_ap, i=in_ap, s=ds.sem, kw=kw: eng.dma_start(o, i, **kw).then_inc(s, 16))

    def need_dma(self, e, ds):
        lvl = ds.issued
        if lvl and self.dma_waited[e].get(id(ds), 0) < lvl:
            self.ops[e].append(lambda eng, s=ds.sem, L=lvl: eng.wait_ge(s, L))
            self.dma_waited[e][id(ds)] = lvl

    def token(self, e):
        return (e, self.cnt[e])

    def join_all(self):
        for e in self.ENGS:
            for pe in self.ENGS:
                if pe != e:
                    self.need(e, (pe, self.cnt[pe]))
            for ds in self.dsems:
                self.need_dma(e, ds)

    def realize(self, block):
        block.tensor(lambda eng: [f(eng) for f in self.ops["t"]])
        block.vector(lambda eng: [f(eng) for f in self.ops["v"]])
        block.scalar(lambda eng: [f(eng) for f in self.ops["s"]])
        block.gpsimd(lambda eng: [f(eng) for f in self.ops["g"]])
        block.sync(lambda eng: [f(eng) for f in self.ops["y"]])


def host_constants():
    return {
        "c_identf": np.eye(128, dtype=np.float32),
        "c_iota_bc": np.broadcast_to(np.arange(256, dtype=np.float32), (128, 256)).astype(bfdt).copy(),
        "c_iotacol0": np.arange(128, dtype=np.float32).reshape(128, 1),
        "c_iotacol1": (np.arange(128, dtype=np.float32) + 128.0).reshape(128, 1),
        "c_onesrow_b": np.ones((1, 128), dtype=bfdt),
        "c_onesrow_f": np.ones((1, 128), dtype=np.float32),
        "c_onescol_b": np.ones((128, 1), dtype=bfdt),
        "c_identb": np.eye(128, dtype=np.float32).astype(bfdt),
        "c_bar": np.zeros((1, 8), dtype=np.float32),
    }


def build_program(cfg):
    nc = bass.Bass("TRN2", target_bir_lowering=False)
    T, H, E, DFF, D = cfg.tok, cfg.heads, cfg.experts, cfg.dff, cfg.d
    NT, ND, NQ, NDC = cfg.nt, cfg.nd, cfg.nq, cfg.ndffc
    EH = 256

    din = {}
    def dram_in(name, shape, dt=F32):
        din[name] = nc.dram_tensor(name, list(shape), dt, kind="ExternalInput")
        return din[name]

    x_in = dram_in("x", (T, D))
    wq = dram_in("wq", (H, 128, 2 * 1024)); wk = dram_in("wk", (H, 128, 2 * 1024))
    wv = dram_in("wv", (H, 128, 2 * 1024)); wo = dram_in("wo", (H, 128, 2 * 1024))
    lq1 = dram_in("lq1", (H, 128)); lk1 = dram_in("lk1", (H, 128))
    lq2 = dram_in("lq2", (H, 128)); lk2 = dram_in("lk2", (H, 128))
    rmss = dram_in("rmss", (1, EH))
    rw1 = dram_in("rw1", (D, 128)); rw2 = dram_in("rw2", (128, E))
    w1 = dram_in("w1", (E * NDC, 128, ND * 512)); w2 = dram_in("w2", (E * NDC, 128, ND * 512))
    w3 = dram_in("w3", (E * NDC, 128, 4 * 1024))
    identf = dram_in("c_identf", (128, 128))
    iota_bc = dram_in("c_iota_bc", (128, 256), BF16)
    iotacol0 = dram_in("c_iotacol0", (128, 1))
    iotacol1 = dram_in("c_iotacol1", (128, 1))
    onesrow_b = dram_in("c_onesrow_b", (1, 128), BF16)
    onesrow_f = dram_in("c_onesrow_f", (1, 128))
    onescol_b = dram_in("c_onescol_b", (128, 1), BF16)
    identb = dram_in("c_identb", (128, 128), BF16)
    dram_in("c_bar", (1, 8), F32)

    out_t = nc.dram_tensor("out", [T, D], F32, kind="ExternalOutput")
    dbgkind = "ExternalOutput" if (cfg.debug or cfg.expose) else "Internal"
    sc_ot = nc.dram_tensor("sc_ot", [2 * H * 128, T], BF16, kind=dbgkind)
    sc_x2 = nc.dram_tensor("sc_x2", [T, D], F32, kind=dbgkind)
    sc_wot = nc.dram_tensor("sc_wot", [2 * H * 128, 1024], BF16, kind="Internal")
    dbg = {}
    if cfg.debug:
        for nm, shape, dt in [
            ("dbg_xnt", (128, ND * T), BF16),
            ("dbg_probs", (16, T), F32),
            ("dbg_sm", (16, T), F32),
            ("dbg_attnout", (128, NT * D), BF16),
            ("dbg_ye", (128, 2 * 1024), BF16),
            ("dbg_xgt", (128, 2 * 1024), BF16),
            ("dbg_wcol", (128, 2), F32),
        ]:
            dbg[nm] = nc.dram_tensor(nm, list(shape), dt, kind="ExternalOutput")

    sb = nc.alloc_sbuf_tensor
    s_identf = sb("s_identf", [128, 128], F32)
    s_iota = sb("s_iota", [128, 256], BF16)
    s_ic0 = sb("s_ic0", [128, 1], F32)
    s_ic1 = sb("s_ic1", [128, 1], F32)
    s_or_b = sb("s_or_b", [1, 128], BF16)
    s_or_f = sb("s_or_f", [1, 128], F32)
    s_oc_b = sb("s_oc_b", [128, 1], BF16)
    s_identb = sb("s_identb", [128, 128], BF16)
    s_rsbc = sb("s_rsbc", [128, 256], F32)
    s_lam = sb("s_lam", [H, 1], F32)
    s_lamt = sb("s_lamt", [1, H], F32)
    s_lambc = sb("s_lambc", [128, H], F32)
    s_rw1b = sb("s_rw1b", [128, ND * 128], BF16)
    s_rw2b = sb("s_rw2b", [128, E], BF16)
    s_scal = sb("s_scal", [1, 16], F32)
    s_col = [sb(f"s_col{i}", [128, 1], F32) for i in range(8)]
    s_bn = sb("s_bn", [128, 12], F32)
    s_mv = [sb(f"s_mv{i}", [128, 2], F32) for i in range(2)]
    s_xstage = [sb(f"s_xstage{i}", [128, D], F32) for i in range(2)]
    s_xn = [sb(f"s_xn{i}", [128, D], BF16) for i in range(2)]
    s_f32t = sb("s_f32t", [128, D], F32)
    s_x2tile = sb("s_x2tile", [128, D], F32)
    s_lq = sb("s_lq", [H, 128], F32); s_lk = sb("s_lk", [H, 128], F32)
    s_lq2 = sb("s_lq2", [H, 128], F32); s_lk2 = sb("s_lk2", [H, 128], F32)
    s_rms = sb("s_rms", [1, 256], F32)
    s_x2n = sb("s_x2n", [128, NT * D], BF16)
    s_smT = sb("s_smT", [128, NT * 16], BF16)
    s_smTf = sb("s_smTf", [128, NT * 16], F32)
    s_probsT = sb("s_probsT", [128, NT * 16], BF16)

    p = Plan(nc)
    ds_c = p.dma_sem("ds_c")
    ds_rw = p.dma_sem("ds_rw")
    ds_x = [p.dma_sem(f"ds_x{i}") for i in range(2)]
    ds_xt = p.dma_sem("ds_xt")
    ds_w = [p.dma_sem(f"ds_w{i}") for i in range(2)]
    ds_ott = p.dma_sem("ds_ott")
    ds_ots = p.dma_sem("ds_ots")
    ds_wos = p.dma_sem("ds_wos")
    ds_wor = p.dma_sem("ds_wor")
    ds_otr = [p.dma_sem(f"ds_otr{i}") for i in range(2)]
    ds_x2 = p.dma_sem("ds_x2")
    ds_x2t = [p.dma_sem(f"ds_x2t{i}") for i in range(2)]
    ds_xr = [p.dma_sem(f"ds_xr{i}") for i in range(2)]
    ds_smt = p.dma_sem("ds_smt")
    ds_oht = [p.dma_sem(f"ds_oht{i}") for i in range(2)]
    ds_x2r = [p.dma_sem(f"ds_x2r{i}") for i in range(2)]
    ds_out = p.dma_sem("ds_out")

    nc.psum_base = 0
    pQ = [nc.alloc_psum_tensor(f"pQ{i}", [128, 512], F32) for i in range(2)]
    pA = [nc.alloc_psum_tensor(f"pA{i}", [128, 512], F32) for i in range(2)]
    pO = [nc.alloc_psum_tensor(f"pO{i}", [128, 512], F32) for i in range(4)]

    for dst, src in [(s_identf, identf), (s_iota, iota_bc), (s_ic0, iotacol0),
                     (s_ic1, iotacol1), (s_or_b, onesrow_b), (s_or_f, onesrow_f),
                     (s_oc_b, onescol_b), (s_identb, identb), (s_lq, lq1), (s_lk, lk1), (s_lq2, lq2),
                     (s_lk2, lk2), (s_rms, rmss)]:
        p.dma("y", dst[:], src[:], ds_c)
    p.dma("g", s_rw1b.ap().rearrange("p (c f) -> p c f", c=ND),
          rw1.ap().rearrange("(c p) f -> p c f", p=128), ds_rw)
    p.dma("g", s_rw2b[:], rw2[:], ds_rw)

    p.need_dma("v", ds_c)
    s_ltmp = sb("s_ltmp", [H, 128], F32)
    s_l1 = sb("s_l1", [H, 1], F32); s_l2 = sb("s_l2", [H, 1], F32)
    tk = p.prod("v", lambda v: v.tensor_mul(s_ltmp[:], s_lq[:], s_lk[:]))
    p.need("v", tk)
    t1 = p.prod("v", lambda v: v.reduce_sum(s_l1[:], s_ltmp[:], AX.X))
    p.need("v", t1)
    tk = p.prod("v", lambda v: v.tensor_mul(s_ltmp[:], s_lq2[:], s_lk2[:]))
    p.need("v", tk)
    t2 = p.prod("v", lambda v: v.reduce_sum(s_l2[:], s_ltmp[:], AX.X))
    p.need("s", t1, t2)
    te1 = p.prod("s", lambda s: s.activation(s_l1[:], s_l1[:], AF.Exp))
    p.need("s", te1)
    te2 = p.prod("s", lambda s: s.activation(s_l2[:], s_l2[:], AF.Exp))
    p.need("v", te1, te2)
    tk = p.prod("v", lambda v: v.tensor_sub(s_lam[:], s_l1[:], s_l2[:]))
    p.need("v", tk)
    tlam = p.prod("v", lambda v: v.tensor_scalar_add(s_lam[:], s_lam[:], LAM_INIT))
    p.need("t", tlam)
    p.need_dma("t", ds_c)
    tk = p.prod("t", lambda t: t.transpose(pQ[0][0:1, 0:H], s_lam[:], s_identf[0:H, 0:H]))
    p.need("v", tk)
    tk = p.prod("v", lambda v: v.tensor_copy(s_lamt[:], pQ[0][0:1, 0:H]))
    p.need("t", tk)
    tk = p.prod("t", lambda t: t.matmul(pQ[1][:, 0:H], s_or_f[:], s_lamt[:], start=True, stop=True))
    p.need("v", tk)
    tlambc = p.prod("v", lambda v: v.tensor_copy(s_lambc[:], pQ[1][:, 0:H]))
    tk = p.prod("t", lambda t: t.matmul(pQ[0][:, 0:256], s_or_f[:], s_rms[:], start=True, stop=True))
    p.need("v", tk)
    trsbc = p.prod("v", lambda v: v.tensor_scalar_mul(s_rsbc[:], pQ[0][:, 0:256], 1.0 - LAM_INIT))
    pQ_tok = {0: trsbc, 1: tlambc}

    def ln_stats(tile_ap, mv_ap):
        t1 = p.prod("v", lambda v: v.bn_stats(s_bn[:, 0:6], tile_ap[:, 0:512]))
        t2 = p.prod("v", lambda v: v.bn_stats(s_bn[:, 6:12], tile_ap[:, 512:1024]))
        p.need("v", t1, t2)
        return p.prod("v", lambda v: v.bn_aggr(mv_ap, s_bn[:]))

    def rstd_chain(mv_ap, out_col):
        tv = p.prod("v", lambda v: v.tensor_scalar_add(out_col, mv_ap[:, 1:2], EPS))
        p.need("s", tv)
        t1 = p.prod("s", lambda s: s.activation(out_col, out_col, AF.Ln))
        p.need("s", t1)
        return p.prod("s", lambda s: s.activation(out_col, out_col, AF.Exp, scale=-0.5))

    mark0 = nc.sbuf_base
    s_xnT = sb("s_xnT", [128, ND * T], BF16)
    s_wsf = [sb(f"s_wsf{i}", [128, 1024], F32) for i in range(2)]
    s_wqT = [sb(f"s_wqT{i}", [128, ND * 256], BF16) for i in range(2)]
    s_wkT = [sb(f"s_wkT{i}", [128, ND * 256], BF16) for i in range(2)]
    s_wvT = [sb(f"s_wvT{i}", [128, ND * 256], BF16) for i in range(2)]
    s_woTh = sb("s_woTh", [128, 2 * 1024], BF16)
    s_qt = sb("s_qt", [128, 2 * T], BF16)
    s_kt = sb("s_kt", [128, 2 * T], BF16)
    s_v = sb("s_v", [128, NT * 257], BF16)
    s_sq = sb("s_sq", [128, 2 * T], BF16)
    s_et = sb("s_et", [128, NT * 512], BF16)
    s_o1 = sb("s_o1", [128, 4 * 257], F32)
    s_o2 = sb("s_o2", [128, 4 * 257], F32)
    s_oc = sb("s_oc", [128, 256], F32)
    s_t2 = sb("s_t2", [128, 256], F32)
    s_otil = [sb(f"s_otil{i}", [128, 256], BF16) for i in range(2)]
    s_otT = sb("s_otT", [128, 2 * T], BF16)
    s_bias = sb("s_bias", [128, 1], F32)

    xre = x_in.ap().rearrange("(n p) f -> n p f", p=128)
    txn_toks = []
    for i in range(NT):
        par = i % 2
        if i >= 2:
            p.need("y", txn_toks[i - 2])
        p.dma("y", s_xstage[par][:], xre[i], ds_x[par])
        p.need_dma("v", ds_x[par])
        if i >= 2:
            p.need_dma("v", ds_xt)
        tst = ln_stats(s_xstage[par], s_mv[par][:])
        p.need("v", tst)
        trs = rstd_chain(s_mv[par], s_col[par][:])
        p.need("v", trs)
        txn = p.prod("v", lambda v, par=par: v.tensor_scalar(
            s_xn[par][:], s_xstage[par][:], s_mv[par][:, 0:1], s_col[par][:],
            ALU.subtract, ALU.mult))
        txn_toks.append(txn)
        p.need("y", txn)
        for j in range(ND):
            p.dma("y", s_xnT[:, j * T + i * 128: j * T + (i + 1) * 128],
                  s_xn[par][:, j * 128:(j + 1) * 128], ds_xt, transpose=True)
    if cfg.debug:
        p.need_dma("y", ds_xt)
        p.dma("y", dbg["dbg_xnt"][:], s_xnT[:], ds_out)

    wslice_done = {}

    wcv_tok = [None, None]
    hw_idx = [0]
    conv_done = {0: [], 1: []}

    def head_weight_load(h, par, prev_qkv_t):
        toks = []
        p.need("v", prev_qkv_t)
        for wi, (wp, dstbuf) in enumerate([(wq, s_wqT[par]), (wk, s_wkT[par]),
                                           (wv, s_wvT[par]), (wo, s_woTh)]):
            for c in range(2):
                sp = hw_idx[0] % 2
                hw_idx[0] += 1
                p.need("s", wcv_tok[sp])
                p.dma("s", s_wsf[sp][:], wp.ap()[h][:, c * 1024:(c + 1) * 1024],
                      ds_w[sp])
                p.need_dma("v", ds_w[sp])
                if wi == 3 and c == 0:
                    p.need_dma("v", ds_wos)
                tcv = p.prod("v", lambda v, sp=sp, dstbuf=dstbuf, c=c: v.tensor_copy(
                    dstbuf[:, c * 1024:(c + 1) * 1024], s_wsf[sp][:]))
                wcv_tok[sp] = tcv
                toks.append(tcv)
        conv_done[par] = toks
        p.need("y", toks[-1])
        for ch in range(2):
            ec = 2 * h + ch
            p.dma("y", sc_wot.ap()[ec * 128:(ec + 1) * 128, :],
                  s_woTh[:, ch * 1024:(ch + 1) * 1024], ds_wos)

    head_weight_load(0, 0, None)
    p.need_dma("t", ds_xt)

    qkv_bank_init = {0: pQ_tok.get(0), 1: pQ_tok.get(1), 2: None, 3: None}
    qkv_t_done = {}
    prev_head_consumers = None
    for h in range(H):
        par = h % 2
        if h + 1 < H:
            head_weight_load(h + 1, (h + 1) % 2, qkv_t_done.get(h - 1))
        p.need("t", *conv_done[par])

        if prev_head_consumers is not None:
            p.need("v", *prev_head_consumers)
        qkvb = [pQ[0], pQ[1], pA[0], pA[1]]
        bank_tok = dict(qkv_bank_init)
        gi = 0
        for (wbuf, obuf) in [(s_wqT[par], s_qt), (s_wkT[par], s_kt)]:
            for et in range(2):
                for qc in range(NQ):
                    bi = gi % 4
                    gi += 1
                    p.need("t", bank_tok.get(bi))
                    def mmf(t, wbuf=wbuf, et=et, qc=qc, bank=qkvb[bi]):
                        for j in range(ND):
                            ins = t.matmul(bank[:],
                                           wbuf[:, j * 256 + et * 128: j * 256 + et * 128 + 128],
                                           s_xnT[:, j * T + qc * 512: j * T + (qc + 1) * 512],
                                           start=(j == 0), stop=(j == ND - 1))
                        return ins
                    tmm = p.prod("t", mmf)
                    p.need("v", tmm)
                    bank_tok[bi] = p.prod("v", lambda v, obuf=obuf, et=et, qc=qc, bi=bi:
                                          v.tensor_copy(obuf[:, et * T + qc * 512: et * T + (qc + 1) * 512],
                                                        qkvb[bi][:]))
        if h == 0:
            t_vones = p.prod("v", lambda v: v.memset(s_v[:], 1.0))
            p.need("t", t_vones)
        for tt in range(NT):
            bi = gi % 4
            gi += 1
            p.need("t", bank_tok.get(bi))
            def mmv(t, tt=tt, bank=qkvb[bi], par=par):
                for j in range(ND):
                    ins = t.matmul(bank[:, 0:256],
                                   s_xnT[:, j * T + tt * 128: j * T + tt * 128 + 128],
                                   s_wvT[par][:, j * 256:(j + 1) * 256],
                                   start=(j == 0), stop=(j == ND - 1))
                return ins
            tmm = p.prod("t", mmv)
            p.need("v", tmm)
            bank_tok[bi] = p.prod("v", lambda v, tt=tt, bi=bi:
                                  v.tensor_copy(s_v[:, tt * 257: tt * 257 + 256], qkvb[bi][:, 0:256]))
        qkv_done_v = p.token("v")

        p.need("s", qkv_done_v)
        sq_consumed = None
        pA0_tok = None
        last_red = None
        for bi_src, srcbuf in [(0, s_qt), (1, s_kt)]:
            if sq_consumed is not None:
                p.need("s", sq_consumed)
            tsq0 = p.prod("s", lambda s, srcbuf=srcbuf: s.activation(
                s_sq[:, 0:T], srcbuf[:, 0:T], AF.Square))
            tsq1 = p.prod("s", lambda s, srcbuf=srcbuf: s.activation(
                s_sq[:, T:2 * T], srcbuf[:, T:2 * T], AF.Square))
            p.need("t", tsq0, tsq1)
            red_toks = []
            last_mm = None
            for ch in range(NQ):
                if last_red is not None:
                    p.need("t", last_red)
                def mmn(t, ch=ch):
                    t.matmul(pO[0][0:1, 0:512], s_oc_b[:], s_sq[:, ch * 512:(ch + 1) * 512],
                             start=True, stop=False)
                    return t.matmul(pO[0][0:1, 0:512], s_oc_b[:],
                                    s_sq[:, T + ch * 512: T + (ch + 1) * 512],
                                    start=False, stop=True)
                last_mm = p.prod("t", mmn)
                p.need("v", last_mm)
                last_red = p.prod("v", lambda v, ch=ch: v.reduce_max(
                    s_scal[0:1, ch:ch + 1], pO[0][0:1, 0:512], AX.X))
                red_toks.append(last_red)
            p.need("v", *red_toks)
            tmx = p.prod("v", lambda v, bi_src=bi_src: v.reduce_max(
                s_scal[0:1, 8 + bi_src: 9 + bi_src], s_scal[0:1, 0:NQ], AX.X))
            p.need("v", tmx)
            sq_consumed = last_mm
            pA0_tok = red_toks[-1]
        tmul = p.prod("v", lambda v: v.tensor_mul(s_scal[0:1, 10:11], s_scal[0:1, 8:9],
                                                  s_scal[0:1, 9:10]))
        p.need("s", tmul)
        tln = p.prod("s", lambda s: s.activation(s_scal[0:1, 10:11], s_scal[0:1, 10:11], AF.Ln))
        p.need("s", tln)
        tex = p.prod("s", lambda s: s.activation(s_scal[0:1, 10:11], s_scal[0:1, 10:11],
                                                 AF.Exp, scale=0.5))
        p.need("v", tex)
        tbn = p.prod("v", lambda v: v.tensor_scalar_mul(s_scal[0:1, 10:11],
                                                        s_scal[0:1, 10:11], -SCALING))
        p.need("t", tbn, pA0_tok)
        tbb = p.prod("t", lambda t: t.matmul(pO[0][:, 0:1], s_or_f[:], s_scal[0:1, 10:11],
                                             start=True, stop=True))
        p.need("v", tbb)
        tbias = p.prod("v", lambda v: v.tensor_copy(s_bias[:], pO[0][:, 0:1]))
        pAb = [pQ[0], pQ[1], pA[0], pA[1]]
        pA_tok = dict(bank_tok)

        p.need("s", tbias)
        pO_free = {0: tbias, 1: None, 2: None, 3: None}
        et_free = None
        last_a_mm = None
        for qc in range(NQ):
            o1_evac = [None] * 4
            ev_last = [None] * 4
            for half in range(2):
                def mk_a(kt):
                    bi = kt % 4
                    p.need("t", pA_tok.get(bi))
                    def mma(t, half=half, kt=kt, qc=qc, bi=bi):
                        return t.matmul(pAb[bi][:],
                                        s_kt[:, half * T + kt * 128: half * T + (kt + 1) * 128],
                                        s_qt[:, half * T + qc * 512: half * T + (qc + 1) * 512],
                                        start=True, stop=True)
                    return p.prod("t", mma)
                tka = {}
                for kt in range(min(4, NT)):
                    tka[kt] = mk_a(kt)
                for kt in range(NT):
                    bi = kt % 4
                    last_a_mm = tka[kt]
                    p.need("s", tka[kt])
                    if kt == 0 and et_free is not None:
                        p.need("s", *et_free)
                    texp = p.prod("s", lambda s, kt=kt, bi=bi: s.activation(
                        s_et[:, kt * 512:(kt + 1) * 512], pAb[bi][:], AF.Exp,
                        bias=s_bias[:], scale=SCALING))
                    pA_tok[bi] = texp
                    p.need("t", texp)
                    for qt in range(4):
                        if kt == 0:
                            p.need("t", pO_free[qt])
                        def mme(t, kt=kt, qt=qt):
                            return t.matmul(pO[qt][:, 0:257],
                                            s_et[:, kt * 512 + qt * 128: kt * 512 + qt * 128 + 128],
                                            s_v[:, kt * 257:(kt + 1) * 257],
                                            start=(kt == 0), stop=(kt == NT - 1))
                        ev_last[qt] = p.prod("t", mme)
                    if kt + 4 < NT:
                        def mma2(t, half=half, kt2=kt + 4, qc=qc, bi=(kt + 4) % 4):
                            return t.matmul(pAb[bi][:],
                                            s_kt[:, half * T + kt2 * 128: half * T + (kt2 + 1) * 128],
                                            s_qt[:, half * T + qc * 512: half * T + (qc + 1) * 512],
                                            start=True, stop=True)
                        tka[kt + 4] = p.prod("t", mma2)
                et_free = list(ev_last)
                dstb = s_o1 if half == 0 else s_o2
                for qt in range(4):
                    p.need("v", ev_last[qt])
                    tkev = p.prod("v", lambda v, qt=qt, dstb=dstb: v.tensor_copy(
                        dstb[:, qt * 257:(qt + 1) * 257], pO[qt][:, 0:257]))
                    pO_free[qt] = tkev
                    if half == 0:
                        o1_evac[qt] = tkev
                    else:
                        ev_last[qt] = tkev
            for qt in range(4):
                p.need("v", ev_last[qt], o1_evac[qt], tlambc, trsbc)
                tqt = qc * 4 + qt
                c0, c1, c2, c5 = s_col[2][:], s_col[3][:], s_col[4][:], s_col[5][:]
                tk = p.prod("v", lambda v, qt=qt, c0=c0: v.reciprocal(
                    c0, s_o1[:, qt * 257 + 256: qt * 257 + 257]))
                tk2 = p.prod("v", lambda v, qt=qt, c1=c1: v.reciprocal(
                    c1, s_o2[:, qt * 257 + 256: qt * 257 + 257]))
                p.need("v", tk, tk2)
                tk = p.prod("v", lambda v, h=h, c1=c1, c2=c2: v.tensor_mul(
                    c2, c1, s_lambc[:, h:h + 1]))
                p.need("v", tk)
                tk = p.prod("v", lambda v, qt=qt, c2=c2: v.tensor_scalar(
                    s_t2[:], s_o2[:, qt * 257: qt * 257 + 256], c2, None, ALU.mult))
                p.need("v", tk)
                tk = p.prod("v", lambda v, qt=qt, c0=c0: v.scalar_tensor_tensor(
                    s_oc[:], s_o1[:, qt * 257: qt * 257 + 256], c0, s_t2[:],
                    ALU.mult, ALU.subtract))
                p.need("v", tk)
                tssq = p.prod("v", lambda v, c5=c5: v.scalar_tensor_tensor(
                    s_t2[:], s_oc[:], 1.0, s_oc[:], ALU.mult, ALU.mult, accum_out=c5))
                p.need("v", tssq)
                tk = p.prod("v", lambda v, c5=c5: v.tensor_scalar(
                    c5, c5, 1.0 / 256.0, EPS, ALU.mult, ALU.add))
                p.need("s", tk)
                tk = p.prod("s", lambda s, c5=c5: s.activation(c5, c5, AF.Ln))
                p.need("s", tk)
                trst = p.prod("s", lambda s, c5=c5: s.activation(c5, c5, AF.Exp, scale=-0.5))
                p.need("v", trst)
                p.need_dma("v", ds_ott)
                tk = p.prod("v", lambda v, c5=c5: v.tensor_scalar(
                    s_t2[:], s_oc[:], c5, None, ALU.mult))
                p.need("v", tk)
                totil = p.prod("v", lambda v, qt=qt: v.tensor_mul(
                    s_otil[qt % 2][:], s_t2[:], s_rsbc[:]))
                p.need("y", totil)
                for c in range(2):
                    p.dma("y", s_otT[:, c * T + tqt * 128: c * T + (tqt + 1) * 128],
                          s_otil[qt % 2][:, c * 128:(c + 1) * 128], ds_ott, transpose=True)
        qkv_t_done[h] = last_a_mm
        qkv_bank_init = dict(pA_tok)
        pQ_tok = {0: pA_tok.get(0), 1: pA_tok.get(1)}
        prev_head_consumers = [last_a_mm, p.token("s")]
        p.need_dma("y", ds_ott)
        for c in range(2):
            ec = 2 * h + c
            p.dma("y", sc_ot.ap()[ec * 128:(ec + 1) * 128, :], s_otT[:, c * T:(c + 1) * T], ds_ots)
        p.need_dma("y", ds_ots)

    p.join_all()
    nc.sbuf_base = mark0
    s_woT = sb("s_woT", [128, 2 * H * 1024], BF16)
    s_otq = [sb(f"s_otq{i}", [128, 2 * H * 512], BF16) for i in range(2)]
    s_ao4 = sb("s_ao4", [128, 4 * D], BF16)
    s_x4 = sb("s_x4", [128, 4 * D], F32)
    s_x24 = sb("s_x24", [128, 4 * D], F32)
    s_bn4 = sb("s_bn4", [128, 48], F32)
    s_mv4 = sb("s_mv4", [128, 8], F32)

    for ec in range(2 * H):
        p.dma("y", s_woT[:, ec * 1024:(ec + 1) * 1024], sc_wot.ap()[ec * 128:(ec + 1) * 128, :],
              ds_wor)
    p.need_dma("t", ds_wor)

    otq_used = [None, None]
    ao_used = None
    x4_used = None
    x2n_done_tok = None
    wo_bank_tok = {i: None for i in range(4)}
    wob = [pQ[0], pQ[1], pA[0], pA[1]]
    for qc in range(NQ):
        par = qc % 2
        p.need("y", otq_used[par])
        for ec in range(2 * H):
            p.dma("y", s_otq[par][:, ec * 512:(ec + 1) * 512],
                  sc_ot.ap()[ec * 128:(ec + 1) * 128, qc * 512:(qc + 1) * 512], ds_otr[par])
        p.need("y", x4_used)
        for i in range(4):
            p.dma("y", s_x4[:, i * D:(i + 1) * D], xre[qc * 4 + i], ds_xr[0])
        p.need_dma("t", ds_otr[par])
        evs = []
        for qt in range(4):
            for dc in range(2):
                bi = (qt * 2 + dc) % 4
                p.need("t", wo_bank_tok.get(bi))
                def mmw(t, par=par, qt=qt, dc=dc, bank=wob[bi]):
                    for ec in range(2 * H):
                        ins = t.matmul(bank[:],
                                       s_otq[par][:, ec * 512 + qt * 128: ec * 512 + qt * 128 + 128],
                                       s_woT[:, ec * 1024 + dc * 512: ec * 1024 + (dc + 1) * 512],
                                       start=(ec == 0), stop=(ec == 2 * H - 1))
                    return ins
                tmm = p.prod("t", mmw)
                p.need("v", tmm)
                p.need("v", ao_used)
                if cfg.debug:
                    p.need_dma("v", ds_out)
                wo_bank_tok[bi] = p.prod("v", lambda v, qt=qt, dc=dc, bi=bi: v.tensor_copy(
                    s_ao4[:, qt * D + dc * 512: qt * D + (dc + 1) * 512], wob[bi][:]))
                evs.append(wo_bank_tok[bi])
        otq_used[par] = p.token("t")
        p.need("v", *evs)
        st_toks = []
        for i in range(4):
            t1 = p.prod("v", lambda v, i=i: v.bn_stats(
                s_bn4[:, i * 12: i * 12 + 6], s_ao4[:, i * D: i * D + 512]))
            t2 = p.prod("v", lambda v, i=i: v.bn_stats(
                s_bn4[:, i * 12 + 6: i * 12 + 12], s_ao4[:, i * D + 512: i * D + 1024]))
            st_toks += [t1, t2]
        p.need("v", *st_toks)
        ag_toks = [p.prod("v", lambda v, i=i: v.bn_aggr(
            s_mv4[:, 2 * i: 2 * i + 2], s_bn4[:, i * 12: i * 12 + 12])) for i in range(4)]
        p.need("v", *ag_toks)
        ep_toks = [p.prod("v", lambda v, i=i: v.tensor_scalar_add(
            s_col[i][:], s_mv4[:, 2 * i + 1: 2 * i + 2], EPS)) for i in range(4)]
        p.need("s", *ep_toks)
        ln_toks = [p.prod("s", lambda sx, i=i: sx.activation(
            s_col[i][:], s_col[i][:], AF.Ln)) for i in range(4)]
        p.need("s", *ln_toks)
        rs_toks = [p.prod("s", lambda sx, i=i: sx.activation(
            s_col[i][:], s_col[i][:], AF.Exp, scale=-0.5)) for i in range(4)]
        p.need("v", *rs_toks)
        p.need_dma("v", ds_xr[0])
        p.need_dma("v", ds_x2)
        ap_toks = [p.prod("v", lambda v, i=i: v.tensor_scalar(
            s_x24[:, i * D:(i + 1) * D], s_ao4[:, i * D:(i + 1) * D],
            s_mv4[:, 2 * i: 2 * i + 1], s_col[i][:], ALU.subtract, ALU.mult))
            for i in range(4)]
        ao_used = ap_toks[-1]
        p.need("v", *ap_toks)
        ad_toks = [p.prod("v", lambda v, i=i: v.tensor_add(
            s_x24[:, i * D:(i + 1) * D], s_x24[:, i * D:(i + 1) * D],
            s_x4[:, i * D:(i + 1) * D])) for i in range(4)]
        x4_used = ad_toks[-1]
        p.need("y", *ad_toks)
        for i in range(4):
            p.dma("y", sc_x2.ap()[(qc * 4 + i) * 128:(qc * 4 + i + 1) * 128, :],
                  s_x24[:, i * D:(i + 1) * D], ds_x2)
        p.need("v", *ad_toks)
        st_toks = []
        for i in range(4):
            t1 = p.prod("v", lambda v, i=i: v.bn_stats(
                s_bn4[:, i * 12: i * 12 + 6], s_x24[:, i * D: i * D + 512]))
            t2 = p.prod("v", lambda v, i=i: v.bn_stats(
                s_bn4[:, i * 12 + 6: i * 12 + 12], s_x24[:, i * D + 512: i * D + 1024]))
            st_toks += [t1, t2]
        p.need("v", *st_toks)
        ag_toks = [p.prod("v", lambda v, i=i: v.bn_aggr(
            s_mv4[:, 2 * i: 2 * i + 2], s_bn4[:, i * 12: i * 12 + 12])) for i in range(4)]
        p.need("v", *ag_toks)
        ep_toks = [p.prod("v", lambda v, i=i: v.tensor_scalar_add(
            s_col[i][:], s_mv4[:, 2 * i + 1: 2 * i + 2], EPS)) for i in range(4)]
        p.need("s", *ep_toks)
        ln_toks = [p.prod("s", lambda sx, i=i: sx.activation(
            s_col[i][:], s_col[i][:], AF.Ln)) for i in range(4)]
        p.need("s", *ln_toks)
        rs_toks = [p.prod("s", lambda sx, i=i: sx.activation(
            s_col[i][:], s_col[i][:], AF.Exp, scale=-0.5)) for i in range(4)]
        p.need("v", *rs_toks)
        xn_toks = [p.prod("v", lambda v, i=i, qc=qc: v.tensor_scalar(
            s_x2n[:, (qc * 4 + i) * D:(qc * 4 + i + 1) * D], s_x24[:, i * D:(i + 1) * D],
            s_mv4[:, 2 * i: 2 * i + 1], s_col[i][:], ALU.subtract, ALU.mult))
            for i in range(4)]
        x2n_done_tok = xn_toks[-1]
        if cfg.debug:
            p.need("y", evs[-1])
            for i in range(4):
                p.dma("y", dbg["dbg_attnout"].ap()[:, (qc * 4 + i) * D:(qc * 4 + i + 1) * D],
                      s_ao4[:, i * D:(i + 1) * D], ds_out)

    p.join_all()
    nc.sbuf_base = mark0
    s_x2nT = [sb(f"s_x2nT{i}", [128, ND * 512], BF16) for i in range(2)]
    s_t1s = sb("s_t1s", [128, T], BF16)
    s_pu = sb("s_pu", [16, T], F32)
    s_probs = sb("s_probs", [16, T], F32)
    s_ew = sb("s_ew", [16, 16], F32)
    s_ones16 = sb("s_ones16", [16, T], F32)
    s_m16 = sb("s_m16", [16, T], F32)
    s_gt = sb("s_gt", [16, T], F32)
    s_eq = sb("s_eq", [16, T], F32)
    s_sel = sb("s_sel", [16, T], F32)
    s_sm = sb("s_sm", [16, T], F32)
    s_smb = sb("s_smb", [16, T], BF16)
    s_wpb = sb("s_wpb", [16, T], BF16)
    s_bis = sb("s_bis", [16, 12], F32)

    sil_toks = []
    x2nt_used = [None, None]
    for qc in range(NQ):
        par = qc % 2
        p.need("y", x2n_done_tok)
        p.need("y", x2nt_used[par])
        p.need_dma("y", ds_x2t[par])
        for tq in range(4):
            tt = qc * 4 + tq
            for j in range(ND):
                p.dma("y", s_x2nT[par][:, j * 512 + tq * 128: j * 512 + (tq + 1) * 128],
                      s_x2n[:, tt * D + j * 128: tt * D + (j + 1) * 128],
                      ds_x2t[par], transpose=True)
        p.need_dma("t", ds_x2t[par])
        bi = qc % 2
        p.need("t", pQ_tok.get(bi))
        def mmr(t, par=par, bank=pQ[bi]):
            for j in range(ND):
                ins = t.matmul(bank[:], s_rw1b[:, j * 128:(j + 1) * 128],
                               s_x2nT[par][:, j * 512:(j + 1) * 512],
                               start=(j == 0), stop=(j == ND - 1))
            return ins
        tmm = p.prod("t", mmr)
        x2nt_used[par] = tmm
        p.need("s", tmm)
        tsl = p.prod("s", lambda s, qc=qc, bi=bi: s.activation(
            s_t1s[:, qc * 512:(qc + 1) * 512], pQ[bi][:], AF.Silu))
        pQ_tok[bi] = tsl
        sil_toks.append(tsl)
    p.need_dma("t", ds_rw)
    lg_toks = []
    for qc in range(NQ):
        p.need("t", sil_toks[qc])
        def mml(t, qc=qc):
            return t.matmul(pO[qc % 4][0:16, 0:512], s_rw2b[:, 0:E],
                            s_t1s[:, qc * 512:(qc + 1) * 512], start=True, stop=True)
        lg_toks.append(p.prod("t", mml))
    p.need("v", *lg_toks)
    mx_toks = []
    for qc in range(NQ):
        mx_toks.append(p.prod("v", lambda v, qc=qc: v.reduce_max(
            s_ew[0:16, qc:qc + 1], pO[qc % 4][0:16, 0:512], AX.X)))
    p.need("v", *mx_toks)
    tk = p.prod("v", lambda v: v.reduce_max(s_ew[0:16, 4:5], s_ew[0:16, 0:NQ], AX.X))
    p.need("v", tk)
    tnm = p.prod("v", lambda v: v.tensor_scalar_mul(s_ew[0:16, 5:6], s_ew[0:16, 4:5], -1.0))
    p.need("s", tnm)
    ex_toks = []
    for qc in range(NQ):
        ex_toks.append(p.prod("s", lambda s, qc=qc: s.activation(
            s_pu[0:16, qc * 512:(qc + 1) * 512], pO[qc % 4][0:16, 0:512], AF.Exp,
            bias=s_ew[0:16, 5:6], accum_out=s_ew[0:16, 8 + qc: 9 + qc])))
    p.need("v", *ex_toks)
    tk = p.prod("v", lambda v: v.reduce_sum(s_ew[0:16, 6:7], s_ew[0:16, 8:8 + NQ], AX.X))
    p.need("v", tk)
    tk = p.prod("v", lambda v: v.reciprocal(s_ew[0:16, 7:8], s_ew[0:16, 6:7]))
    p.need("v", tk)
    tprobs = p.prod("v", lambda v: v.tensor_scalar(
        s_probs[0:16, :], s_pu[0:16, :], s_ew[0:16, 7:8], None, ALU.mult))
    if cfg.debug:
        p.need("y", tprobs)
        p.dma("y", dbg["dbg_probs"][:], s_probs[:], ds_out)

    t_ones16 = p.prod("v", lambda v: v.memset(s_ones16[:], 1.0))
    p.need("v", tprobs)
    tk = p.prod("v", lambda v: v.reduce_max(s_bis[0:16, 1:2], s_probs[0:16, :], AX.X))
    p.need("v", tk)
    tk = p.prod("v", lambda v: v.memset(s_bis[0:16, 0:1], 0.0))
    p.need("v", tk)
    tk = p.prod("v", lambda v: v.tensor_scalar_mul(s_bis[0:16, 2:3], s_bis[0:16, 1:2], 0.5))
    p.need("v", tk)
    for it in range(40):
        tk = p.prod("v", lambda v: v.tensor_scalar(
            s_m16[0:16, :], s_probs[0:16, :], s_bis[0:16, 2:3], None, ALU.is_gt, ALU.add,
            accum_out=s_bis[0:16, 3:4]))
        p.need("v", tk)
        tk = p.prod("v", lambda v: v.tensor_single_scalar(
            s_bis[0:16, 4:5], s_bis[0:16, 3:4], 256.0, ALU.is_ge))
        p.need("v", tk)
        tk = p.prod("v", lambda v: v.tensor_scalar(
            s_bis[0:16, 9:10], s_bis[0:16, 4:5], -1.0, 1.0, ALU.mult, ALU.add))
        p.need("v", tk)
        tk = p.prod("v", lambda v: v.tensor_scalar(
            s_bis[0:16, 8:9], s_bis[0:16, 2:3], s_bis[0:16, 4:5], None, ALU.mult))
        p.need("v", tk)
        tk = p.prod("v", lambda v: v.scalar_tensor_tensor(
            s_bis[0:16, 0:1], s_bis[0:16, 0:1], s_bis[0:16, 9:10], s_bis[0:16, 8:9],
            ALU.mult, ALU.add))
        p.need("v", tk)
        tk = p.prod("v", lambda v: v.tensor_scalar(
            s_bis[0:16, 8:9], s_bis[0:16, 2:3], s_bis[0:16, 9:10], None, ALU.mult))
        p.need("v", tk)
        tk = p.prod("v", lambda v: v.scalar_tensor_tensor(
            s_bis[0:16, 1:2], s_bis[0:16, 1:2], s_bis[0:16, 4:5], s_bis[0:16, 8:9],
            ALU.mult, ALU.add))
        p.need("v", tk)
        tk = p.prod("v", lambda v: v.tensor_add(s_bis[0:16, 2:3], s_bis[0:16, 0:1],
                                                s_bis[0:16, 1:2]))
        p.need("v", tk)
        tk = p.prod("v", lambda v: v.tensor_scalar_mul(s_bis[0:16, 2:3], s_bis[0:16, 2:3], 0.5))
        p.need("v", tk)
    tk = p.prod("v", lambda v: v.tensor_scalar(
        s_m16[0:16, :], s_probs[0:16, :], s_bis[0:16, 1:2], None, ALU.is_le))
    p.need("v", tk)
    tk = p.prod("v", lambda v: v.tensor_mul(s_m16[0:16, :], s_m16[0:16, :], s_probs[0:16, :]))
    p.need("v", tk)
    tk = p.prod("v", lambda v: v.reduce_max(s_bis[0:16, 5:6], s_m16[0:16, :], AX.X))
    p.need("v", tk)
    tk = p.prod("v", lambda v: v.tensor_scalar(
        s_gt[0:16, :], s_probs[0:16, :], s_bis[0:16, 5:6], None, ALU.is_gt, ALU.add,
        accum_out=s_bis[0:16, 3:4]))
    p.need("v", tk)
    tk = p.prod("v", lambda v: v.tensor_scalar(
        s_bis[0:16, 6:7], s_bis[0:16, 3:4], -1.0, 256.0, ALU.mult, ALU.add))
    p.need("v", tk)
    tk = p.prod("v", lambda v: v.tensor_scalar(
        s_eq[0:16, :], s_probs[0:16, :], s_bis[0:16, 5:6], None, ALU.is_equal))
    p.need("v", tk, t_ones16)
    tk = p.prod("v", lambda v: v.tensor_tensor_scan(
        s_m16[0:16, :], s_ones16[0:16, :], s_eq[0:16, :], 0.0, ALU.mult, ALU.add))
    p.need("v", tk)
    tk = p.prod("v", lambda v: v.tensor_sub(s_m16[0:16, :], s_m16[0:16, :], s_eq[0:16, :]))
    p.need("v", tk)
    tk = p.prod("v", lambda v: v.tensor_scalar(
        s_m16[0:16, :], s_m16[0:16, :], s_bis[0:16, 6:7], None, ALU.is_lt))
    p.need("v", tk)
    tk = p.prod("v", lambda v: v.tensor_mul(s_m16[0:16, :], s_m16[0:16, :], s_eq[0:16, :]))
    p.need("v", tk)
    tsel = p.prod("v", lambda v: v.tensor_add(s_sel[0:16, :], s_gt[0:16, :], s_m16[0:16, :]))
    p.need("v", tsel)
    tk = p.prod("v", lambda v: v.tensor_tensor_scan(
        s_sm[0:16, :], s_ones16[0:16, :], s_sel[0:16, :], 0.0, ALU.mult, ALU.add))
    p.need("v", tk)
    tk = p.prod("v", lambda v: v.tensor_sub(s_sm[0:16, :], s_sm[0:16, :], s_sel[0:16, :]))
    p.need("v", tk)
    tk = p.prod("v", lambda v: v.scalar_tensor_tensor(
        s_sm[0:16, :], s_sel[0:16, :], -384.0, s_sm[0:16, :], ALU.mult, ALU.add))
    p.need("v", tk)
    tk = p.prod("v", lambda v: v.tensor_scalar_add(s_sm[0:16, :], s_sm[0:16, :], 384.0))
    p.need("v", tk)
    tsmb = p.prod("v", lambda v: v.tensor_copy(s_smb[0:16, :], s_sm[0:16, :]))
    if cfg.debug:
        p.need("y", tsmb)
        p.dma("y", dbg["dbg_sm"][:], s_sm[:], ds_out)
    tk = p.prod("v", lambda v: v.scalar_tensor_tensor(
        s_m16[0:16, :], s_sel[0:16, :], 1.0, s_probs[0:16, :], ALU.mult, ALU.mult,
        accum_out=s_bis[0:16, 7:8]))
    p.need("v", tk)
    tk = p.prod("v", lambda v: v.tensor_scalar_add(s_bis[0:16, 7:8], s_bis[0:16, 7:8], 1e-9))
    p.need("v", tk)
    tSinv = p.prod("v", lambda v: v.reciprocal(s_bis[0:16, 7:8], s_bis[0:16, 7:8]))
    p.need("v", tSinv)
    twp = p.prod("v", lambda v: v.tensor_scalar(
        s_pu[0:16, :], s_probs[0:16, :], s_bis[0:16, 7:8], None, ALU.mult))
    p.need("v", twp)
    twpb = p.prod("v", lambda v: v.tensor_copy(s_wpb[0:16, :], s_pu[0:16, :]))
    p.need("y", tsmb, twpb)
    for tt in range(NT):
        p.dma("y", s_smT[:, tt * 16:(tt + 1) * 16], s_smb[0:16, tt * 128:(tt + 1) * 128],
              ds_smt, transpose=True)
        p.dma("y", s_probsT[:, tt * 16:(tt + 1) * 16], s_wpb[0:16, tt * 128:(tt + 1) * 128],
              ds_smt, transpose=True)
    p.need_dma("v", ds_smt)
    tsmtf = p.prod("v", lambda v: v.tensor_copy(s_smTf[:], s_smT[:]))

    p.join_all()
    nc.sbuf_base = mark0
    s_ye = sb("s_ye", [128, E * 2 * 1024], BF16)
    markF = nc.sbuf_base
    s_xgT = sb("s_xgT", [128, 2 * 1024], BF16)
    s_ohg = sb("s_ohg", [128, NT * 256], BF16)
    s_w1c = [sb(f"s_w1c{i}", [128, ND * 512], BF16) for i in range(2)]
    s_w2c = [sb(f"s_w2c{i}", [128, ND * 512], BF16) for i in range(2)]
    s_w3c = [sb(f"s_w3c{i}", [128, 4 * 1024], BF16) for i in range(2)]
    s_gs = sb("s_gs", [128, 1024], BF16)
    s_hb = sb("s_hb", [128, 1024], BF16)
    s_gp = sb("s_gp", [16, 256], F32)
    s_wrf = sb("s_wrf", [1, 256], F32)
    s_wcol = sb("s_wcol", [128, 2], F32)

    gv_free = None
    ye_free = None
    gather_free = None
    gs_free = None
    ohg_consumed = None
    prev_tym = None

    s_wf1 = [sb(f"s_wf1{i}", [128, 1024], F32) for i in range(2)]
    s_wf2 = sb("s_wf2", [128, 1024], F32)
    ds_f1 = [p.dma_sem("ds_f1a"), p.dma_sem("ds_f1b")]
    ds_f2 = p.dma_sem("ds_f2b")
    ds_w3 = [p.dma_sem("ds_w3a"), p.dma_sem("ds_w3b")]
    stg1_free = [None, None]
    wcast1 = [[None] * 4 for _ in range(2)]
    wcast2 = [[None] * 4 for _ in range(2)]
    w_cons1 = [[None] * 4 for _ in range(2)]
    w_cons2 = [[None] * 4 for _ in range(2)]
    w_cons3 = [None, None]

    def wsrc(k, e, dffc, qtr):
        w = w1 if k == 1 else w2
        return w.ap()[e * NDC + dffc][:, qtr * 1024:(qtr + 1) * 1024]

    def issue_w1(e, dffc, qtr):
        buf = qtr % 2
        p.need("y", stg1_free[buf])
        p.dma("y", s_wf1[buf][:], wsrc(1, e, dffc, qtr), ds_f1[buf])

    def cast_w1(qtr, par):
        buf = qtr % 2
        p.need_dma("v", ds_f1[buf])
        p.need("v", w_cons1[par][qtr])
        tok = p.prod("v", lambda v, qtr=qtr, par=par, buf=buf: v.tensor_copy(
            s_w1c[par][:, qtr * 1024:(qtr + 1) * 1024], s_wf1[buf][:]))
        stg1_free[buf] = tok
        wcast1[par][qtr] = tok

    stg2_box = [None]

    def issue_w2(e, dffc, qtr):
        p.need("s", stg2_box[0])
        p.dma("s", s_wf2[:], wsrc(2, e, dffc, qtr), ds_f2)

    def cast_w2(qtr, par):
        p.need_dma("s", ds_f2)
        p.need("s", w_cons2[par][qtr])
        tok = p.prod("s", lambda s, qtr=qtr, par=par: s.activation(
            s_w2c[par][:, qtr * 1024:(qtr + 1) * 1024], s_wf2[:], AF.Copy))
        stg2_box[0] = tok
        wcast2[par][qtr] = tok

    def load_w3(e, dffc, par):
        p.need("g", w_cons3[par])
        p.dma("g", s_w3c[par][:], w3.ap()[e * NDC + dffc], ds_w3[par])

    load_w3(0, 0, 0)
    for _q in range(4):
        issue_w1(0, 0, _q)
        cast_w1(_q, 0)
    for _q in range(4):
        issue_w2(0, 0, _q)
        cast_w2(_q, 0)

    def build_onehots(e):
        toks = []
        for tt in range(NT):
            toks.append(p.prod("v", lambda v, tt=tt, e=e: v.tensor_scalar(
                s_ohg[:, tt * 256:(tt + 1) * 256], s_iota[:],
                s_smTf[:, tt * 16 + e: tt * 16 + e + 1], None, ALU.is_equal)))
        return toks

    p.need("v", tsmtf)
    ohg_toks = build_onehots(0)
    for e in range(E):
        p.need("t", *ohg_toks)
        p.need("t", x2n_done_tok)
        p.need("t", gv_free)
        p.need("t", ye_free)
        if e == 0:
            p.need("t", *ex_toks)
        gbanks8 = [pQ[0], pQ[1], pA[0], pA[1], pO[0], pO[1], pO[2], pO[3]]
        def mmg(t, e=e):
            for tt in range(NT):
                for dt in range(ND):
                    ins = t.matmul(gbanks8[dt][:, 0:256],
                                   s_x2n[:, tt * D + dt * 128: tt * D + dt * 128 + 128],
                                   s_ohg[:, tt * 256:(tt + 1) * 256],
                                   start=(tt == 0), stop=(tt == NT - 1))
            return ins
        tgm = p.prod("t", mmg)
        p.need("v", tgm)
        if prev_tym is not None:
            p.need("v", prev_tym)
        if cfg.debug:
            p.need_dma("v", ds_out)
        tge = []
        for b in range(8):
            tge.append(p.prod("v", lambda v, b=b: v.tensor_copy(
                s_xgT[:, b * 256:(b + 1) * 256], gbanks8[b][:, 0:256])))
        gather_free = tge[-1]

        p.need("t", *tge)
        def mmp(t, e=e):
            for tt in range(NT):
                ins = t.matmul(pO[0][0:16, 0:256], s_probsT[:, tt * 16:(tt + 1) * 16],
                               s_ohg[:, tt * 256:(tt + 1) * 256],
                               start=(tt == 0), stop=(tt == NT - 1))
            return ins
        tpm = p.prod("t", mmp)
        ohg_consumed = tpm
        p.need("v", tpm)
        twall = p.prod("v", lambda v: v.tensor_copy(s_gp[0:16, :], pO[0][0:16, 0:256]))
        p.need("t", twall)
        trow = p.prod("t", lambda t, e=e: t.matmul(
            pO[1][0:1, 0:256], s_identf[0:16, e:e + 1], s_gp[0:16, :],
            start=True, stop=True))
        p.need("v", trow)
        twrf = p.prod("v", lambda v: v.tensor_copy(s_wrf[:], pO[1][0:1, 0:256]))
        p.need("t", twrf)
        ttr0 = p.prod("t", lambda t: t.transpose(pO[1][:, 0:1], s_wrf[0:1, 0:128],
                                                 s_identf[0:1, 0:1]))
        ttr1 = p.prod("t", lambda t: t.transpose(pO[1][:, 1:2], s_wrf[0:1, 128:256],
                                                 s_identf[0:1, 0:1]))
        p.need("v", ttr0, ttr1)
        twcol = p.prod("v", lambda v: v.tensor_copy(s_wcol[:], pO[1][:, 0:2]))
        if e + 1 < E:
            p.need("v", tpm)
            ohg_toks = build_onehots(e + 1)

        hb_used = None
        for dffc in range(NDC):
            par = (e * NDC + dffc) % 2
            npar = (e * NDC + dffc + 1) % 2
            nxt = ((e, dffc + 1) if dffc + 1 < NDC
                   else ((e + 1, 0) if e + 1 < E else None))
            if nxt is not None:
                ne, nd = nxt
                load_w3(ne, nd, npar)
                issue_w1(ne, nd, 0)
                issue_w1(ne, nd, 1)
                cast_w1(0, npar)
                issue_w1(ne, nd, 2)
                cast_w1(1, npar)
                issue_w1(ne, nd, 3)
                cast_w1(2, npar)
                cast_w1(3, npar)
                issue_w2(ne, nd, 0)
            p.need("t", gather_free)
            fb = [pQ[0], pQ[1], pA[0], pA[1]]
            def mk_mm1(f, wc, par=par):
                def mm(t, f=f, wc=wc, par=par):
                    for j in range(ND):
                        g_ins = t.matmul(fb[f][:, 0:256],
                                         wc[par][:, f * 1024 + j * 128: f * 1024 + j * 128 + 128],
                                         s_xgT[:, j * 256:(j + 1) * 256],
                                         start=(j == 0), stop=(j == ND - 1))
                    return g_ins
                return mm
            tgg_f = []
            for f in range(4):
                p.need("t", wcast1[par][f])
                tgg_f.append(p.prod("t", mk_mm1(f, s_w1c)))
                w_cons1[par][f] = tgg_f[-1]
            if gs_free is not None:
                p.need("s", gs_free)
            sil = []
            for f in range(4):
                p.need("s", tgg_f[f])
                sil.append(p.prod("s", lambda s, f=f: s.activation(
                    s_gs[:, f * 256:(f + 1) * 256], fb[f][:, 0:256], AF.Silu)))
            if nxt is not None:
                for qq in range(4):
                    cast_w2(qq, npar)
                    if qq < 3:
                        issue_w2(nxt[0], nxt[1], qq + 1)
            tgv_f = []
            for f in range(4):
                p.need("t", sil[f])
                p.need("t", wcast2[par][f])
                tgv_f.append(p.prod("t", mk_mm1(f, s_w2c)))
                w_cons2[par][f] = tgv_f[-1]
            if hb_used is not None:
                p.need("v", hb_used)
            muls = []
            for f in range(4):
                p.need("v", tgv_f[f])
                p.need("v", sil[f])
                muls.append(p.prod("v", lambda v, f=f: v.tensor_mul(
                    s_hb[:, f * 256:(f + 1) * 256], fb[f][:, 0:256],
                    s_gs[:, f * 256:(f + 1) * 256])))
            thb = muls[-1]
            gs_free = thb
            gv_free = thb
            p.need_dma("t", ds_w3[par])
            if dffc == 0:
                p.need("t", tpm, ttr0, ttr1, twall, twcol)
            tym = None
            for f in range(4):
                p.need("t", muls[f])
                def mmyf(t, dffc=dffc, par=par, f=f):
                    for st in range(2):
                        for dc in range(2):
                            ins = t.matmul(
                                pO[2 * st + dc][:, 0:512],
                                s_hb[:, f * 256 + st * 128: f * 256 + st * 128 + 128],
                                s_w3c[par][:, f * 1024 + dc * 512: f * 1024 + (dc + 1) * 512],
                                start=(dffc == 0 and f == 0), stop=(dffc == NDC - 1 and f == 3))
                    return ins
                tym = p.prod("t", mmyf)
            w_cons3[par] = tym
            hb_used = tym
        p.need("v", hb_used, twcol)
        yev = []
        for st in range(2):
            for dc in range(2):
                yev.append(p.prod("v", lambda v, st=st, dc=dc, e=e: v.tensor_scalar(
                    s_ye[:, e * 2048 + st * 1024 + dc * 512: e * 2048 + st * 1024 + (dc + 1) * 512],
                    pO[2 * st + dc][:, 0:512], s_wcol[:, st:st + 1], None, ALU.mult)))
        ye_free = yev[-1]
        p.need("t", ye_free)
        prev_tym = hb_used

        if cfg.debug and e == 0:
            p.need("y", *yev)
            p.dma("y", dbg["dbg_ye"][:], s_ye[:, 0:2048], ds_out)
            p.need("y", tge[-1])
            p.dma("y", dbg["dbg_xgt"][:], s_xgT[:], ds_out)
            p.need("y", twcol)
            p.dma("y", dbg["dbg_wcol"][:], s_wcol[:], ds_out)

    p.join_all()
    nc.sbuf_base = markF
    s_ohgs = [sb(f"s_ohgs{i}", [128, 2 * 256], BF16) for i in range(2)]
    s_ohs = [sb(f"s_ohs{i}", [128, 4 * 128], BF16) for i in range(2)]
    s_ffn = sb("s_ffn", [128, D], BF16)
    s_outt = sb("s_outt", [128, D], F32)

    NTC = NT // 2
    nc.psum_base = 0
    scat_banks = [nc.alloc_psum_tensor(f"pS{i}", [128, 512], F32) for i in range(4)]
    pT = nc.alloc_psum_tensor("pT", [128, 512], BF16)
    p.need_dma("y", ds_x2)
    ohs_used = [None, None]
    ohgs_guard = [None, None]
    pT_free = None
    bank_free = {i: None for i in range(4)}
    for tc in range(NTC):
        mm_last = {i: None for i in range(4)}
        for e in range(E):
            par = e % 2
            p.need("v", tsmtf)
            p.need("v", ohgs_guard[par])
            og = []
            for ts_ in range(2):
                tt = tc * 2 + ts_
                og.append(p.prod("v", lambda v, ts_=ts_, tt=tt, e=e, par=par: v.tensor_scalar(
                    s_ohgs[par][:, ts_ * 256:(ts_ + 1) * 256], s_iota[:],
                    s_smTf[:, tt * 16 + e: tt * 16 + e + 1], None, ALU.is_equal)))
            p.need("t", *og)
            p.need("t", pT_free)
            ttr = []
            for ts_ in range(2):
                for st in range(2):
                    ttr.append(p.prod("t", lambda t, ts_=ts_, st=st, par=par: t.transpose(
                        pT[:, (st * 2 + ts_) * 128:(st * 2 + ts_ + 1) * 128],
                        s_ohgs[par][:, ts_ * 256 + st * 128: ts_ * 256 + st * 128 + 128],
                        s_identb[:])))
            ohgs_guard[par] = ttr[-1]
            p.need("v", *ttr)
            p.need("v", ohs_used[par])
            tev = []
            for k4 in range(4):
                tev.append(p.prod("v", lambda v, k4=k4, par=par: v.tensor_copy(
                    s_ohs[par][:, k4 * 128:(k4 + 1) * 128], pT[:, k4 * 128:(k4 + 1) * 128])))
            pT_free = tev[-1]
            p.need("t", *tev)
            if e == 0:
                for bi in range(4):
                    p.need("t", bank_free[bi])
            for ts_ in range(2):
                for dc in range(2):
                    bi = ts_ * 2 + dc
                    def mms(t, e=e, ts_=ts_, dc=dc, par=par, bi=bi):
                        for st in range(2):
                            ins = t.matmul(
                                scat_banks[bi][:],
                                s_ohs[par][:, (st * 2 + ts_) * 128:(st * 2 + ts_ + 1) * 128],
                                s_ye[:, e * 2048 + st * 1024 + dc * 512:
                                     e * 2048 + st * 1024 + (dc + 1) * 512],
                                start=(e == 0 and st == 0), stop=(e == E - 1 and st == 1))
                        return ins
                    mm_last[bi] = p.prod("t", mms)
            ohs_used[par] = mm_last[3]
        for ts_ in range(2):
            tt = tc * 2 + ts_
            par = tt % 2
            p.dma("y", s_xstage[par][:], sc_x2.ap()[tt * 128:(tt + 1) * 128, :], ds_x2r[par])
            ev = []
            for dc in range(2):
                bi = ts_ * 2 + dc
                p.need("v", mm_last[bi])
                ev.append(p.prod("v", lambda v, dc=dc, bi=bi: v.tensor_copy(
                    s_ffn[:, dc * 512:(dc + 1) * 512], scat_banks[bi][:])))
                bank_free[bi] = ev[-1]
            p.need("v", *ev)
            tst = ln_stats(s_ffn, s_mv[par][:])
            p.need("v", tst)
            trs = rstd_chain(s_mv[par], s_col[par][:])
            p.need("v", trs)
            p.need_dma("v", ds_x2r[par])
            tk = p.prod("v", lambda v, par=par: v.tensor_scalar(
                s_f32t[:], s_ffn[:], s_mv[par][:, 0:1], s_col[par][:],
                ALU.subtract, ALU.mult))
            p.need("v", tk)
            p.need_dma("v", ds_out)
            tout = p.prod("v", lambda v, par=par: v.tensor_add(s_outt[:], s_f32t[:],
                                                               s_xstage[par][:]))
            p.need("y", tout)
            p.dma("y", out_t.ap()[tt * 128:(tt + 1) * 128, :], s_outt[:], ds_out)

    p.join_all()
    with nc.Block() as block:
        p.realize(block)
    nc.finalize()
    return nc, din, dbg


_CACHE = {}


def _pack_attn_w(wq, wk, wv, wo):
    HH = wq.shape[0] // 256
    def pk(w):
        return np.ascontiguousarray(
            w.reshape(HH, 2, 128, 8, 128).transpose(0, 4, 3, 1, 2)).reshape(HH, 128, 2048)
    woP = np.ascontiguousarray(
        wo.reshape(8, 128, HH, 2, 128).transpose(2, 4, 3, 0, 1)).reshape(HH, 128, 2048)
    return pk(wq), pk(wk), pk(wv), woP


def _reorder_w(w1, w2, w3):
    E, D, DFF = w1.shape
    NDC = DFF // 512
    ND = D // 128
    w1r = np.ascontiguousarray(
        w1.reshape(E, ND, 128, NDC, 4, 128).transpose(0, 3, 2, 4, 1, 5)
    ).reshape(E * NDC, 128, ND * 512)
    w2r = np.ascontiguousarray(
        w2.reshape(E, ND, 128, NDC, 4, 128).transpose(0, 3, 2, 4, 1, 5)
    ).reshape(E * NDC, 128, ND * 512)
    w3r = np.ascontiguousarray(
        w3.reshape(E, NDC, 4, 128, D).transpose(0, 1, 3, 2, 4)
    ).reshape(E * NDC, 128, 4 * D)
    return w1r, w2r, w3r


def kernel(L0_attn_X, L0_attn_Wq, L0_attn_Wk, L0_attn_Wv, L0_attn_Wo,
           L0_attn_lambda_q1, L0_attn_lambda_k1, L0_attn_lambda_q2, L0_attn_lambda_k2,
           L0_attn_rms_scale, L0_ffn_router_w1, L0_ffn_router_w2,
           L0_ffn_w1, L0_ffn_w2, L0_ffn_w3):
    cfg = Cfg()
    if "nc" not in _CACHE:
        _CACHE["nc"] = build_program(cfg)[0]
    nc = _CACHE["nc"]
    consts = host_constants()
    X = np.ascontiguousarray(np.asarray(L0_attn_X, dtype=np.float32))
    wqP, wkP, wvP, woP = _pack_attn_w(
        np.asarray(L0_attn_Wq, np.float32), np.asarray(L0_attn_Wk, np.float32),
        np.asarray(L0_attn_Wv, np.float32), np.asarray(L0_attn_Wo, np.float32))
    base = {
        "wq": wqP, "wk": wkP, "wv": wvP, "wo": woP,
        "lq1": np.ascontiguousarray(np.asarray(L0_attn_lambda_q1, np.float32)),
        "lk1": np.ascontiguousarray(np.asarray(L0_attn_lambda_k1, np.float32)),
        "lq2": np.ascontiguousarray(np.asarray(L0_attn_lambda_q2, np.float32)),
        "lk2": np.ascontiguousarray(np.asarray(L0_attn_lambda_k2, np.float32)),
        "rmss": np.ascontiguousarray(np.asarray(L0_attn_rms_scale, np.float32).reshape(1, 256)),
        "rw1": np.ascontiguousarray(np.asarray(L0_ffn_router_w1, np.float32)),
        "rw2": np.ascontiguousarray(np.asarray(L0_ffn_router_w2, np.float32)),
    }
    base["w1"], base["w2"], base["w3"] = _reorder_w(
        np.asarray(L0_ffn_w1, np.float32), np.asarray(L0_ffn_w2, np.float32),
        np.asarray(L0_ffn_w3, np.float32))
    base.update(consts)
    in_maps = []
    for c in range(8):
        m = dict(base)
        m["x"] = np.ascontiguousarray(X[c])
        in_maps.append(m)
    res = run_bass_kernel_spmd(nc, in_maps, core_ids=list(range(8)))
    out = np.stack([np.asarray(r["out"], np.float32) for r in res.results], axis=0)
    return out
